# version 01
# baseline (speedup 1.0000x reference)
import math
import numpy as np
import ml_dtypes
import concourse.bass as bass
import concourse.mybir as mybir
from concourse.bass_utils import run_bass_kernel_spmd

F32 = mybir.dt.float32
BF16 = mybir.dt.bfloat16
I32 = mybir.dt.int32
AF = mybir.ActivationFunctionType
ALU = mybir.AluOpType
ET = mybir.EngineType
ds = bass.ds
BF = ml_dtypes.bfloat16

D = 2048
NCH = 16
NT = 4096
OWN = 2048
TT = 512
NTT = OWN // TT
HEADS, GROUPS, KVC = 16, 4, 128
FI, NIC = 1408, 11
NE = 8
SCALE = KVC ** (-0.5)
EXPB = -20.0
ENG_LIST = ("pe", "act", "dve", "pool", "sp")
ENG = {"pe": "tensor", "act": "scalar", "dve": "vector", "pool": "gpsimd", "sp": "sync"}
SAME_ENGINE_SYNC = True
RELU_ON_DVE = True
REPLICA_PAIRS = [[0, 1], [2, 3], [4, 5], [6, 7]]


class Op:
    __slots__ = ("idx", "eng", "fn", "reads", "writes", "dma", "deps", "needed", "sem", "tag")

    def __init__(self, idx, eng, fn, reads, writes, dma, tag):
        self.idx, self.eng, self.fn = idx, eng, fn
        self.reads, self.writes, self.dma, self.tag = tuple(reads), tuple(writes), dma, tag
        self.deps, self.needed, self.sem = (), False, None


class Prog:
    def __init__(self, nc):
        self.nc = nc
        self.ops = []

    def emit(self, eng, fn, reads=(), writes=(), dma=None, tag=""):
        op = Op(len(self.ops), eng, fn, reads, writes, dma, tag)
        self.ops.append(op)
        return op

    def dma(self, queue, out, in_, reads=(), writes=(), tag=""):
        return self.emit(queue, lambda e, o=out, i=in_: e.dma_start(out=o, in_=i),
                         reads=reads, writes=writes, dma=queue, tag=tag)

    def fence(self, tag_key, old_keys, eng="dve"):
        return self.emit(eng, lambda e: e.nop(), reads=(), writes=tuple(old_keys) + (tag_key,),
                         tag="fence")

    def finalize(self):
        nc = self.nc
        last_write, readers = {}, {}
        for op in self.ops:
            deps = {}
            for k in op.reads:
                d = last_write.get(k)
                if d is not None:
                    deps[d.idx] = d
            for k in op.writes:
                d = last_write.get(k)
                if d is not None:
                    deps[d.idx] = d
                for r in readers.get(k, ()):
                    deps[r.idx] = r
            pruned = []
            for d in deps.values():
                if d is op:
                    continue
                if d.dma is None and d.eng == op.eng and (d.eng == "pe" or not SAME_ENGINE_SYNC):
                    continue
                if d.dma is not None and d.dma == op.dma:
                    continue
                pruned.append(d)
            op.deps = tuple(pruned)
            for d in pruned:
                d.needed = True
            for k in op.reads:
                readers.setdefault(k, []).append(op)
            for k in op.writes:
                last_write[k] = op
                readers[k] = []

        NLANES_BY_Q = {"pool": 4, "sp": 8, "act": 8}
        counters = {}
        qcount = {}
        sems = set()
        for op in self.ops:
            if op.dma is not None:
                if op.dma == "cc":
                    q = "q_cc"
                    counters[q] = counters.get(q, 0) + 1
                    op.sem = (q, counters[q], 1)
                else:
                    i = qcount.get(op.dma, 0)
                    qcount[op.dma] = i + 1
                    q = f"q_{op.dma}{i % NLANES_BY_Q[op.dma]}"
                    counters[q] = counters.get(q, 0) + 16
                    op.sem = (q, counters[q], 16)
                sems.add(q)
            elif op.needed:
                q = "e_" + op.eng
                counters[q] = counters.get(q, 0) + 1
                op.sem = (q, counters[q], 1)
                sems.add(q)
        self.sem_handles = {n: nc.alloc_semaphore(n) for n in sorted(sems)}

        by_eng = {}
        for op in self.ops:
            by_eng.setdefault(op.eng, []).append(op)

        def emitter(ops_list):
            def run(engine):
                wm = {}
                for op in ops_list:
                    for d in op.deps:
                        sn, val, _ = d.sem
                        if wm.get(sn, 0) < val:
                            engine.wait_ge(self.sem_handles[sn], val)
                            wm[sn] = val
                    if op.dma is not None and op.dma != "cc":
                        sn, val, _ = op.sem
                        if val > 16 and wm.get(sn, 0) < val - 16:
                            engine.wait_ge(self.sem_handles[sn], val - 16)
                            wm[sn] = val - 16
                    inst = op.fn(engine)
                    if op.sem is not None:
                        sn, _, inc = op.sem
                        if inst is None:
                            inst = engine.nop()
                        inst.then_inc(self.sem_handles[sn], inc)
            return run

        with nc.Block() as block:
            for eng, ops_list in by_eng.items():
                getattr(block, ENG[eng])(emitter(ops_list))

    def stats(self):
        from collections import Counter
        c = Counter(op.eng for op in self.ops)
        return dict(ops=dict(c), n_ops=len(self.ops),
                    n_incs=sum(1 for o in self.ops if o.sem is not None),
                    n_waits=sum(len(o.deps) for o in self.ops))


def _own_perm(h):
    return np.r_[np.arange(h, 8, 2), np.arange(1 - h, 8, 2)]


def _colblocks(w, nb):
    R, C = w.shape
    cb = C // nb
    return np.ascontiguousarray(w.reshape(R, nb, cb).transpose(1, 0, 2))


def stage_shared(inp):
    s = {}
    f32 = np.float32
    s["wq_r"] = _colblocks(np.asarray(inp["L0_attn_w_q"], f32), 16)
    wkv = np.asarray(inp["L0_attn_w_kv"], f32)
    s["wkvk_r"] = _colblocks(wkv[:, :512], 4)
    s["wkvv_r"] = np.ascontiguousarray(wkv[:, 512:].reshape(16, 128, 512))
    s["wd_r"] = _colblocks(np.asarray(inp["L0_attn_w_dense"], f32), 16)
    s["wr"] = np.ascontiguousarray(np.asarray(inp["L0_ffn_Wr"], f32).reshape(16, 128, 8).transpose(1, 0, 2).reshape(128, 128))
    wg = np.asarray(inp["L0_ffn_Wg"], f32)
    wu = np.asarray(inp["L0_ffn_Wu"], f32)
    s["wg_r"] = np.ascontiguousarray(wg.reshape(8, 2048, 11, 128).transpose(0, 2, 1, 3))
    s["wu_r"] = np.ascontiguousarray(wu.reshape(8, 2048, 11, 128).transpose(0, 2, 1, 3))
    wd = np.asarray(inp["L0_ffn_Wd"], f32)
    s["wdd_r"] = np.ascontiguousarray(wd.reshape(8, 1408, 16, 128).transpose(0, 2, 1, 3))
    s["wts_r"] = _colblocks(np.ascontiguousarray(np.asarray(inp["L1_attn_Wts"], f32).T), 16)
    s["wkT"] = np.ascontiguousarray(np.asarray(inp["L1_attn_Wk"], f32).T)
    s["wvT"] = np.ascontiguousarray(np.asarray(inp["L1_attn_Wv"], f32).T)
    s["woT"] = np.ascontiguousarray(np.asarray(inp["L1_attn_Wo"], f32).T)
    s["gate2d"] = np.ascontiguousarray(np.asarray(inp["L1_attn_gate"], f32).reshape(16, 128).T)
    s["lnw2d"] = np.ascontiguousarray(np.asarray(inp["L1_attn_ln_w"], f32).reshape(16, 128).T)
    s["lnb2d"] = np.ascontiguousarray(np.asarray(inp["L1_attn_ln_b"], f32).reshape(16, 128).T)
    s["td"] = np.ascontiguousarray(np.tile(np.asarray(inp["L1_attn_time_decay"], f32).reshape(64, 1), (2, 1)))
    s["tf"] = np.ascontiguousarray(np.asarray(inp["L1_attn_time_first"], f32).reshape(64, 1))
    s["wupT_r"] = _colblocks(np.ascontiguousarray(np.asarray(inp["L1_ffn_W_up"], f32).T), 16)
    s["wdownT_r"] = _colblocks(np.ascontiguousarray(np.asarray(inp["L1_ffn_W_down"], f32).T), 16)
    s["wexp_r"] = _colblocks(np.asarray(inp["L1_ffn_W_expert"], f32), 16)
    s["wgT"] = np.ascontiguousarray(np.asarray(inp["L1_ffn_W_gate"], f32).T.reshape(16, 128, 8).transpose(1, 0, 2).reshape(128, 128))
    return s


def stage_core(inp, c):
    b, h = c // 2, c % 2
    x = np.asarray(inp["L0_attn_x"], np.float32)[b]
    xt = np.ascontiguousarray(x.T)
    perm = _own_perm(h)
    xtp = np.ascontiguousarray(xt.reshape(2048, 8, 512)[:, perm, :].reshape(2048, 4096))
    return {
        "xt": xtp,
        "hflag": np.full((128, 1), float(h), dtype=np.float32),
        "hoff": np.array([[h * 512]], dtype=np.int32),
    }


def make_consts():
    c = {}
    c["ident_bf"] = np.eye(128, dtype=BF)
    c["ident_f32"] = np.eye(128, dtype=np.float32)
    c["ones_col_bf"] = np.ones((128, 1), dtype=BF)
    c["ones_row_f32"] = np.ones((1, 128), dtype=np.float32)
    c["ones_row_bf"] = np.ones((1, 128), dtype=BF)
    dm = np.zeros((128, 4 * 512), dtype=BF)
    for j in range(4):
        kk = np.arange(128)[:, None] + j * 128
        q = np.arange(512)[None, :]
        dm[:, j * 512:(j + 1) * 512] = (kk <= q).astype(BF)
    c["diagmask"] = dm
    oh = np.zeros((8, 8, 64), dtype=BF)
    for e in range(8):
        oh[e, e, :] = 1
    c["onehot64"] = oh.reshape(8, 512)
    cc = np.zeros((128, 8), dtype=np.float32)
    cc[:, 0] = 1e-5
    cc[:, 1] = EXPB
    c["cstcols"] = cc
    return c


class Ctx:
    pass


def make_ctx(nc, blocks):
    cx = Ctx()
    cx.nc = nc
    cx.p = Prog(nc)
    dt = nc.dram_tensor

    need = set(blocks)
    if "l0a" in need:
        cx.d_xt = dt("xt", [2048, 4096], F32, kind="ExternalInput")
        cx.d_wq = dt("wq_r", [16, 2048, 128], F32, kind="ExternalInput")
        cx.d_wkvk = dt("wkvk_r", [4, 2048, 128], F32, kind="ExternalInput")
        cx.d_wkvv = dt("wkvv_r", [16, 128, 512], F32, kind="ExternalInput")
        cx.d_wd = dt("wd_r", [16, 2048, 128], F32, kind="ExternalInput")
        cx.d_hflag = dt("hflag", [128, 1], F32, kind="ExternalInput")
    if "l0f" in need:
        cx.d_wr = dt("wr", [128, 128], F32, kind="ExternalInput")
        cx.d_wg = dt("wg_r", [8, 11, 2048, 128], F32, kind="ExternalInput")
        cx.d_wu = dt("wu_r", [8, 11, 2048, 128], F32, kind="ExternalInput")
        cx.d_wdd = dt("wdd_r", [8, 16, 1408, 128], F32, kind="ExternalInput")
    if "l1a" in need:
        cx.d_wts = dt("wts_r", [16, 2048, 128], F32, kind="ExternalInput")
        cx.d_wkT = dt("wkT", [2048, 64], F32, kind="ExternalInput")
        cx.d_wvT = dt("wvT", [2048, 64], F32, kind="ExternalInput")
        cx.d_woT = dt("woT", [64, 2048], F32, kind="ExternalInput")
        cx.d_gate = dt("gate2d", [128, 16], F32, kind="ExternalInput")
        cx.d_lnw = dt("lnw2d", [128, 16], F32, kind="ExternalInput")
        cx.d_lnb = dt("lnb2d", [128, 16], F32, kind="ExternalInput")
        cx.d_td = dt("td", [128, 1], F32, kind="ExternalInput")
        cx.d_tf = dt("tf", [64, 1], F32, kind="ExternalInput")
        cx.d_hoff = dt("hoff", [1, 1], I32, kind="ExternalInput")
        cx.d_ccin = dt("ccin", [64, 2048], F32, kind="Internal")
        cx.d_ccout = dt("ccout", [128, 2048], F32, kind="Internal")
    if "l1f" in need:
        cx.d_wupT = dt("wupT_r", [16, 2048, 128], F32, kind="ExternalInput")
        cx.d_wdownT = dt("wdownT_r", [16, 2048, 128], F32, kind="ExternalInput")
        cx.d_wexp = dt("wexp_r", [16, 2048, 128], F32, kind="ExternalInput")
        cx.d_wgT = dt("wgT", [128, 128], F32, kind="ExternalInput")

    first = blocks[0]
    cx.unfused_in = None
    if first != "l0a":
        cx.unfused_in = dt("xin", [2048, 2048], F32, kind="ExternalInput")
    last = blocks[-1]
    if last == "l1f":
        cx.d_out = dt("xout", [2048, 2048], F32, kind="ExternalOutput")
    else:
        cx.d_out = dt("xout", [2048, 2048], F32, kind="ExternalOutput")

    cn = make_consts()
    cx.ident_bf = nc.inline_tensor(cn["ident_bf"], name="ident_bf")
    cx.ident_f32 = nc.inline_tensor(cn["ident_f32"], name="ident_f32")
    cx.ones_col = nc.inline_tensor(cn["ones_col_bf"], name="ones_col")
    cx.ones_row = nc.inline_tensor(cn["ones_row_f32"], name="ones_row")
    cx.ones_row_bf = nc.inline_tensor(cn["ones_row_bf"], name="ones_row_bf")
    cx.diagmask = nc.inline_tensor(cn["diagmask"], name="diagmask")
    cx.cstcols = nc.inline_tensor(cn["cstcols"], name="cstcols")
    cx.onehot64 = nc.inline_tensor(cn["onehot64"], name="onehot64")

    al = nc.alloc_sbuf_tensor
    cx.pad = al("pad", [128, 1024], mybir.dt.uint8)
    cx.A = al("A", [128, 16 * 2048], BF16)
    cx.R1 = al("R1", [128, 32768], BF16)
    cx.R2 = al("R2", [128, 22528], BF16)
    cx.WS = al("WS", [128, 6144], BF16)
    cx.rows = al("rows", [1, 1536], F32)
    cx.sc_f = al("sc_f", [128, 2048], F32)
    cx.sc_b = al("sc_b", [128, 3072], BF16)
    cx.srw = cx.sc_b[0:1, 1024:3072]
    cx.sml = al("sml", [128, 80], F32)
    cx.smlb = al("smlb", [128, 32], BF16)
    cx.i32 = al("i32", [1, 2], I32)

    cx.identb = al("identb", [128, 128], BF16)
    cx.identf = al("identf", [128, 128], F32)
    cx.onesc = al("onesc", [128, 1], BF16)
    cx.onesr = al("onesr", [1, 128], F32)
    cx.onesrb = al("onesrb", [1, 128], BF16)

    cx.cst = al("cst", [128, 8], F32)
    cx.dmask = cx.R2[:, 17408:19456]
    cx.pb = [nc.alloc_psum_tensor(f"pb{k}", [128, 512], F32) for k in range(8)]

    p = cx.p
    p.dma("sp", cx.identb[:, :], cx.ident_bf[:, :], writes=("identb",))
    p.dma("sp", cx.identf[:, :], cx.ident_f32[:, :], writes=("identf",))
    p.dma("sp", cx.onesc[:, :], cx.ones_col[:, :], writes=("onesc",))
    p.dma("sp", cx.onesr[:, :], cx.ones_row[:, :], writes=("onesr",))
    p.dma("sp", cx.onesrb[:, :], cx.ones_row_bf[:, :], writes=("onesrb",))
    if "l0a" in blocks:
        p.dma("sp", cx.dmask[:, :], cx.diagmask[:, :], writes=("dmask",))
    p.dma("sp", cx.cst[:, :], cx.cstcols[:, :], writes=("cst",))
    cx.const_keys = ("identb", "identf", "onesc", "onesr", "dmask")
    return cx


def ln_normalize(cx, src, src_key, dst, dst_key, tt, nchunk=16, colstride=2048,
                 col0=0, lnwb=None, extra_reads=()):
    p, nc = cx.p, cx.nc
    pb = cx.pb
    ps_sum, ps_sq = pb[4], pb[5]
    ps_rb, ps_nb = pb[6], pb[7]
    rows = cx.rows
    sq = cx.sc_b

    def col(c):
        return c * colstride + col0 + tt * 512

    for c in range(nchunk):
        sqslice = sq[:, (c % 2) * 512:(c % 2) * 512 + 512]
        p.emit("act", lambda e, c=c, s=sqslice: e.activation(s, src[:, col(c):col(c) + 512], AF.Square),
               reads=(src_key(c),) + tuple(extra_reads), writes=(f"lnsq{c % 2}",), tag="lnsq")
        p.emit("pe", lambda e, c=c: e.matmul(ps_sum[0:1, :], cx.onesc[:, :], src[:, col(c):col(c) + 512],
                                             start=(c == 0), stop=(c == nchunk - 1)),
               reads=(src_key(c), "onesc") + tuple(extra_reads), writes=("pb4",), tag="lnsum")
        p.emit("pe", lambda e, c=c, s=sqslice: e.matmul(ps_sq[0:1, :], cx.onesc[:, :], s,
                                                        start=(c == 0), stop=(c == nchunk - 1)),
               reads=(f"lnsq{c % 2}", "onesc"), writes=("pb5",), tag="lnsumsq")
    m = rows[0:1, 0:512]
    rs = rows[0:1, 512:1024]
    nm = rows[0:1, 1024:1536]
    tmpv = rs
    msq = nm
    inv = 1.0 / D
    p.emit("dve", lambda e: e.tensor_scalar_mul(m, ps_sum[0:1, :], inv),
           reads=("pb4",), writes=("r_m",), tag="lnrow")
    p.emit("dve", lambda e: e.tensor_scalar_mul(tmpv, ps_sq[0:1, :], inv),
           reads=("pb5",), writes=("r_rs",), tag="lnrow")
    p.emit("dve", lambda e: e.tensor_mul(msq, m, m), reads=("r_m",), writes=("r_nm",), tag="lnrow")
    p.emit("dve", lambda e: e.tensor_sub(tmpv, tmpv, msq), reads=("r_rs", "r_nm"), writes=("r_rs",), tag="lnrow")
    p.emit("act", lambda e: e.activation(rs, tmpv, AF.Ln, bias=cx.cst[0:1, 0:1]),
           reads=("r_rs", "cst"), writes=("r_rs",), tag="lnrow")
    p.emit("act", lambda e: e.activation(rs, rs, AF.Exp, scale=-0.5),
           reads=("r_rs",), writes=("r_rs",), tag="lnrow")
    p.emit("dve", lambda e: e.scalar_tensor_tensor(nm, m, -1.0, rs, ALU.mult, ALU.mult),
           reads=("r_m", "r_rs"), writes=("r_nm",), tag="lnrow")
    p.emit("pe", lambda e: e.matmul(ps_rb[:, :], cx.onesr[:, :], rs, start=True, stop=True),
           reads=("r_rs", "onesr"), writes=("pb6",), tag="lnbc")
    p.emit("pe", lambda e: e.matmul(ps_nb[:, :], cx.onesr[:, :], nm, start=True, stop=True),
           reads=("r_nm", "onesr"), writes=("pb7",), tag="lnbc")
    for c in range(nchunk):
        dsl = dst[:, col(c):col(c) + 512]
        p.emit("dve", lambda e, c=c, dsl=dsl: e.tensor_mul(dsl, src[:, col(c):col(c) + 512], ps_rb[:, :]),
               reads=(src_key(c), "pb6") + tuple(extra_reads), writes=(dst_key(c),), tag="lnn1")
        p.emit("dve", lambda e, c=c, dsl=dsl: e.tensor_add(dsl, dsl, ps_nb[:, :]),
               reads=(dst_key(c), "pb7"), writes=(dst_key(c),), tag="lnn2")
        if lnwb is not None:
            lnw_sb, lnb_sb = lnwb
            p.emit("dve", lambda e, c=c, dsl=dsl: e.tensor_scalar(
                dsl, dsl, lnw_sb[:, c:c + 1], lnb_sb[:, c:c + 1], ALU.mult, ALU.add),
                reads=(dst_key(c), "lnw", "lnb"), writes=(dst_key(c),), tag="lnn3")


def load_resid(cx, src_dram):
    p = cx.p
    for c in range(16):
        p.dma("pool", cx.A[:, c * 2048:(c + 1) * 2048],
              src_dram[c * 128:(c + 1) * 128, :],
              writes=tuple(("A", c, t) for t in range(4)), tag="residload")


def store_out_from_A(cx):
    p = cx.p
    for c in range(16):
        p.dma("pool", cx.d_out[c * 128:(c + 1) * 128, :],
              cx.A[:, c * 2048:(c + 1) * 2048],
              reads=tuple(("A", c, t) for t in range(4)), tag="residstore")


def gate_math(cx, xn, xn_key, wsb, wsb_key, mode, out_key, extra_reads=()):
    p = cx.p
    ps_l = cx.pb[3]
    m1, m2, d21, z, den, rden, negm1, zr = (cx.sml[:, j:j + 1] for j in range(8))
    K = lambda j: f"sml{j}"
    g8 = [cx.sc_f[:, 512 * j: 512 * j + 8] for j in range(4)]
    ge1, msk, expl, ctok = g8
    for tc in range(16):
        for d in range(16):
            p.emit("pe", lambda e, tc=tc, d=d: e.matmul(
                ps_l[0:128, 0:8],
                xn[:, d * 2048 + tc * 128: d * 2048 + (tc + 1) * 128],
                wsb[:, d * 8:(d + 1) * 8], start=(d == 0), stop=(d == 15)),
                reads=(xn_key(d, tc // 4), wsb_key) + tuple(extra_reads), writes=("pb3",), tag="glog")
        p.emit("dve", lambda e: e.reduce_max(m1, ps_l[0:128, 0:8], mybir.AxisListType.X),
               reads=("pb3",), writes=(K(0),), tag="gmath")
        p.emit("dve", lambda e: e.tensor_scalar(ge1, ps_l[0:128, 0:8], m1, None, ALU.is_ge),
               reads=("pb3", K(0)), writes=("scf0",), tag="gmath")
        p.emit("dve", lambda e: e.scalar_tensor_tensor(msk, ge1, -1e30, ps_l[0:128, 0:8], ALU.mult, ALU.add),
               reads=("pb3", "scf0"), writes=("scf1",), tag="gmath")
        p.emit("dve", lambda e: e.reduce_max(m2, msk, mybir.AxisListType.X),
               reads=("scf1",), writes=(K(1),), tag="gmath")
        p.emit("dve", lambda e: e.tensor_sub(d21, m2, m1),
               reads=(K(0), K(1)), writes=(K(2),), tag="gmath")
        p.emit("act", lambda e: e.activation(z, d21, AF.Exp),
               reads=(K(2),), writes=(K(3),), tag="gmath")
        p.emit("dve", lambda e: e.tensor_scalar_add(den, z, 1.0),
               reads=(K(3),), writes=(K(4),), tag="gmath")
        p.emit("dve", lambda e: e.reciprocal(rden, den),
               reads=(K(4),), writes=(K(5),), tag="gmath")
        if mode == "c8":
            p.emit("dve", lambda e: e.tensor_scalar_mul(negm1, m1, -1.0),
                   reads=(K(0),), writes=(K(6),), tag="gmath")
            p.emit("act", lambda e: e.activation(expl, ps_l[0:128, 0:8], AF.Exp, bias=negm1),
                   reads=("pb3", K(6)), writes=("scf2",), tag="gmath")
            p.emit("dve", lambda e: e.tensor_scalar(ctok, ps_l[0:128, 0:8], m2, None, ALU.is_ge),
                   reads=("pb3", K(1)), writes=("scf3",), tag="gmath")
            p.emit("dve", lambda e: e.tensor_mul(ctok, ctok, expl),
                   reads=("scf3", "scf2"), writes=("scf3",), tag="gmath")
            p.emit("dve", lambda e: e.tensor_scalar_mul(ctok, ctok, rden),
                   reads=("scf3", K(5)), writes=("scf3",), tag="gmath")
            p.emit("pe", lambda e: e.transpose(cx.pb[2][0:8, 0:128], ctok, cx.identf[:, :]),
                   reads=("scf3", "identf"), writes=("pb2",), tag="gtp")
            p.emit("dve", lambda e, tc=tc: e.tensor_copy(cx.crows[0:8, tc * 128:(tc + 1) * 128], cx.pb[2][0:8, 0:128]),
                   reads=("pb2",), writes=(out_key(tc),), tag="gtp")
        else:
            p.emit("dve", lambda e: e.tensor_mul(zr, z, rden),
                   reads=(K(3), K(5)), writes=(K(7),), tag="gmath")
            p.emit("dve", lambda e: e.tensor_add(zr, rden, zr),
                   reads=(K(5), K(7)), writes=(K(7),), tag="gmath")
            p.emit("pe", lambda e: e.transpose(cx.pb[2][0:1, 0:128], zr, cx.identf[:, :]),
                   reads=(K(7), "identf"), writes=("pb2",), tag="gtp")
            p.emit("dve", lambda e, tc=tc: e.tensor_copy(cx.srw[0:1, tc * 128:(tc + 1) * 128], cx.pb[2][0:1, 0:128]),
                   reads=("pb2",), writes=(out_key(tc),), tag="gtp")


def bld_l1f(cx, fences=()):
    p, nc = cx.p, cx.nc
    A = cx.A
    xn = cx.R1
    y1 = cx.R2
    fr = tuple(fences)

    cx.wg_sb = nc.alloc_sbuf_tensor("wg_sb", [128, 128], BF16)
    p.dma("pool", cx.wg_sb[:, :], cx.d_wgT[:, :], writes=("wgT",), tag="wgT")

    akey = lambda c, t: ("A", c, t)
    xnkey = lambda c, t: ("xn3", c, t)

    if cx.dbg not in ("w0nl",):
        for tt in range(4):
            ln_normalize(cx, A, lambda c: akey(c, tt), xn, lambda c: xnkey(c, tt), tt, extra_reads=fr)

    if cx.dbg == "xn":
        for c in range(16):
            p.dma("pool", cx.d_out[c * 128:(c + 1) * 128, :], xn[:, c * 2048:(c + 1) * 2048],
                  reads=tuple(xnkey(c, t) for t in range(4)), tag="dbg")
        return
    if cx.dbg not in ("xng", "w0ng", "w0nl", "y1ng", "psdump", "y1nc", "mx"):
        gate_math(cx, xn, xnkey, cx.wg_sb, "wgT", "rw", lambda tc: ("rw", tc))
    if cx.dbg == "xng":
        for c in range(16):
            p.dma("pool", cx.d_out[c * 128:(c + 1) * 128, :], xn[:, c * 2048:(c + 1) * 2048],
                  reads=tuple(xnkey(c, t) for t in range(4)), tag="dbg")
        return
    if cx.dbg == "rw":
        p.dma("pool", cx.d_out[0:1, 0:2048], cx.srw[0:1, :],
              reads=tuple(("rw", tc) for tc in range(16)), tag="dbg")
        return

    wub = [cx.WS[:, 0:2048], cx.WS[:, 0:2048]]
    wdb = [cx.WS[:, 2048 + b * 2048: 2048 + (b + 1) * 2048] for b in range(2)]
    web = [cx.R2[:, 18432 + b * 2048: 18432 + (b + 1) * 2048] for b in range(2)]

    def wdma(dram, buf, key, idx):
        p.dma("pool", buf[:, :].rearrange("p (c n) -> p c n", n=128),
              dram[idx, :, :].rearrange("(c p) n -> p c n", p=128),
              writes=(key,), tag="wstream")

    SCF = lambda j: cx.sc_f[:, j * 512:(j + 1) * 512]

    if cx.dbg == "w2":
        wdma(cx.d_wupT, wub[0], ("wup", 0), 0)
        wdma(cx.d_wupT, wub[1], ("wup", 1), 1)
        for b2 in range(2):
            p.dma("pool", cx.d_out[b2 * 128:(b2 + 1) * 128, 0:2048], wub[b2][:, :],
                  reads=(("wup", b2),), tag="dbg")
        return

    NOCU = {"y1one": 1, "y1four": 4, "y1ng": 1, "psdump": 1, "y1nc": 1, "mx": 1}.get(cx.dbg, 16)
    for half in range(2):
        tts = (2 * half, 2 * half + 1)
        for ocu in range(NOCU):
            bu = 0
            wdma(cx.d_wupT, wub[bu], ("wup", bu), ocu)
            for d in range(16):
                for j, tt in enumerate(tts):
                    p.emit("pe", lambda e, bu=bu, d=d, tt=tt, j=j: e.matmul(
                        cx.pb[4 + j][:, :], wub[bu][:, d * 128:(d + 1) * 128],
                        xn[:, d * 2048 + tt * 512: d * 2048 + tt * 512 + 512],
                        start=(d == 0), stop=(d == 15)),
                        reads=(("wup", bu), xnkey(d, tt)), writes=(f"pb{4 + j}",), tag="up")
            if cx.dbg == "psdump" and half == 0 and ocu == 0:
                p.emit("dve", lambda e: e.tensor_copy(cx.sc_f[:, 0:512], cx.pb[4][:, :]),
                       reads=("pb4",), writes=("scf0",), tag="dbg")
                p.dma("sp", cx.d_out[0:128, 0:512], cx.sc_f[:, 0:512], reads=("scf0",), tag="dbg")
                p.emit("dve", lambda e: e.tensor_copy(cx.sc_f[:, 512:1024], cx.pb[5][:, :]),
                       reads=("pb5",), writes=("scf1",), tag="dbg")
                p.dma("sp", cx.d_out[0:128, 512:1024], cx.sc_f[:, 512:1024], reads=("scf1",), tag="dbg")
                p.dma("pool", cx.d_out[128:256, 0:2048], wub[0][:, :], reads=(("wup", 0),), tag="dbg")
                p.dma("pool", cx.d_out[256:384, 0:2048], xn[:, 0:2048], reads=tuple(xnkey(0, t) for t in range(4)), tag="dbg")
                return
            for j, tt in enumerate(tts):
                if RELU_ON_DVE:
                    p.emit("dve", lambda e, ocu=ocu, j=j: e.tensor_relu(
                        y1[:, ocu * 1024 + j * 512: ocu * 1024 + j * 512 + 512],
                        cx.pb[4 + j][:, :]),
                        reads=(f"pb{4 + j}",) + fr, writes=(("y1", ocu, j),), tag="relu")
                else:
                    p.emit("act", lambda e, ocu=ocu, j=j: e.activation(
                        y1[:, ocu * 1024 + j * 512: ocu * 1024 + j * 512 + 512],
                        cx.pb[4 + j][:, :], AF.Relu),
                        reads=(f"pb{4 + j}",) + fr, writes=(("y1", ocu, j),), tag="relu")
        if cx.dbg == "m_y1" and half == 0:
            for c in range(16):
                p.emit("dve", lambda e, c=c: e.tensor_copy(cx.sc_f[:, 0:512], y1[:, c * 1024:c * 1024 + 512]),
                       reads=(("y1", c, 0),), writes=("scf0",), tag="dbg")
                p.dma("sp", cx.d_out[c * 128:(c + 1) * 128, 0:512], cx.sc_f[:, 0:512], reads=("scf0",), tag="dbg")
                p.emit("dve", lambda e, c=c: e.tensor_copy(cx.sc_f[:, 512:1024], y1[:, c * 1024 + 512:c * 1024 + 1024]),
                       reads=(("y1", c, 1),), writes=("scf1",), tag="dbg")
                p.dma("sp", cx.d_out[c * 128:(c + 1) * 128, 512:1024], cx.sc_f[:, 512:1024], reads=("scf1",), tag="dbg")
            return
        for j, tt in enumerate(tts):
            rb_slot = 1 if j == 0 else 3
            p.emit("pe", lambda e, tt=tt: e.matmul(
                cx.pb[6][:, :], cx.onesrb[:, :], cx.srw[0:1, tt * 512:(tt + 1) * 512],
                start=True, stop=True),
                reads=tuple(("rw", 4 * tt + i2) for i2 in range(4)) + ("onesrb",),
                writes=("pb6",), tag="rwbc")
            p.emit("act", lambda e, rb_slot=rb_slot: e.copy(SCF(rb_slot), cx.pb[6][:, :]),
                   reads=("pb6",), writes=(f"scf{rb_slot}",), tag="rwbc")
            if cx.dbg == "m_rb" and half == 0 and j == 0:
                p.dma("sp", cx.d_out[0:128, 0:512], SCF(rb_slot), reads=(f"scf{rb_slot}",), tag="dbg")
                p.emit("dve", lambda e: e.tensor_copy(cx.sc_f[:, 1024:1536], cx.pb[6][:, :]),
                       reads=("pb6",), writes=("scf2",), tag="dbg")
                p.dma("sp", cx.d_out[128:256, 0:512], cx.sc_f[:, 1024:1536], reads=("scf2",), tag="dbg")
                p.emit("dve", lambda e: e.tensor_copy(cx.sc_f[0:1, 0:512], cx.srw[0:1, 0:512]),
                       reads=tuple(("rw", i2) for i2 in range(4)), writes=("scf0",), tag="dbg")
                p.dma("sp", cx.d_out[256:257, 0:512], cx.sc_f[0:1, 0:512], reads=("scf0",), tag="dbg")
                return
        for oc in range(16):
            bd = oc % 2
            wdma(cx.d_wdownT, wdb[bd], ("wdown", bd), oc)
            wdma(cx.d_wexp, web[bd], ("wexp", bd), oc)
            for j, tt in enumerate(tts):
                rb_slot = 1 if j == 0 else 3
                for od in range(16):
                    p.emit("pe", lambda e, bd=bd, od=od, j=j: e.matmul(
                        cx.pb[j][:, :], wdb[bd][:, od * 128:(od + 1) * 128],
                        y1[:, od * 1024 + j * 512: od * 1024 + j * 512 + 512],
                        start=(od == 0), stop=(od == 15)),
                        reads=(("wdown", bd), ("y1", od, j)), writes=(f"pb{j}",), tag="down")
                for d in range(16):
                    p.emit("pe", lambda e, bd=bd, d=d, tt=tt, j=j: e.matmul(
                        cx.pb[2 + j][:, :], web[bd][:, d * 128:(d + 1) * 128],
                        xn[:, d * 2048 + tt * 512: d * 2048 + tt * 512 + 512],
                        start=(d == 0), stop=(d == 15)),
                        reads=(("wexp", bd), xnkey(d, tt)), writes=(f"pb{2 + j}",), tag="wex")
                p.emit("dve", lambda e, rb_slot=rb_slot, j=j: e.tensor_mul(SCF(2), cx.pb[2 + j][:, :], SCF(rb_slot)),
                       reads=(f"pb{2 + j}", f"scf{rb_slot}"), writes=("scf2",), tag="comb")
                p.emit("dve", lambda e, j=j: e.tensor_add(SCF(2), SCF(2), cx.pb[j][:, :]),
                       reads=("scf2", f"pb{j}"), writes=("scf2",), tag="comb")
                p.emit("dve", lambda e, oc=oc, tt=tt: e.tensor_add(
                    SCF(0), SCF(2), A[:, oc * 2048 + tt * 512: oc * 2048 + tt * 512 + 512]),
                    reads=("scf2", akey(oc, tt)), writes=("scf0",), tag="comb")
                p.dma("sp", cx.d_out[oc * 128:(oc + 1) * 128, tt * 512:(tt + 1) * 512], SCF(0),
                      reads=("scf0",), tag="outstore")
                if cx.dbg == "m_comb" and half == 0 and oc == 0 and j == 0:
                    p.emit("dve", lambda e, j=j: e.tensor_copy(cx.sc_f[:, 512:1024], cx.pb[j][:, :]),
                           reads=(f"pb{j}",), writes=("scf1",), tag="dbg")
                    p.dma("sp", cx.d_out[128:256, 0:512], cx.sc_f[:, 512:1024], reads=("scf1",), tag="dbg")
                    p.emit("dve", lambda e, j=j: e.tensor_copy(cx.sc_f[:, 1536:2048], cx.pb[2 + j][:, :]),
                           reads=(f"pb{2 + j}",), writes=("scf3",), tag="dbg")
                    p.dma("sp", cx.d_out[256:384, 0:512], cx.sc_f[:, 1536:2048], reads=("scf3",), tag="dbg")
                    p.dma("sp", cx.d_out[384:512, 0:512], SCF(2), reads=("scf2",), tag="dbg")
                    p.dma("sp", cx.d_out[512:640, 0:512], SCF(1), reads=("scf1",), tag="dbg")
                    p.dma("sp", cx.d_out[640:768, 0:512], SCF(0), reads=("scf0",), tag="dbg")
                    p.dma("pool", cx.d_out[768:896, 0:512], A[:, 0:512], reads=(akey(0, 0),), tag="dbg")
                    return


def build_launch(blocks, dbg=None):
    nc = bass.Bass("TRN2", target_bir_lowering=False, dynamic_dma_scratch_size=16384)
    cx = make_ctx(nc, blocks)
    cx.dbg = dbg
    if "l0f" in blocks:
        cx.crows = cx.sc_b[0:8, 1024:3072]
    if cx.unfused_in is not None:
        load_resid(cx, cx.unfused_in)
    for b in blocks:
        if b == "l0a":
            bld_l0a(cx)
        elif b == "l0f":
            bld_l0f(cx)
        elif b == "l1a":
            bld_l1a(cx)
        elif b == "l1f":
            bld_l1f(cx)
            if cx.dbg:
                return_after = True
    if blocks[-1] != "l1f" and (not dbg or dbg == "f_e0"):
        store_out_from_A(cx)
    cx.p.finalize()
    return nc, cx


def own_cols(h):
    perm = _own_perm(h)[:4]
    return (perm[:, None] * 512 + np.arange(512)[None, :]).reshape(-1)


def run_blocks_unfused(inp, blocks_groups, x_stage=None):
    sh = stage_shared(inp)
    cores = [stage_core(inp, c) for c in range(8)]
    cur = None
    for gi, blocks in enumerate(blocks_groups):
        nc, cx = build_launch(blocks)
        in_maps = []
        for c in range(8):
            im = {}
            for attr, key in [("d_xt", "xt"), ("d_wq", "wq_r"), ("d_wkvk", "wkvk_r"),
                              ("d_wkvv", "wkvv_r"), ("d_wd", "wd_r"), ("d_hflag", "hflag"),
                              ("d_wr", "wr"), ("d_wg", "wg_r"), ("d_wu", "wu_r"), ("d_wdd", "wdd_r"),
                              ("d_wts", "wts_r"), ("d_wkT", "wkT"), ("d_wvT", "wvT"), ("d_woT", "woT"),
                              ("d_gate", "gate2d"), ("d_lnw", "lnw2d"), ("d_lnb", "lnb2d"),
                              ("d_td", "td"), ("d_tf", "tf"), ("d_hoff", "hoff"),
                              ("d_wupT", "wupT_r"), ("d_wdownT", "wdownT_r"), ("d_wexp", "wexp_r"),
                              ("d_wgT", "wgT")]:
                if hasattr(cx, attr):
                    im[key] = sh[key] if key in sh else cores[c][key]
            if cx.unfused_in is not None:
                im["xin"] = x_stage[c] if x_stage is not None else cur[c]
            in_maps.append(im)
        res = run_bass_kernel_spmd(nc, in_maps, core_ids=list(range(8)))
        cur = [np.ascontiguousarray(res.results[c]["xout"]) for c in range(8)]
        x_stage = None
    out = np.zeros((4, 4096, 2048), np.float32)
    for c in range(8):
        b, h = c // 2, c % 2
        cols = own_cols(h)
        out[b][cols, :] = cur[c].T
    return out



def bld_l0a(cx, fences=()):
    p, nc = cx.p, cx.nc
    A = cx.A
    fr = tuple(fences)
    akey = lambda c, t: ("A", c, t)

    kT = cx.R1[:, 0:16384]
    vt = cx.R1[:, 16384:32768]
    xtile = cx.R2[:, 0:8192]
    ptb = [cx.R2[:, 8192 + b * 512: 8192 + (b + 1) * 512] for b in range(2)]
    ctxT = cx.R2[:, 9216:17408]
    wsA = cx.WS[:, 0:2048]
    wsB = cx.WS[:, 2048:4096]
    wvv = [cx.WS[:, 4096 + b * 512: 4096 + (b + 1) * 512] for b in range(2)]
    hf = cx.sml[:, 74:75]
    p.dma("sp", hf, cx.d_hflag[:, :], writes=("hflag",), tag="l0aw")

    for pt in range(8):
        xk = lambda c, pt=pt: ("xt", c)
        p.dma("pool", xtile[:, :].rearrange("p (c n) -> p c n", n=512),
              cx.d_xt[:, pt * 512:(pt + 1) * 512].rearrange("(c p) n -> p c n", p=128),
              reads=tuple(("xt", c) for c in range(16)), writes=tuple(("xt", c) for c in range(16)),
              tag="l0ax")
        ln_normalize(cx, xtile, xk, xtile, xk, 0, colstride=512, extra_reads=fr)
        for kfc in range(4):
            wb, wk = (wsA, "wsA") if kfc % 2 == 0 else (wsB, "wsB")
            p.dma("pool", wb[:, :].rearrange("p (c n) -> p c n", n=128),
                  cx.d_wkvk[kfc, :, :].rearrange("(c p) n -> p c n", p=128),
                  writes=(wk,), tag="l0aw")
            for d in range(16):
                p.emit("pe", lambda en, wb=wb, d=d: en.matmul(
                    cx.pb[0][:, :], wb[:, d * 128:(d + 1) * 128],
                    xtile[:, d * 512:(d + 1) * 512], start=(d == 0), stop=(d == 15)),
                    reads=(wk, ("xt", d)), writes=("pb0",), tag="l0akv")
            p.emit("dve", lambda en, kfc=kfc, pt=pt: en.tensor_copy(
                kT[:, kfc * 4096 + pt * 512: kfc * 4096 + (pt + 1) * 512], cx.pb[0][:, :]),
                reads=("pb0",), writes=(("kT", kfc, pt),), tag="l0akv")
        for d in range(16):
            b = d % 2
            p.dma("pool", wvv[b][:, :], cx.d_wkvv[d, :, :],
                  writes=(("wvv", b),), tag="l0aw")
            for ps4 in range(4):
                p.emit("pe", lambda en, d=d, ps4=ps4, b=b: en.matmul(
                    cx.pb[1 + ps4][:, :], xtile[:, d * 512 + ps4 * 128: d * 512 + (ps4 + 1) * 128],
                    wvv[b][:, :], start=(d == 0), stop=(d == 15)),
                    reads=(("wvv", b), ("xt", d)), writes=(f"pb{1 + ps4}",), tag="l0akv")
        for ps4 in range(4):
            p.emit("dve", lambda en, pt=pt, ps4=ps4: en.tensor_copy(
                vt[:, (pt * 4 + ps4) * 512: (pt * 4 + ps4 + 1) * 512], cx.pb[1 + ps4][:, :]),
                reads=(f"pb{1 + ps4}",), writes=(("vt", pt * 4 + ps4),), tag="l0akv")
        if pt < 4:
            for ec in range(16):
                wb, wk = (wsA, "wsA") if ec % 2 == 0 else (wsB, "wsB")
                p.dma("pool", wb[:, :].rearrange("p (c n) -> p c n", n=128),
                      cx.d_wq[ec, :, :].rearrange("(c p) n -> p c n", p=128),
                      writes=(wk,), tag="l0aw")
                for d in range(16):
                    p.emit("pe", lambda en, wb=wb, d=d, qb=5 + ec % 2: en.matmul(
                        cx.pb[qb][:, :], wb[:, d * 128:(d + 1) * 128],
                        xtile[:, d * 512:(d + 1) * 512], start=(d == 0), stop=(d == 15)),
                        reads=(wk, ("xt", d)), writes=(f"pb{5 + ec % 2}",), tag="l0aq")
                p.emit("dve", lambda en, ec=ec, pt=pt: en.tensor_copy(
                    A[:, ec * 2048 + pt * 512: ec * 2048 + (pt + 1) * 512], cx.pb[5 + ec % 2][:, :]),
                    reads=(f"pb{5 + ec % 2}",), writes=(akey(ec, pt),), tag="l0aq")

    if cx.dbg == "a_s1":
        for c in range(8):
            p.emit("dve", lambda en, c=c: en.tensor_copy(cx.sc_f[:, 0:512], A[:, c * 2048: c * 2048 + 512]),
                   reads=(akey(c, 0),), writes=("scf0",), tag="dbg")
            p.dma("sp", cx.d_out[c * 128:(c + 1) * 128, 0:512], cx.sc_f[:, 0:512], reads=("scf0",), tag="dbg")
        for kfc in range(4):
            p.emit("dve", lambda en, kfc=kfc: en.tensor_copy(cx.sc_f[:, 0:512], kT[:, kfc * 4096: kfc * 4096 + 512]),
                   reads=(("kT", kfc, 0),), writes=("scf0",), tag="dbg")
            p.dma("sp", cx.d_out[(8 + kfc) * 128:(9 + kfc) * 128, 0:512], cx.sc_f[:, 0:512], reads=("scf0",), tag="dbg")
        for p4 in range(4):
            p.emit("dve", lambda en, p4=p4: en.tensor_copy(cx.sc_f[:, 0:512], vt[:, p4 * 512:(p4 + 1) * 512]),
                   reads=(("vt", p4),), writes=("scf0",), tag="dbg")
            p.dma("sp", cx.d_out[(12 + p4) * 128:(13 + p4) * 128, 0:512], cx.sc_f[:, 0:512], reads=("scf0",), tag="dbg")
        return
    for qt in range(4):
        nkt = 8 * (qt + 1)
        for head in range(16):
            g = head // 4
            cB = f"pb{2 + head % 2}"
            for kt in range(nkt):
                if kt < 4 * (qt + 1):
                    kcol = kt * 128
                else:
                    kcol = 2048 + (kt - 4 * (qt + 1)) * 128
                sB = f"pb{0 + kt % 2}"
                pb_s = cx.pb[0 + kt % 2]
                ptbuf = ptb[kt % 2]
                ptk = ("ptb", kt % 2)
                p.emit("pe", lambda en, g=g, kcol=kcol, head=head, qt=qt, pb_s=pb_s: en.matmul(
                    pb_s[:, :], kT[:, g * 4096 + kcol: g * 4096 + kcol + 128],
                    A[:, head * 2048 + qt * 512: head * 2048 + (qt + 1) * 512],
                    start=True, stop=True),
                    reads=(("kT", g, kcol // 512), akey(head, qt)), writes=(sB,), tag="l0as")
                p.emit("act", lambda en, pb_s=pb_s, ptbuf=ptbuf: en.activation(
                    ptbuf[:, :], pb_s[:, :], AF.Exp, bias=cx.cst[:, 1:2], scale=SCALE),
                    reads=(sB, "cst"), writes=(ptk,), tag="l0ae")
                if 4 * qt <= kt < 4 * (qt + 1):
                    j = kt - 4 * qt
                    p.emit("dve", lambda en, j=j, ptbuf=ptbuf: en.tensor_mul(
                        ptbuf[:, :], ptbuf[:, :], cx.dmask[:, j * 512:(j + 1) * 512]),
                        reads=(ptk, "dmask"), writes=(ptk,), tag="l0am")
                if kt >= nkt - 4:
                    p.emit("dve", lambda en, ptbuf=ptbuf: en.tensor_scalar_mul(
                        ptbuf[:, :], ptbuf[:, :], hf),
                        reads=(ptk, "hflag"), writes=(ptk,), tag="l0am")
                p.emit("pe", lambda en, kcol=kcol, g=g, kt=kt, nkt=nkt, ptbuf=ptbuf, cBi=2 + head % 2: en.matmul(
                    cx.pb[cBi][:, :], vt[:, (kcol // 128) * 512 + g * 128: (kcol // 128) * 512 + (g + 1) * 128],
                    ptbuf[:, :], start=(kt == 0), stop=(kt == nkt - 1)),
                    reads=(("vt", kcol // 128), ptk), writes=(cB,), tag="l0ac")
                p.emit("pe", lambda en, kt=kt, nkt=nkt, ptbuf=ptbuf: en.matmul(
                    cx.pb[4][0:1, :], cx.onesc[:, :], ptbuf[:, :],
                    start=(kt == 0), stop=(kt == nkt - 1)),
                    reads=("onesc", ptk), writes=("pb4",), tag="l0ar")
            rsum = cx.rows[0:1, 0:512]
            p.emit("dve", lambda en, rsum=rsum: en.reciprocal(rsum, cx.pb[4][0:1, :]),
                   reads=("pb4",), writes=("r_m",), tag="l0an")
            p.emit("pe", lambda en, rsum=rsum: en.matmul(
                cx.pb[7][:, :], cx.onesr[:, :], rsum, start=True, stop=True),
                reads=("r_m", "onesr"), writes=("pb7",), tag="l0an")
            p.emit("act", lambda en: en.copy(cx.sc_f[:, 0:512], cx.pb[7][:, :]),
                   reads=("pb7",), writes=("scf0",), tag="l0an")
            p.emit("dve", lambda en, head=head, cBi=2 + head % 2: en.tensor_mul(
                ctxT[:, head * 512:(head + 1) * 512], cx.pb[cBi][:, :], cx.sc_f[:, 0:512]),
                reads=(cB, "scf0"), writes=(("ctx", head),), tag="l0an")
        for oc in range(16):
            wb, wk = (wsA, "wsA") if oc % 2 == 0 else (wsB, "wsB")
            p.dma("pool", wb[:, :].rearrange("p (c n) -> p c n", n=128),
                  cx.d_wd[oc, :, :].rearrange("(c p) n -> p c n", p=128),
                  writes=(wk,), tag="l0aw")
            x0sb = cx.sc_f[:, 512:1024]
            p.dma("sp", x0sb, cx.d_xt[oc * 128:(oc + 1) * 128, qt * 512:(qt + 1) * 512],
                  reads=("scf1",), writes=("scf1",), tag="l0ax0")
            dB = f"pb{5 + oc % 2}"
            for ec in range(16):
                p.emit("pe", lambda en, wb=wb, ec=ec, dBi=5 + oc % 2: en.matmul(
                    cx.pb[dBi][:, :], wb[:, ec * 128:(ec + 1) * 128],
                    ctxT[:, ec * 512:(ec + 1) * 512], start=(ec == 0), stop=(ec == 15)),
                    reads=(wk, ("ctx", ec)), writes=(dB,), tag="l0ad")
            p.emit("dve", lambda en, oc=oc, qt=qt, dBi=5 + oc % 2: en.tensor_add(
                A[:, oc * 2048 + qt * 512: oc * 2048 + (qt + 1) * 512],
                cx.pb[dBi][:, :], x0sb),
                reads=(dB, "scf1", akey(oc, qt)), writes=(akey(oc, qt),), tag="l0adr")


def bld_l0f(cx, fences=()):
    p, nc = cx.p, cx.nc
    A, xn, hT = cx.A, cx.R1, cx.R2
    fr = tuple(fences)
    akey = lambda c, t: ("A", c, t)
    xnkey = lambda c, t: ("l0fxn", c, t)

    wgb = cx.WS[:, 0:2048]
    wub_ = cx.WS[:, 2048:4096]
    wdb0 = cx.WS[:, 4096:5504]
    wdb1 = cx.WS[:, 2048:3456]
    wr_sb = cx.WS[:, 5504:5632]
    oh_sb = cx.WS[0:8, 5632:6144]
    p.dma("pool", wr_sb, cx.d_wr[:, :], writes=("wrsb",), tag="l0fw")
    p.dma("sp", oh_sb, cx.onehot64[:, :], writes=("oh64",), tag="l0fw")

    for tt in range(4):
        ln_normalize(cx, A, lambda c: akey(c, tt), xn, lambda c: xnkey(c, tt), tt, extra_reads=fr)

    gate_math(cx, xn, xnkey, wr_sb, "wrsb", "c8", lambda tc: ("crows", tc))
    crk = tuple(("crows", tc) for tc in range(16))

    if cx.dbg == "f_c":
        for q in range(4):
            p.emit("dve", lambda en, q=q: en.tensor_copy(cx.sc_f[0:8, 0:512], cx.crows[0:8, q * 512:(q + 1) * 512]),
                   reads=crk, writes=("scf0",), tag="dbg")
            p.dma("sp", cx.d_out[0:8, q * 512:(q + 1) * 512], cx.sc_f[0:8, 0:512], reads=("scf0",), tag="dbg")
        return

    sscr = cx.sc_b[:, 0:512]

    for e in range(1 if cx.dbg == 'f_e0' else 8):
        for i in range(NIC):
            p.dma("pool", wgb[:, :].rearrange("p (c n) -> p c n", n=128),
                  cx.d_wg[e, i, :, :].rearrange("(c p) n -> p c n", p=128),
                  writes=("wsA",), tag="l0fw")
            for d in range(16):
                for tt in range(4):
                    p.emit("pe", lambda en, d=d, tt=tt: en.matmul(
                        cx.pb[tt][:, :], wgb[:, d * 128:(d + 1) * 128],
                        xn[:, d * 2048 + tt * 512: d * 2048 + tt * 512 + 512],
                        start=(d == 0), stop=(d == 15)),
                        reads=("wsA", xnkey(d, tt)), writes=(f"pb{tt}",), tag="l0fg")
            p.dma("pool", wub_[:, :].rearrange("p (c n) -> p c n", n=128),
                  cx.d_wu[e, i, :, :].rearrange("(c p) n -> p c n", p=128),
                  writes=("wsB",), tag="l0fw")
            for d in range(16):
                for tt in range(4):
                    p.emit("pe", lambda en, d=d, tt=tt: en.matmul(
                        cx.pb[4 + tt][:, :], wub_[:, d * 128:(d + 1) * 128],
                        xn[:, d * 2048 + tt * 512: d * 2048 + tt * 512 + 512],
                        start=(d == 0), stop=(d == 15)),
                        reads=("wsB", xnkey(d, tt)), writes=(f"pb{4 + tt}",), tag="l0fu")
            for tt in range(4):
                p.emit("act", lambda en, tt=tt: en.activation(sscr, cx.pb[tt][:, :], AF.Silu),
                       reads=(f"pb{tt}",), writes=("lnsq0",), tag="l0fh")
                p.emit("dve", lambda en, i=i, tt=tt: en.tensor_mul(
                    hT[:, i * 2048 + tt * 512: i * 2048 + tt * 512 + 512], sscr, cx.pb[4 + tt][:, :]),
                    reads=("lnsq0", f"pb{4 + tt}") + fr, writes=(("hT", i, tt),), tag="l0fh")
        if cx.dbg == "f_h" and e == 0:
            for i2 in (0, 5):
                for q in range(4):
                    p.emit("dve", lambda en, i2=i2, q=q: en.tensor_copy(
                        cx.sc_f[:, 0:512], hT[:, i2 * 2048 + q * 512: i2 * 2048 + q * 512 + 512]),
                        reads=(("hT", i2, q),), writes=("scf0",), tag="dbg")
                    p.dma("sp", cx.d_out[(0 if i2 == 0 else 128):(128 if i2 == 0 else 256), q * 512:(q + 1) * 512],
                          cx.sc_f[:, 0:512], reads=("scf0",), tag="dbg")
            return
        for tt in range(4):
            p.emit("pe", lambda en, e=e, tt=tt: en.matmul(
                cx.pb[6][0:64, :], oh_sb[0:8, e * 64:(e + 1) * 64],
                cx.crows[0:8, tt * 512:(tt + 1) * 512], start=True, stop=True),
                reads=("oh64",) + crk, writes=("pb6",), tag="l0fcb")
            p.emit("act", lambda en: en.copy(sscr[0:64, :], cx.pb[6][0:64, :]),
                   reads=("pb6",), writes=("lnsq0",), tag="l0fcb")
            p.emit("act", lambda en: en.copy(sscr[64:128, :], cx.pb[6][0:64, :]),
                   reads=("pb6",), writes=("lnsq0",), tag="l0fcb")
            if cx.dbg == "f_cb" and e == 0:
                p.emit("dve", lambda en: en.tensor_copy(cx.sc_f[:, 0:512], sscr),
                       reads=("lnsq0",), writes=("scf0",), tag="dbg")
                p.dma("sp", cx.d_out[0:128, tt * 512:(tt + 1) * 512], cx.sc_f[:, 0:512],
                      reads=("scf0",), tag="dbg")
            for i in range(NIC):
                p.emit("dve", lambda en, i=i, tt=tt: en.tensor_mul(
                    hT[:, i * 2048 + tt * 512: i * 2048 + tt * 512 + 512],
                    hT[:, i * 2048 + tt * 512: i * 2048 + tt * 512 + 512], sscr),
                    reads=(("hT", i, tt), "lnsq0"), writes=(("hT", i, tt),), tag="l0fcs")
        if cx.dbg == "f_h2" and e == 0:
            for i2 in (0, 5):
                for q in range(4):
                    p.emit("dve", lambda en, i2=i2, q=q: en.tensor_copy(
                        cx.sc_f[:, 0:512], hT[:, i2 * 2048 + q * 512: i2 * 2048 + q * 512 + 512]),
                        reads=(("hT", i2, q),), writes=("scf0",), tag="dbg")
                    p.dma("sp", cx.d_out[(0 if i2 == 0 else 128):(128 if i2 == 0 else 256), q * 512:(q + 1) * 512],
                          cx.sc_f[:, 0:512], reads=("scf0",), tag="dbg")
            return
        if cx.dbg == "f_cb" and e == 0:
            return
        for oc in range(16):
            wdb = wdb0 if oc % 2 == 0 else wdb1
            wdk = "wsC" if oc % 2 == 0 else "wsB"
            p.dma("pool", wdb[:, :].rearrange("p (c n) -> p c n", n=128),
                  cx.d_wdd[e, oc, :, :].rearrange("(c p) n -> p c n", p=128),
                  writes=(wdk,), tag="l0fw")
            for i in range(NIC):
                for tt in range(4):
                    p.emit("pe", lambda en, i=i, tt=tt, wdb=wdb: en.matmul(
                        cx.pb[tt][:, :], wdb[:, i * 128:(i + 1) * 128],
                        hT[:, i * 2048 + tt * 512: i * 2048 + tt * 512 + 512],
                        start=(i == 0), stop=(i == NIC - 1)),
                        reads=(wdk, ("hT", i, tt)), writes=(f"pb{tt}",), tag="l0fy")
            for tt in range(4):
                p.emit("dve", lambda en, oc=oc, tt=tt: en.tensor_add(
                    A[:, oc * 2048 + tt * 512: oc * 2048 + tt * 512 + 512],
                    A[:, oc * 2048 + tt * 512: oc * 2048 + tt * 512 + 512], cx.pb[tt][:, :]),
                    reads=(f"pb{tt}", akey(oc, tt)), writes=(akey(oc, tt),), tag="l0fres")


def bld_l1a(cx, fences=()):
    p, nc = cx.p, cx.nc
    A = cx.A
    xs = cx.R1
    fr = tuple(fences)
    akey = lambda c, t: ("A", c, t)
    xskey = lambda c, t: ("l1xs", c, t)

    l1kv = cx.R2[0:64, 0:4096]
    stateb = cx.R2[0:64, 4096:8192]
    wo_sb = cx.R2[0:64, 8192:10240]
    r2addr = nc.lookup_mloc(cx.R2).addr
    l1buf = nc.alloc_sbuf_tensor_at("l1buf", [128, 4096], F32, offset=r2addr + 20480)
    scr64 = cx.sc_b[0:64, 1024:3072]

    gcol = cx.sml[:, 8:24]
    omg = cx.sml[:, 24:40]
    lnw_sb = cx.sml[:, 40:56]
    lnb_sb = cx.sml[:, 56:72]
    wcol = cx.sml[:, 72:73]
    netf = cx.sml[0:64, 73:74]
    p.dma("sp", gcol, cx.d_gate[:, :], writes=("l1gate",), tag="l1c")
    p.dma("sp", lnw_sb, cx.d_lnw[:, :], writes=("lnw",), tag="l1c")
    p.dma("sp", lnb_sb, cx.d_lnb[:, :], writes=("lnb",), tag="l1c")
    p.dma("sp", wcol, cx.d_td[:, :], writes=("l1td",), tag="l1c")
    p.dma("sp", netf, cx.d_tf[:, :], writes=("l1tf",), tag="l1c")
    p.dma("sp", cx.i32[0:1, 0:1], cx.d_hoff[:, :], writes=("hoff",), tag="l1c")
    p.emit("act", lambda e: e.activation(gcol, gcol, AF.Sigmoid), reads=("l1gate",), writes=("l1gate",), tag="l1c")
    p.emit("dve", lambda e: e.tensor_scalar(omg, gcol, -1.0, 1.0, ALU.mult, ALU.add),
           reads=("l1gate",), writes=("l1omg",), tag="l1c")
    p.emit("act", lambda e: e.activation(wcol, wcol, AF.Exp), reads=("l1td",), writes=("l1td",), tag="l1c")
    p.emit("act", lambda e: e.activation(netf, netf, AF.Exp), reads=("l1tf",), writes=("l1tf",), tag="l1c")
    p.emit("dve", lambda e: e.tensor_scalar_mul(netf, netf, -1.0), reads=("l1tf",), writes=("l1tf",), tag="l1c")

    wtsb = [cx.WS[:, b * 2048:(b + 1) * 2048] for b in range(2)]
    wk_sb = cx.WS[:, 4096:5120]
    wv_sb = cx.WS[:, 5120:6144]
    p.dma("pool", wk_sb.rearrange("p (c n) -> p c n", n=64),
          cx.d_wkT[:, :].rearrange("(c p) n -> p c n", p=128), writes=("wks",), tag="l1w")
    p.dma("pool", wv_sb.rearrange("p (c n) -> p c n", n=64),
          cx.d_wvT[:, :].rearrange("(c p) n -> p c n", p=128), writes=("wvs",), tag="l1w")
    p.dma("pool", wo_sb, cx.d_woT[:, :], writes=("wos",), tag="l1w")

    t2 = cx.sc_b[:, 0:512]
    for ec in range(16):
        b = ec % 2
        p.dma("pool", wtsb[b][:, :].rearrange("p (c n) -> p c n", n=128),
              cx.d_wts[ec, :, :].rearrange("(c p) n -> p c n", p=128),
              writes=(("wts", b),), tag="l1w")
        for tt in range(4):
            rtt = (tt + 2) % 4
            for d in range(16):
                p.emit("pe", lambda e, b=b, d=d, rtt=rtt: e.matmul(
                    cx.pb[tt % 2][:, :] if False else cx.pb[0][:, :],
                    wtsb[b][:, d * 128:(d + 1) * 128],
                    A[:, d * 2048 + rtt * 512: d * 2048 + rtt * 512 + 512],
                    start=(d == 0), stop=(d == 15)),
                    reads=(("wts", b), akey(d, rtt)) + fr, writes=("pb0",), tag="l1mm")
            p.emit("dve", lambda e, ec=ec, tt=tt: e.tensor_scalar_mul(
                t2, A[:, ec * 2048 + tt * 512: ec * 2048 + tt * 512 + 512], omg[:, ec:ec + 1]),
                reads=(akey(ec, tt), "l1omg") + fr, writes=("lnsq0",), tag="l1mix")
            p.emit("dve", lambda e, ec=ec, tt=tt: e.scalar_tensor_tensor(
                xs[:, ec * 2048 + tt * 512: ec * 2048 + tt * 512 + 512],
                cx.pb[0][:, :], gcol[:, ec:ec + 1], t2, ALU.mult, ALU.add),
                reads=("pb0", "l1gate", "lnsq0") + fr, writes=(xskey(ec, tt),), tag="l1mix")

    for tt in range(4):
        ln_normalize(cx, xs, lambda c: xskey(c, tt), xs, lambda c: xskey(c, tt), tt,
                     lnwb=(lnw_sb, lnb_sb))

    for tt in range(4):
        for d in range(16):
            p.emit("pe", lambda e, d=d, tt=tt: e.matmul(
                cx.pb[1][0:64, :], wk_sb[:, d * 64:(d + 1) * 64],
                xs[:, d * 2048 + tt * 512: d * 2048 + tt * 512 + 512],
                start=(d == 0), stop=(d == 15)),
                reads=("wks", xskey(d, tt)), writes=("pb1",), tag="l1kv")
        for d in range(16):
            p.emit("pe", lambda e, d=d, tt=tt: e.matmul(
                cx.pb[2][0:64, :], wv_sb[:, d * 64:(d + 1) * 64],
                xs[:, d * 2048 + tt * 512: d * 2048 + tt * 512 + 512],
                start=(d == 0), stop=(d == 15)),
                reads=("wvs", xskey(d, tt)), writes=("pb2",), tag="l1kv")
        p.emit("dve", lambda e, tt=tt: e.tensor_copy(l1kv[0:64, tt * 512:(tt + 1) * 512], cx.pb[1][0:64, :]),
               reads=("pb1",), writes=(("l1k", tt),), tag="l1kv")
        p.emit("dve", lambda e, tt=tt: e.tensor_copy(l1kv[0:64, 2048 + tt * 512: 2048 + (tt + 1) * 512], cx.pb[2][0:64, :]),
               reads=("pb2",), writes=(("l1v", tt),), tag="l1kv")

    kread = tuple(("l1k", t) for t in range(4))
    vread = tuple(("l1v", t) for t in range(4))
    p.emit("act", lambda e: e.activation(scr64, l1kv[0:64, 0:2048], AF.Sigmoid),
           reads=kread, writes=("l1s1",), tag="l1wkv")
    p.emit("dve", lambda e: e.tensor_scalar_mul(scr64, scr64, netf[0:64, :]),
           reads=("l1s1", "l1tf"), writes=("l1s1",), tag="l1wkv")
    p.emit("act", lambda e: e.activation(scr64, scr64, AF.Exp),
           reads=("l1s1",), writes=("l1s1",), tag="l1wkv")
    p.emit("dve", lambda e: e.tensor_mul(l1buf[0:64, 0:2048], scr64, l1kv[0:64, 2048:4096]),
           reads=("l1s1",) + vread, writes=("wkvown",), tag="l1wkv")

    p.dma("pool", cx.d_ccin[:, :], l1buf[0:64, 0:2048], reads=("wkvown",), tag="l1cc")
    p.emit("pool", lambda e: e.collective_compute(
        "AllGather", ALU.bypass, replica_groups=REPLICA_PAIRS,
        ins=[cx.d_ccin.ap().opt()], outs=[cx.d_ccout.ap().opt()]),
        reads=("wkvown",), writes=("ccout",), dma="cc", tag="l1cc")
    for g in range(8):
        p.dma("pool", l1buf[64:128, g * 512:(g + 1) * 512],
              cx.d_ccout[(g % 2) * 64:(g % 2) * 64 + 64, (g // 2) * 512:(g // 2) * 512 + 512],
              reads=("ccout",), writes=(("wkvg", g),), tag="l1cc")

    sml_h = cx.sml
    smlw = sml_h.shape[1]
    wbcast = bass.AP(sml_h, 64 * smlw + 72, [[smlw, 64], [0, 4096]])
    p.emit("dve", lambda e: e.tensor_tensor_scan(
        l1buf[0:64, :], wbcast, l1buf[64:128, :], 0.0, ALU.mult, ALU.add),
        reads=("l1td",) + tuple(("wkvg", g) for g in range(8)) + ("wkvown",),
        writes=("states", "wkvown"), tag="l1scan")
    p.emit("dve", lambda e: e.tensor_copy(stateb, l1buf[0:64, :]),
           reads=("states",), writes=("stateb",), tag="l1scan")

    hreg = {}

    def loadreg(e):
        hreg["r"] = nc.values_load(cx.i32[0:1, 0:1], engines=[ET.PE], min_val=0, max_val=512,
                                   skip_runtime_bounds_check=True)
        return None
    p.emit("pe", loadreg, reads=("hoff",), writes=(), tag="l1reg")
    for oc in range(16):
        for c in range(4):
            p.emit("pe", lambda e, oc=oc, c=c: e.matmul(
                cx.pb[3][:, :], wo_sb[:, oc * 128:(oc + 1) * 128],
                stateb[0:64, ds(hreg["r"] + c * 1024, 512)],
                start=True, stop=True),
                reads=("wos", "stateb", "hoff"), writes=("pb3",), tag="l1out")
            p.emit("dve", lambda e, oc=oc, c=c: e.tensor_add(
                A[:, oc * 2048 + c * 512: oc * 2048 + c * 512 + 512],
                A[:, oc * 2048 + c * 512: oc * 2048 + c * 512 + 512], cx.pb[3][:, :]),
                reads=("pb3", akey(oc, c)), writes=(akey(oc, c),), tag="l1res")



def kernel(**inputs):
    return run_blocks_unfused(inputs, [["l0a"], ["l0f"], ["l1a"], ["l1f"]])


# version 02
# speedup vs baseline: 1.0005x; 1.0005x over previous
import math
import numpy as np
import ml_dtypes
import concourse.bass as bass
import concourse.mybir as mybir
from concourse.bass_utils import run_bass_kernel_spmd

F32 = mybir.dt.float32
BF16 = mybir.dt.bfloat16
I32 = mybir.dt.int32
AF = mybir.ActivationFunctionType
ALU = mybir.AluOpType
ET = mybir.EngineType
ds = bass.ds
BF = ml_dtypes.bfloat16

D = 2048
NCH = 16
NT = 4096
OWN = 2048
TT = 512
NTT = OWN // TT
HEADS, GROUPS, KVC = 16, 4, 128
FI, NIC = 1408, 11
NE = 8
SCALE = KVC ** (-0.5)
EXPB = -20.0
ENG_LIST = ("pe", "act", "dve", "pool", "sp")
ENG = {"pe": "tensor", "act": "scalar", "dve": "vector", "pool": "gpsimd", "sp": "sync"}
SAME_ENGINE_SYNC = True
RELU_ON_DVE = True
REPLICA_PAIRS = [[0, 1], [2, 3], [4, 5], [6, 7]]


class Op:
    __slots__ = ("idx", "eng", "fn", "reads", "writes", "dma", "deps", "needed", "sem", "tag")

    def __init__(self, idx, eng, fn, reads, writes, dma, tag):
        self.idx, self.eng, self.fn = idx, eng, fn
        self.reads, self.writes, self.dma, self.tag = tuple(reads), tuple(writes), dma, tag
        self.deps, self.needed, self.sem = (), False, None


class Prog:
    def __init__(self, nc):
        self.nc = nc
        self.ops = []

    def emit(self, eng, fn, reads=(), writes=(), dma=None, tag=""):
        op = Op(len(self.ops), eng, fn, reads, writes, dma, tag)
        self.ops.append(op)
        if not hasattr(self, "seen_keys"):
            self.seen_keys = set()
        self.seen_keys.update(writes)
        return op

    def barrier(self):
        keys = tuple(getattr(self, "seen_keys", ()))
        return self.emit("dve", lambda e: e.nop(), reads=(), writes=keys, tag="barrier")

    def dma(self, queue, out, in_, reads=(), writes=(), tag=""):
        return self.emit(queue, lambda e, o=out, i=in_: e.dma_start(out=o, in_=i),
                         reads=reads, writes=writes, dma=queue, tag=tag)

    def fence(self, tag_key, old_keys, eng="dve"):
        return self.emit(eng, lambda e: e.nop(), reads=(), writes=tuple(old_keys) + (tag_key,),
                         tag="fence")

    def finalize(self):
        nc = self.nc
        last_write, readers = {}, {}
        last_barrier = None
        for op in self.ops:
            deps = {}
            for k in op.reads:
                d = last_write.get(k)
                if d is not None:
                    deps[d.idx] = d
            for k in op.writes:
                d = last_write.get(k)
                if d is not None:
                    deps[d.idx] = d
                for r in readers.get(k, ()):
                    deps[r.idx] = r
            pruned = []
            for d in deps.values():
                if d is op:
                    continue
                if d.dma is None and d.eng == op.eng and (d.eng == "pe" or not SAME_ENGINE_SYNC):
                    continue
                if d.dma is not None and d.dma == op.dma:
                    continue
                pruned.append(d)
            if last_barrier is not None and op.tag != "barrier":
                if all(d.idx < last_barrier.idx for d in pruned):
                    pruned.append(last_barrier)
            op.deps = tuple(pruned)
            for d in pruned:
                d.needed = True
            for k in op.reads:
                readers.setdefault(k, []).append(op)
            for k in op.writes:
                last_write[k] = op
                readers[k] = []
            if op.tag == "barrier":
                last_barrier = op

        NLANES_BY_Q = {"pool": 4, "sp": 8, "act": 8}
        counters = {}
        qcount = {}
        sems = set()
        for op in self.ops:
            if op.dma is not None:
                if op.dma == "cc":
                    q = "q_cc"
                    counters[q] = counters.get(q, 0) + 1
                    op.sem = (q, counters[q], 1)
                else:
                    i = qcount.get(op.dma, 0)
                    qcount[op.dma] = i + 1
                    q = f"q_{op.dma}{i % NLANES_BY_Q[op.dma]}"
                    counters[q] = counters.get(q, 0) + 16
                    op.sem = (q, counters[q], 16)
                sems.add(q)
            elif op.needed:
                q = "e_" + op.eng
                counters[q] = counters.get(q, 0) + 1
                op.sem = (q, counters[q], 1)
                sems.add(q)
        self.sem_handles = {n: nc.alloc_semaphore(n) for n in sorted(sems)}

        by_eng = {}
        for op in self.ops:
            by_eng.setdefault(op.eng, []).append(op)

        def emitter(ops_list):
            def run(engine):
                wm = {}
                for op in ops_list:
                    for d in op.deps:
                        sn, val, _ = d.sem
                        if wm.get(sn, 0) < val:
                            engine.wait_ge(self.sem_handles[sn], val)
                            wm[sn] = val
                    if op.dma is not None and op.dma != "cc":
                        sn, val, _ = op.sem
                        if val > 16 and wm.get(sn, 0) < val - 16:
                            engine.wait_ge(self.sem_handles[sn], val - 16)
                            wm[sn] = val - 16
                    inst = op.fn(engine)
                    if op.sem is not None:
                        sn, _, inc = op.sem
                        if inst is None:
                            inst = engine.nop()
                        inst.then_inc(self.sem_handles[sn], inc)
            return run

        with nc.Block() as block:
            for eng, ops_list in by_eng.items():
                getattr(block, ENG[eng])(emitter(ops_list))

    def stats(self):
        from collections import Counter
        c = Counter(op.eng for op in self.ops)
        return dict(ops=dict(c), n_ops=len(self.ops),
                    n_incs=sum(1 for o in self.ops if o.sem is not None),
                    n_waits=sum(len(o.deps) for o in self.ops))


def _own_perm(h):
    return np.r_[np.arange(h, 8, 2), np.arange(1 - h, 8, 2)]


def _colblocks(w, nb):
    R, C = w.shape
    cb = C // nb
    return np.ascontiguousarray(w.reshape(R, nb, cb).transpose(1, 0, 2))


def stage_shared(inp):
    s = {}
    f32 = np.float32
    s["wq_r"] = _colblocks(np.asarray(inp["L0_attn_w_q"], f32), 16)
    wkv = np.asarray(inp["L0_attn_w_kv"], f32)
    s["wkvk_r"] = _colblocks(wkv[:, :512], 4)
    s["wkvv_r"] = np.ascontiguousarray(wkv[:, 512:].reshape(16, 128, 512))
    s["wd_r"] = _colblocks(np.asarray(inp["L0_attn_w_dense"], f32), 16)
    s["wr"] = np.ascontiguousarray(np.asarray(inp["L0_ffn_Wr"], f32).reshape(16, 128, 8).transpose(1, 0, 2).reshape(128, 128))
    wg = np.asarray(inp["L0_ffn_Wg"], f32)
    wu = np.asarray(inp["L0_ffn_Wu"], f32)
    s["wg_r"] = np.ascontiguousarray(wg.reshape(8, 2048, 11, 128).transpose(0, 2, 1, 3))
    s["wu_r"] = np.ascontiguousarray(wu.reshape(8, 2048, 11, 128).transpose(0, 2, 1, 3))
    wd = np.asarray(inp["L0_ffn_Wd"], f32)
    s["wdd_r"] = np.ascontiguousarray(wd.reshape(8, 1408, 16, 128).transpose(0, 2, 1, 3))
    s["wts_r"] = _colblocks(np.ascontiguousarray(np.asarray(inp["L1_attn_Wts"], f32).T), 16)
    s["wkT"] = np.ascontiguousarray(np.asarray(inp["L1_attn_Wk"], f32).T)
    s["wvT"] = np.ascontiguousarray(np.asarray(inp["L1_attn_Wv"], f32).T)
    s["woT"] = np.ascontiguousarray(np.asarray(inp["L1_attn_Wo"], f32).T)
    s["gate2d"] = np.ascontiguousarray(np.asarray(inp["L1_attn_gate"], f32).reshape(16, 128).T)
    s["lnw2d"] = np.ascontiguousarray(np.asarray(inp["L1_attn_ln_w"], f32).reshape(16, 128).T)
    s["lnb2d"] = np.ascontiguousarray(np.asarray(inp["L1_attn_ln_b"], f32).reshape(16, 128).T)
    s["td"] = np.ascontiguousarray(np.tile(np.asarray(inp["L1_attn_time_decay"], f32).reshape(64, 1), (2, 1)))
    s["tf"] = np.ascontiguousarray(np.asarray(inp["L1_attn_time_first"], f32).reshape(64, 1))
    s["wupT_r"] = _colblocks(np.ascontiguousarray(np.asarray(inp["L1_ffn_W_up"], f32).T), 16)
    s["wdownT_r"] = _colblocks(np.ascontiguousarray(np.asarray(inp["L1_ffn_W_down"], f32).T), 16)
    s["wexp_r"] = _colblocks(np.asarray(inp["L1_ffn_W_expert"], f32), 16)
    s["wgT"] = np.ascontiguousarray(np.asarray(inp["L1_ffn_W_gate"], f32).T.reshape(16, 128, 8).transpose(1, 0, 2).reshape(128, 128))
    return s


def stage_core(inp, c):
    b, h = c // 2, c % 2
    x = np.asarray(inp["L0_attn_x"], np.float32)[b]
    xt = np.ascontiguousarray(x.T)
    perm = _own_perm(h)
    xtp = np.ascontiguousarray(xt.reshape(2048, 8, 512)[:, perm, :].reshape(2048, 4096))
    return {
        "xt": xtp,
        "hflag": np.full((128, 1), float(h), dtype=np.float32),
        "hoff": np.array([[h * 512]], dtype=np.int32),
    }


def make_consts():
    c = {}
    c["ident_bf"] = np.eye(128, dtype=BF)
    c["ident_f32"] = np.eye(128, dtype=np.float32)
    c["ones_col_bf"] = np.ones((128, 1), dtype=BF)
    c["ones_row_f32"] = np.ones((1, 128), dtype=np.float32)
    c["ones_row_bf"] = np.ones((1, 128), dtype=BF)
    dm = np.zeros((128, 4 * 512), dtype=BF)
    for j in range(4):
        kk = np.arange(128)[:, None] + j * 128
        q = np.arange(512)[None, :]
        dm[:, j * 512:(j + 1) * 512] = (kk <= q).astype(BF)
    c["diagmask"] = dm
    oh = np.zeros((8, 8, 64), dtype=BF)
    for e in range(8):
        oh[e, e, :] = 1
    c["onehot64"] = oh.reshape(8, 512)
    cc = np.zeros((128, 8), dtype=np.float32)
    cc[:, 0] = 1e-5
    cc[:, 1] = EXPB
    c["cstcols"] = cc
    return c


class Ctx:
    pass


def make_ctx(nc, blocks):
    cx = Ctx()
    cx.nc = nc
    cx.p = Prog(nc)
    dt = nc.dram_tensor

    need = set(blocks)
    if "l0a" in need:
        cx.d_xt = dt("xt", [2048, 4096], F32, kind="ExternalInput")
        cx.d_wq = dt("wq_r", [16, 2048, 128], F32, kind="ExternalInput")
        cx.d_wkvk = dt("wkvk_r", [4, 2048, 128], F32, kind="ExternalInput")
        cx.d_wkvv = dt("wkvv_r", [16, 128, 512], F32, kind="ExternalInput")
        cx.d_wd = dt("wd_r", [16, 2048, 128], F32, kind="ExternalInput")
        cx.d_hflag = dt("hflag", [128, 1], F32, kind="ExternalInput")
    if "l0f" in need:
        cx.d_wr = dt("wr", [128, 128], F32, kind="ExternalInput")
        cx.d_wg = dt("wg_r", [8, 11, 2048, 128], F32, kind="ExternalInput")
        cx.d_wu = dt("wu_r", [8, 11, 2048, 128], F32, kind="ExternalInput")
        cx.d_wdd = dt("wdd_r", [8, 16, 1408, 128], F32, kind="ExternalInput")
    if "l1a" in need:
        cx.d_wts = dt("wts_r", [16, 2048, 128], F32, kind="ExternalInput")
        cx.d_wkT = dt("wkT", [2048, 64], F32, kind="ExternalInput")
        cx.d_wvT = dt("wvT", [2048, 64], F32, kind="ExternalInput")
        cx.d_woT = dt("woT", [64, 2048], F32, kind="ExternalInput")
        cx.d_gate = dt("gate2d", [128, 16], F32, kind="ExternalInput")
        cx.d_lnw = dt("lnw2d", [128, 16], F32, kind="ExternalInput")
        cx.d_lnb = dt("lnb2d", [128, 16], F32, kind="ExternalInput")
        cx.d_td = dt("td", [128, 1], F32, kind="ExternalInput")
        cx.d_tf = dt("tf", [64, 1], F32, kind="ExternalInput")
        cx.d_hoff = dt("hoff", [1, 1], I32, kind="ExternalInput")
        cx.d_ccin = dt("ccin", [64, 2048], F32, kind="Internal")
        cx.d_ccout = dt("ccout", [128, 2048], F32, kind="Internal")
    if "l1f" in need:
        cx.d_wupT = dt("wupT_r", [16, 2048, 128], F32, kind="ExternalInput")
        cx.d_wdownT = dt("wdownT_r", [16, 2048, 128], F32, kind="ExternalInput")
        cx.d_wexp = dt("wexp_r", [16, 2048, 128], F32, kind="ExternalInput")
        cx.d_wgT = dt("wgT", [128, 128], F32, kind="ExternalInput")

    first = blocks[0]
    cx.unfused_in = None
    if first != "l0a":
        cx.unfused_in = dt("xin", [2048, 2048], F32, kind="ExternalInput")
    last = blocks[-1]
    if last == "l1f":
        cx.d_out = dt("xout", [2048, 2048], F32, kind="ExternalOutput")
    else:
        cx.d_out = dt("xout", [2048, 2048], F32, kind="ExternalOutput")

    cn = make_consts()
    cx.ident_bf = nc.inline_tensor(cn["ident_bf"], name="ident_bf")
    cx.ident_f32 = nc.inline_tensor(cn["ident_f32"], name="ident_f32")
    cx.ones_col = nc.inline_tensor(cn["ones_col_bf"], name="ones_col")
    cx.ones_row = nc.inline_tensor(cn["ones_row_f32"], name="ones_row")
    cx.ones_row_bf = nc.inline_tensor(cn["ones_row_bf"], name="ones_row_bf")
    cx.diagmask = nc.inline_tensor(cn["diagmask"], name="diagmask")
    cx.cstcols = nc.inline_tensor(cn["cstcols"], name="cstcols")
    cx.onehot64 = nc.inline_tensor(cn["onehot64"], name="onehot64")

    al = nc.alloc_sbuf_tensor
    cx.pad = al("pad", [128, 1024], mybir.dt.uint8)
    cx.A = al("A", [128, 16 * 2048], BF16)
    cx.R1 = al("R1", [128, 32768], BF16)
    cx.R2 = al("R2", [128, 22528], BF16)
    cx.WS = al("WS", [128, 6144], BF16)
    cx.rows = al("rows", [1, 1536], F32)
    cx.sc_f = al("sc_f", [128, 2048], F32)
    cx.sc_b = al("sc_b", [128, 3072], BF16)
    cx.srw = cx.sc_b[0:1, 1024:3072]
    cx.sml = al("sml", [128, 80], F32)
    cx.smlb = al("smlb", [128, 32], BF16)
    cx.i32 = al("i32", [1, 2], I32)

    cx.identb = al("identb", [128, 128], BF16)
    cx.identf = al("identf", [128, 128], F32)
    cx.onesc = al("onesc", [128, 1], BF16)
    cx.onesr = al("onesr", [1, 128], F32)
    cx.onesrb = al("onesrb", [1, 128], BF16)

    cx.cst = al("cst", [128, 8], F32)
    cx.dmask = cx.R2[:, 17408:19456]
    cx.pb = [nc.alloc_psum_tensor(f"pb{k}", [128, 512], F32) for k in range(8)]

    p = cx.p
    p.dma("sp", cx.identb[:, :], cx.ident_bf[:, :], writes=("identb",))
    p.dma("sp", cx.identf[:, :], cx.ident_f32[:, :], writes=("identf",))
    p.dma("sp", cx.onesc[:, :], cx.ones_col[:, :], writes=("onesc",))
    p.dma("sp", cx.onesr[:, :], cx.ones_row[:, :], writes=("onesr",))
    p.dma("sp", cx.onesrb[:, :], cx.ones_row_bf[:, :], writes=("onesrb",))
    if "l0a" in blocks:
        p.dma("sp", cx.dmask[:, :], cx.diagmask[:, :], writes=("dmask",))
    p.dma("sp", cx.cst[:, :], cx.cstcols[:, :], writes=("cst",))
    cx.const_keys = ("identb", "identf", "onesc", "onesr", "dmask")
    return cx


def ln_normalize(cx, src, src_key, dst, dst_key, tt, nchunk=16, colstride=2048,
                 col0=0, lnwb=None, extra_reads=()):
    p, nc = cx.p, cx.nc
    pb = cx.pb
    ps_sum, ps_sq = pb[4], pb[5]
    ps_rb, ps_nb = pb[6], pb[7]
    rows = cx.rows
    sq = cx.sc_b

    def col(c):
        return c * colstride + col0 + tt * 512

    for c in range(nchunk):
        sqslice = sq[:, (c % 2) * 512:(c % 2) * 512 + 512]
        p.emit("act", lambda e, c=c, s=sqslice: e.activation(s, src[:, col(c):col(c) + 512], AF.Square),
               reads=(src_key(c),) + tuple(extra_reads), writes=(f"lnsq{c % 2}",), tag="lnsq")
        p.emit("pe", lambda e, c=c: e.matmul(ps_sum[0:1, :], cx.onesc[:, :], src[:, col(c):col(c) + 512],
                                             start=(c == 0), stop=(c == nchunk - 1)),
               reads=(src_key(c), "onesc") + tuple(extra_reads), writes=("pb4",), tag="lnsum")
        p.emit("pe", lambda e, c=c, s=sqslice: e.matmul(ps_sq[0:1, :], cx.onesc[:, :], s,
                                                        start=(c == 0), stop=(c == nchunk - 1)),
               reads=(f"lnsq{c % 2}", "onesc"), writes=("pb5",), tag="lnsumsq")
    m = rows[0:1, 0:512]
    rs = rows[0:1, 512:1024]
    nm = rows[0:1, 1024:1536]
    tmpv = rs
    msq = nm
    inv = 1.0 / D
    p.emit("dve", lambda e: e.tensor_scalar_mul(m, ps_sum[0:1, :], inv),
           reads=("pb4",), writes=("r_m",), tag="lnrow")
    p.emit("dve", lambda e: e.tensor_scalar_mul(tmpv, ps_sq[0:1, :], inv),
           reads=("pb5",), writes=("r_rs",), tag="lnrow")
    p.emit("dve", lambda e: e.tensor_mul(msq, m, m), reads=("r_m",), writes=("r_nm",), tag="lnrow")
    p.emit("dve", lambda e: e.tensor_sub(tmpv, tmpv, msq), reads=("r_rs", "r_nm"), writes=("r_rs",), tag="lnrow")
    p.emit("act", lambda e: e.activation(rs, tmpv, AF.Ln, bias=cx.cst[0:1, 0:1]),
           reads=("r_rs", "cst"), writes=("r_rs",), tag="lnrow")
    p.emit("act", lambda e: e.activation(rs, rs, AF.Exp, scale=-0.5),
           reads=("r_rs",), writes=("r_rs",), tag="lnrow")
    p.emit("dve", lambda e: e.scalar_tensor_tensor(nm, m, -1.0, rs, ALU.mult, ALU.mult),
           reads=("r_m", "r_rs"), writes=("r_nm",), tag="lnrow")
    p.emit("pe", lambda e: e.matmul(ps_rb[:, :], cx.onesr[:, :], rs, start=True, stop=True),
           reads=("r_rs", "onesr"), writes=("pb6",), tag="lnbc")
    p.emit("pe", lambda e: e.matmul(ps_nb[:, :], cx.onesr[:, :], nm, start=True, stop=True),
           reads=("r_nm", "onesr"), writes=("pb7",), tag="lnbc")
    for c in range(nchunk):
        dsl = dst[:, col(c):col(c) + 512]
        p.emit("dve", lambda e, c=c, dsl=dsl: e.tensor_mul(dsl, src[:, col(c):col(c) + 512], ps_rb[:, :]),
               reads=(src_key(c), "pb6") + tuple(extra_reads), writes=(dst_key(c),), tag="lnn1")
        p.emit("dve", lambda e, c=c, dsl=dsl: e.tensor_add(dsl, dsl, ps_nb[:, :]),
               reads=(dst_key(c), "pb7"), writes=(dst_key(c),), tag="lnn2")
        if lnwb is not None:
            lnw_sb, lnb_sb = lnwb
            p.emit("dve", lambda e, c=c, dsl=dsl: e.tensor_scalar(
                dsl, dsl, lnw_sb[:, c:c + 1], lnb_sb[:, c:c + 1], ALU.mult, ALU.add),
                reads=(dst_key(c), "lnw", "lnb"), writes=(dst_key(c),), tag="lnn3")


def load_resid(cx, src_dram):
    p = cx.p
    for c in range(16):
        p.dma("pool", cx.A[:, c * 2048:(c + 1) * 2048],
              src_dram[c * 128:(c + 1) * 128, :],
              writes=tuple(("A", c, t) for t in range(4)), tag="residload")


def store_out_from_A(cx):
    p = cx.p
    for c in range(16):
        p.dma("pool", cx.d_out[c * 128:(c + 1) * 128, :],
              cx.A[:, c * 2048:(c + 1) * 2048],
              reads=tuple(("A", c, t) for t in range(4)), tag="residstore")


def gate_math(cx, xn, xn_key, wsb, wsb_key, mode, out_key, extra_reads=()):
    p = cx.p
    ps_l = cx.pb[3]
    m1, m2, d21, z, den, rden, negm1, zr = (cx.sml[:, j:j + 1] for j in range(8))
    K = lambda j: f"sml{j}"
    g8 = [cx.sc_f[:, 512 * j: 512 * j + 8] for j in range(4)]
    ge1, msk, expl, ctok = g8
    for tc in range(16):
        for d in range(16):
            p.emit("pe", lambda e, tc=tc, d=d: e.matmul(
                ps_l[0:128, 0:8],
                xn[:, d * 2048 + tc * 128: d * 2048 + (tc + 1) * 128],
                wsb[:, d * 8:(d + 1) * 8], start=(d == 0), stop=(d == 15)),
                reads=(xn_key(d, tc // 4), wsb_key) + tuple(extra_reads), writes=("pb3",), tag="glog")
        p.emit("dve", lambda e: e.reduce_max(m1, ps_l[0:128, 0:8], mybir.AxisListType.X),
               reads=("pb3",), writes=(K(0),), tag="gmath")
        p.emit("dve", lambda e: e.tensor_scalar(ge1, ps_l[0:128, 0:8], m1, None, ALU.is_ge),
               reads=("pb3", K(0)), writes=("scf0",), tag="gmath")
        p.emit("dve", lambda e: e.scalar_tensor_tensor(msk, ge1, -1e30, ps_l[0:128, 0:8], ALU.mult, ALU.add),
               reads=("pb3", "scf0"), writes=("scf1",), tag="gmath")
        p.emit("dve", lambda e: e.reduce_max(m2, msk, mybir.AxisListType.X),
               reads=("scf1",), writes=(K(1),), tag="gmath")
        p.emit("dve", lambda e: e.tensor_sub(d21, m2, m1),
               reads=(K(0), K(1)), writes=(K(2),), tag="gmath")
        p.emit("act", lambda e: e.activation(z, d21, AF.Exp),
               reads=(K(2),), writes=(K(3),), tag="gmath")
        p.emit("dve", lambda e: e.tensor_scalar_add(den, z, 1.0),
               reads=(K(3),), writes=(K(4),), tag="gmath")
        p.emit("dve", lambda e: e.reciprocal(rden, den),
               reads=(K(4),), writes=(K(5),), tag="gmath")
        if mode == "c8":
            p.emit("dve", lambda e: e.tensor_scalar_mul(negm1, m1, -1.0),
                   reads=(K(0),), writes=(K(6),), tag="gmath")
            p.emit("act", lambda e: e.activation(expl, ps_l[0:128, 0:8], AF.Exp, bias=negm1),
                   reads=("pb3", K(6)), writes=("scf2",), tag="gmath")
            p.emit("dve", lambda e: e.tensor_scalar(ctok, ps_l[0:128, 0:8], m2, None, ALU.is_ge),
                   reads=("pb3", K(1)), writes=("scf3",), tag="gmath")
            p.emit("dve", lambda e: e.tensor_mul(ctok, ctok, expl),
                   reads=("scf3", "scf2"), writes=("scf3",), tag="gmath")
            p.emit("dve", lambda e: e.tensor_scalar_mul(ctok, ctok, rden),
                   reads=("scf3", K(5)), writes=("scf3",), tag="gmath")
            p.emit("pe", lambda e: e.transpose(cx.pb[2][0:8, 0:128], ctok, cx.identf[:, :]),
                   reads=("scf3", "identf"), writes=("pb2",), tag="gtp")
            p.emit("dve", lambda e, tc=tc: e.tensor_copy(cx.crows[0:8, tc * 128:(tc + 1) * 128], cx.pb[2][0:8, 0:128]),
                   reads=("pb2",), writes=(out_key(tc),), tag="gtp")
        else:
            p.emit("dve", lambda e: e.tensor_mul(zr, z, rden),
                   reads=(K(3), K(5)), writes=(K(7),), tag="gmath")
            p.emit("dve", lambda e: e.tensor_add(zr, rden, zr),
                   reads=(K(5), K(7)), writes=(K(7),), tag="gmath")
            p.emit("pe", lambda e: e.transpose(cx.pb[2][0:1, 0:128], zr, cx.identf[:, :]),
                   reads=(K(7), "identf"), writes=("pb2",), tag="gtp")
            p.emit("dve", lambda e, tc=tc: e.tensor_copy(cx.srw[0:1, tc * 128:(tc + 1) * 128], cx.pb[2][0:1, 0:128]),
                   reads=("pb2",), writes=(out_key(tc),), tag="gtp")


def bld_l1f(cx, fences=()):
    p, nc = cx.p, cx.nc
    A = cx.A
    xn = cx.R1
    y1 = cx.R2
    fr = tuple(fences)

    cx.wg_sb = nc.alloc_sbuf_tensor("wg_sb", [128, 128], BF16)
    p.dma("pool", cx.wg_sb[:, :], cx.d_wgT[:, :], writes=("wgT",), tag="wgT")

    akey = lambda c, t: ("A", c, t)
    xnkey = lambda c, t: ("xn3", c, t)

    if cx.dbg not in ("w0nl",):
        for tt in range(4):
            ln_normalize(cx, A, lambda c: akey(c, tt), xn, lambda c: xnkey(c, tt), tt, extra_reads=fr)

    if cx.dbg == "xn":
        for c in range(16):
            p.dma("pool", cx.d_out[c * 128:(c + 1) * 128, :], xn[:, c * 2048:(c + 1) * 2048],
                  reads=tuple(xnkey(c, t) for t in range(4)), tag="dbg")
        return
    if cx.dbg not in ("xng", "w0ng", "w0nl", "y1ng", "psdump", "y1nc", "mx"):
        gate_math(cx, xn, xnkey, cx.wg_sb, "wgT", "rw", lambda tc: ("rw", tc))
    if cx.dbg == "xng":
        for c in range(16):
            p.dma("pool", cx.d_out[c * 128:(c + 1) * 128, :], xn[:, c * 2048:(c + 1) * 2048],
                  reads=tuple(xnkey(c, t) for t in range(4)), tag="dbg")
        return
    if cx.dbg == "rw":
        p.dma("pool", cx.d_out[0:1, 0:2048], cx.srw[0:1, :],
              reads=tuple(("rw", tc) for tc in range(16)), tag="dbg")
        return

    wub = [cx.WS[:, 0:2048], cx.WS[:, 0:2048]]
    wdb = [cx.WS[:, 2048 + b * 2048: 2048 + (b + 1) * 2048] for b in range(2)]
    web = [cx.R2[:, 18432 + b * 2048: 18432 + (b + 1) * 2048] for b in range(2)]

    def wdma(dram, buf, key, idx):
        p.dma("pool", buf[:, :].rearrange("p (c n) -> p c n", n=128),
              dram[idx, :, :].rearrange("(c p) n -> p c n", p=128),
              writes=(key,), tag="wstream")

    SCF = lambda j: cx.sc_f[:, j * 512:(j + 1) * 512]

    if cx.dbg == "w2":
        wdma(cx.d_wupT, wub[0], ("wup", 0), 0)
        wdma(cx.d_wupT, wub[1], ("wup", 1), 1)
        for b2 in range(2):
            p.dma("pool", cx.d_out[b2 * 128:(b2 + 1) * 128, 0:2048], wub[b2][:, :],
                  reads=(("wup", b2),), tag="dbg")
        return

    NOCU = {"y1one": 1, "y1four": 4, "y1ng": 1, "psdump": 1, "y1nc": 1, "mx": 1}.get(cx.dbg, 16)
    for half in range(2):
        tts = (2 * half, 2 * half + 1)
        for ocu in range(NOCU):
            bu = 0
            wdma(cx.d_wupT, wub[bu], ("wup", bu), ocu)
            for d in range(16):
                for j, tt in enumerate(tts):
                    p.emit("pe", lambda e, bu=bu, d=d, tt=tt, j=j: e.matmul(
                        cx.pb[4 + j][:, :], wub[bu][:, d * 128:(d + 1) * 128],
                        xn[:, d * 2048 + tt * 512: d * 2048 + tt * 512 + 512],
                        start=(d == 0), stop=(d == 15)),
                        reads=(("wup", bu), xnkey(d, tt)), writes=(f"pb{4 + j}",), tag="up")
            if cx.dbg == "psdump" and half == 0 and ocu == 0:
                p.emit("dve", lambda e: e.tensor_copy(cx.sc_f[:, 0:512], cx.pb[4][:, :]),
                       reads=("pb4",), writes=("scf0",), tag="dbg")
                p.dma("sp", cx.d_out[0:128, 0:512], cx.sc_f[:, 0:512], reads=("scf0",), tag="dbg")
                p.emit("dve", lambda e: e.tensor_copy(cx.sc_f[:, 512:1024], cx.pb[5][:, :]),
                       reads=("pb5",), writes=("scf1",), tag="dbg")
                p.dma("sp", cx.d_out[0:128, 512:1024], cx.sc_f[:, 512:1024], reads=("scf1",), tag="dbg")
                p.dma("pool", cx.d_out[128:256, 0:2048], wub[0][:, :], reads=(("wup", 0),), tag="dbg")
                p.dma("pool", cx.d_out[256:384, 0:2048], xn[:, 0:2048], reads=tuple(xnkey(0, t) for t in range(4)), tag="dbg")
                return
            for j, tt in enumerate(tts):
                if RELU_ON_DVE:
                    p.emit("dve", lambda e, ocu=ocu, j=j: e.tensor_relu(
                        y1[:, ocu * 1024 + j * 512: ocu * 1024 + j * 512 + 512],
                        cx.pb[4 + j][:, :]),
                        reads=(f"pb{4 + j}",) + fr, writes=(("y1", ocu, j),), tag="relu")
                else:
                    p.emit("act", lambda e, ocu=ocu, j=j: e.activation(
                        y1[:, ocu * 1024 + j * 512: ocu * 1024 + j * 512 + 512],
                        cx.pb[4 + j][:, :], AF.Relu),
                        reads=(f"pb{4 + j}",) + fr, writes=(("y1", ocu, j),), tag="relu")
        if cx.dbg == "m_y1" and half == 0:
            for c in range(16):
                p.emit("dve", lambda e, c=c: e.tensor_copy(cx.sc_f[:, 0:512], y1[:, c * 1024:c * 1024 + 512]),
                       reads=(("y1", c, 0),), writes=("scf0",), tag="dbg")
                p.dma("sp", cx.d_out[c * 128:(c + 1) * 128, 0:512], cx.sc_f[:, 0:512], reads=("scf0",), tag="dbg")
                p.emit("dve", lambda e, c=c: e.tensor_copy(cx.sc_f[:, 512:1024], y1[:, c * 1024 + 512:c * 1024 + 1024]),
                       reads=(("y1", c, 1),), writes=("scf1",), tag="dbg")
                p.dma("sp", cx.d_out[c * 128:(c + 1) * 128, 512:1024], cx.sc_f[:, 512:1024], reads=("scf1",), tag="dbg")
            return
        for j, tt in enumerate(tts):
            rb_slot = 1 if j == 0 else 3
            p.emit("pe", lambda e, tt=tt: e.matmul(
                cx.pb[6][:, :], cx.onesrb[:, :], cx.srw[0:1, tt * 512:(tt + 1) * 512],
                start=True, stop=True),
                reads=tuple(("rw", 4 * tt + i2) for i2 in range(4)) + ("onesrb",),
                writes=("pb6",), tag="rwbc")
            p.emit("act", lambda e, rb_slot=rb_slot: e.copy(SCF(rb_slot), cx.pb[6][:, :]),
                   reads=("pb6",), writes=(f"scf{rb_slot}",), tag="rwbc")
            if cx.dbg == "m_rb" and half == 0 and j == 0:
                p.dma("sp", cx.d_out[0:128, 0:512], SCF(rb_slot), reads=(f"scf{rb_slot}",), tag="dbg")
                p.emit("dve", lambda e: e.tensor_copy(cx.sc_f[:, 1024:1536], cx.pb[6][:, :]),
                       reads=("pb6",), writes=("scf2",), tag="dbg")
                p.dma("sp", cx.d_out[128:256, 0:512], cx.sc_f[:, 1024:1536], reads=("scf2",), tag="dbg")
                p.emit("dve", lambda e: e.tensor_copy(cx.sc_f[0:1, 0:512], cx.srw[0:1, 0:512]),
                       reads=tuple(("rw", i2) for i2 in range(4)), writes=("scf0",), tag="dbg")
                p.dma("sp", cx.d_out[256:257, 0:512], cx.sc_f[0:1, 0:512], reads=("scf0",), tag="dbg")
                return
        for oc in range(16):
            bd = oc % 2
            wdma(cx.d_wdownT, wdb[bd], ("wdown", bd), oc)
            wdma(cx.d_wexp, web[bd], ("wexp", bd), oc)
            for j, tt in enumerate(tts):
                rb_slot = 1 if j == 0 else 3
                for od in range(16):
                    p.emit("pe", lambda e, bd=bd, od=od, j=j: e.matmul(
                        cx.pb[j][:, :], wdb[bd][:, od * 128:(od + 1) * 128],
                        y1[:, od * 1024 + j * 512: od * 1024 + j * 512 + 512],
                        start=(od == 0), stop=(od == 15)),
                        reads=(("wdown", bd), ("y1", od, j)), writes=(f"pb{j}",), tag="down")
                for d in range(16):
                    p.emit("pe", lambda e, bd=bd, d=d, tt=tt, j=j: e.matmul(
                        cx.pb[2 + j][:, :], web[bd][:, d * 128:(d + 1) * 128],
                        xn[:, d * 2048 + tt * 512: d * 2048 + tt * 512 + 512],
                        start=(d == 0), stop=(d == 15)),
                        reads=(("wexp", bd), xnkey(d, tt)), writes=(f"pb{2 + j}",), tag="wex")
                p.emit("dve", lambda e, rb_slot=rb_slot, j=j: e.tensor_mul(SCF(2), cx.pb[2 + j][:, :], SCF(rb_slot)),
                       reads=(f"pb{2 + j}", f"scf{rb_slot}"), writes=("scf2",), tag="comb")
                p.emit("dve", lambda e, j=j: e.tensor_add(SCF(2), SCF(2), cx.pb[j][:, :]),
                       reads=("scf2", f"pb{j}"), writes=("scf2",), tag="comb")
                p.emit("dve", lambda e, oc=oc, tt=tt: e.tensor_add(
                    SCF(0), SCF(2), A[:, oc * 2048 + tt * 512: oc * 2048 + tt * 512 + 512]),
                    reads=("scf2", akey(oc, tt)), writes=("scf0",), tag="comb")
                p.dma("sp", cx.d_out[oc * 128:(oc + 1) * 128, tt * 512:(tt + 1) * 512], SCF(0),
                      reads=("scf0",), tag="outstore")
                if cx.dbg == "m_comb" and half == 0 and oc == 0 and j == 0:
                    p.emit("dve", lambda e, j=j: e.tensor_copy(cx.sc_f[:, 512:1024], cx.pb[j][:, :]),
                           reads=(f"pb{j}",), writes=("scf1",), tag="dbg")
                    p.dma("sp", cx.d_out[128:256, 0:512], cx.sc_f[:, 512:1024], reads=("scf1",), tag="dbg")
                    p.emit("dve", lambda e, j=j: e.tensor_copy(cx.sc_f[:, 1536:2048], cx.pb[2 + j][:, :]),
                           reads=(f"pb{2 + j}",), writes=("scf3",), tag="dbg")
                    p.dma("sp", cx.d_out[256:384, 0:512], cx.sc_f[:, 1536:2048], reads=("scf3",), tag="dbg")
                    p.dma("sp", cx.d_out[384:512, 0:512], SCF(2), reads=("scf2",), tag="dbg")
                    p.dma("sp", cx.d_out[512:640, 0:512], SCF(1), reads=("scf1",), tag="dbg")
                    p.dma("sp", cx.d_out[640:768, 0:512], SCF(0), reads=("scf0",), tag="dbg")
                    p.dma("pool", cx.d_out[768:896, 0:512], A[:, 0:512], reads=(akey(0, 0),), tag="dbg")
                    return


def build_launch(blocks, dbg=None):
    nc = bass.Bass("TRN2", target_bir_lowering=False, dynamic_dma_scratch_size=16384)
    cx = make_ctx(nc, blocks)
    cx.dbg = dbg
    if "l0f" in blocks:
        cx.crows = cx.sc_b[0:8, 1024:3072]
    if cx.unfused_in is not None:
        load_resid(cx, cx.unfused_in)
    for bi, b in enumerate(blocks):
        if bi > 0:
            cx.p.barrier()
        if b == "l0a":
            bld_l0a(cx)
        elif b == "l0f":
            bld_l0f(cx)
        elif b == "l1a":
            bld_l1a(cx)
        elif b == "l1f":
            bld_l1f(cx)
            if cx.dbg:
                return_after = True
    if blocks[-1] != "l1f" and (not dbg or dbg == "f_e0"):
        store_out_from_A(cx)
    cx.p.finalize()
    return nc, cx


def own_cols(h):
    perm = _own_perm(h)[:4]
    return (perm[:, None] * 512 + np.arange(512)[None, :]).reshape(-1)


def run_blocks_unfused(inp, blocks_groups, x_stage=None):
    sh = stage_shared(inp)
    cores = [stage_core(inp, c) for c in range(8)]
    cur = None
    for gi, blocks in enumerate(blocks_groups):
        nc, cx = build_launch(blocks)
        in_maps = []
        for c in range(8):
            im = {}
            for attr, key in [("d_xt", "xt"), ("d_wq", "wq_r"), ("d_wkvk", "wkvk_r"),
                              ("d_wkvv", "wkvv_r"), ("d_wd", "wd_r"), ("d_hflag", "hflag"),
                              ("d_wr", "wr"), ("d_wg", "wg_r"), ("d_wu", "wu_r"), ("d_wdd", "wdd_r"),
                              ("d_wts", "wts_r"), ("d_wkT", "wkT"), ("d_wvT", "wvT"), ("d_woT", "woT"),
                              ("d_gate", "gate2d"), ("d_lnw", "lnw2d"), ("d_lnb", "lnb2d"),
                              ("d_td", "td"), ("d_tf", "tf"), ("d_hoff", "hoff"),
                              ("d_wupT", "wupT_r"), ("d_wdownT", "wdownT_r"), ("d_wexp", "wexp_r"),
                              ("d_wgT", "wgT")]:
                if hasattr(cx, attr):
                    im[key] = sh[key] if key in sh else cores[c][key]
            if cx.unfused_in is not None:
                im["xin"] = x_stage[c] if x_stage is not None else cur[c]
            in_maps.append(im)
        res = run_bass_kernel_spmd(nc, in_maps, core_ids=list(range(8)))
        cur = [np.ascontiguousarray(res.results[c]["xout"]) for c in range(8)]
        x_stage = None
    out = np.zeros((4, 4096, 2048), np.float32)
    for c in range(8):
        b, h = c // 2, c % 2
        cols = own_cols(h)
        out[b][cols, :] = cur[c].T
    return out



def bld_l0a(cx, fences=()):
    p, nc = cx.p, cx.nc
    A = cx.A
    fr = tuple(fences)
    akey = lambda c, t: ("A", c, t)

    kT = cx.R1[:, 0:16384]
    vt = cx.R1[:, 16384:32768]
    xtile = cx.R2[:, 0:8192]
    ptb = [cx.R2[:, 8192 + b * 512: 8192 + (b + 1) * 512] for b in range(2)]
    ctxT = cx.R2[:, 9216:17408]
    wsA = cx.WS[:, 0:2048]
    wsB = cx.WS[:, 2048:4096]
    wvv = [cx.WS[:, 4096 + b * 512: 4096 + (b + 1) * 512] for b in range(2)]
    hf = cx.sml[:, 74:75]
    p.dma("sp", hf, cx.d_hflag[:, :], writes=("hflag",), tag="l0aw")

    for pt in range(8):
        xk = lambda c, pt=pt: ("xt", c)
        p.dma("pool", xtile[:, :].rearrange("p (c n) -> p c n", n=512),
              cx.d_xt[:, pt * 512:(pt + 1) * 512].rearrange("(c p) n -> p c n", p=128),
              reads=tuple(("xt", c) for c in range(16)), writes=tuple(("xt", c) for c in range(16)),
              tag="l0ax")
        ln_normalize(cx, xtile, xk, xtile, xk, 0, colstride=512, extra_reads=fr)
        for kfc in range(4):
            wb, wk = (wsA, "wsA") if kfc % 2 == 0 else (wsB, "wsB")
            p.dma("pool", wb[:, :].rearrange("p (c n) -> p c n", n=128),
                  cx.d_wkvk[kfc, :, :].rearrange("(c p) n -> p c n", p=128),
                  writes=(wk,), tag="l0aw")
            for d in range(16):
                p.emit("pe", lambda en, wb=wb, d=d: en.matmul(
                    cx.pb[0][:, :], wb[:, d * 128:(d + 1) * 128],
                    xtile[:, d * 512:(d + 1) * 512], start=(d == 0), stop=(d == 15)),
                    reads=(wk, ("xt", d)), writes=("pb0",), tag="l0akv")
            p.emit("dve", lambda en, kfc=kfc, pt=pt: en.tensor_copy(
                kT[:, kfc * 4096 + pt * 512: kfc * 4096 + (pt + 1) * 512], cx.pb[0][:, :]),
                reads=("pb0",), writes=(("kT", kfc, pt),), tag="l0akv")
        for d in range(16):
            b = d % 2
            p.dma("pool", wvv[b][:, :], cx.d_wkvv[d, :, :],
                  writes=(("wvv", b),), tag="l0aw")
            for ps4 in range(4):
                p.emit("pe", lambda en, d=d, ps4=ps4, b=b: en.matmul(
                    cx.pb[1 + ps4][:, :], xtile[:, d * 512 + ps4 * 128: d * 512 + (ps4 + 1) * 128],
                    wvv[b][:, :], start=(d == 0), stop=(d == 15)),
                    reads=(("wvv", b), ("xt", d)), writes=(f"pb{1 + ps4}",), tag="l0akv")
        for ps4 in range(4):
            p.emit("dve", lambda en, pt=pt, ps4=ps4: en.tensor_copy(
                vt[:, (pt * 4 + ps4) * 512: (pt * 4 + ps4 + 1) * 512], cx.pb[1 + ps4][:, :]),
                reads=(f"pb{1 + ps4}",), writes=(("vt", pt * 4 + ps4),), tag="l0akv")
        if pt < 4:
            for ec in range(16):
                wb, wk = (wsA, "wsA") if ec % 2 == 0 else (wsB, "wsB")
                p.dma("pool", wb[:, :].rearrange("p (c n) -> p c n", n=128),
                      cx.d_wq[ec, :, :].rearrange("(c p) n -> p c n", p=128),
                      writes=(wk,), tag="l0aw")
                for d in range(16):
                    p.emit("pe", lambda en, wb=wb, d=d, qb=5 + ec % 2: en.matmul(
                        cx.pb[qb][:, :], wb[:, d * 128:(d + 1) * 128],
                        xtile[:, d * 512:(d + 1) * 512], start=(d == 0), stop=(d == 15)),
                        reads=(wk, ("xt", d)), writes=(f"pb{5 + ec % 2}",), tag="l0aq")
                p.emit("dve", lambda en, ec=ec, pt=pt: en.tensor_copy(
                    A[:, ec * 2048 + pt * 512: ec * 2048 + (pt + 1) * 512], cx.pb[5 + ec % 2][:, :]),
                    reads=(f"pb{5 + ec % 2}",), writes=(akey(ec, pt),), tag="l0aq")

    if cx.dbg == "a_s1":
        for c in range(8):
            p.emit("dve", lambda en, c=c: en.tensor_copy(cx.sc_f[:, 0:512], A[:, c * 2048: c * 2048 + 512]),
                   reads=(akey(c, 0),), writes=("scf0",), tag="dbg")
            p.dma("sp", cx.d_out[c * 128:(c + 1) * 128, 0:512], cx.sc_f[:, 0:512], reads=("scf0",), tag="dbg")
        for kfc in range(4):
            p.emit("dve", lambda en, kfc=kfc: en.tensor_copy(cx.sc_f[:, 0:512], kT[:, kfc * 4096: kfc * 4096 + 512]),
                   reads=(("kT", kfc, 0),), writes=("scf0",), tag="dbg")
            p.dma("sp", cx.d_out[(8 + kfc) * 128:(9 + kfc) * 128, 0:512], cx.sc_f[:, 0:512], reads=("scf0",), tag="dbg")
        for p4 in range(4):
            p.emit("dve", lambda en, p4=p4: en.tensor_copy(cx.sc_f[:, 0:512], vt[:, p4 * 512:(p4 + 1) * 512]),
                   reads=(("vt", p4),), writes=("scf0",), tag="dbg")
            p.dma("sp", cx.d_out[(12 + p4) * 128:(13 + p4) * 128, 0:512], cx.sc_f[:, 0:512], reads=("scf0",), tag="dbg")
        return
    for qt in range(4):
        nkt = 8 * (qt + 1)
        for head in range(16):
            g = head // 4
            cB = f"pb{2 + head % 2}"
            for kt in range(nkt):
                if kt < 4 * (qt + 1):
                    kcol = kt * 128
                else:
                    kcol = 2048 + (kt - 4 * (qt + 1)) * 128
                sB = f"pb{0 + kt % 2}"
                pb_s = cx.pb[0 + kt % 2]
                ptbuf = ptb[kt % 2]
                ptk = ("ptb", kt % 2)
                p.emit("pe", lambda en, g=g, kcol=kcol, head=head, qt=qt, pb_s=pb_s: en.matmul(
                    pb_s[:, :], kT[:, g * 4096 + kcol: g * 4096 + kcol + 128],
                    A[:, head * 2048 + qt * 512: head * 2048 + (qt + 1) * 512],
                    start=True, stop=True),
                    reads=(("kT", g, kcol // 512), akey(head, qt)), writes=(sB,), tag="l0as")
                p.emit("act", lambda en, pb_s=pb_s, ptbuf=ptbuf: en.activation(
                    ptbuf[:, :], pb_s[:, :], AF.Exp, bias=cx.cst[:, 1:2], scale=SCALE),
                    reads=(sB, "cst"), writes=(ptk,), tag="l0ae")
                if 4 * qt <= kt < 4 * (qt + 1):
                    j = kt - 4 * qt
                    p.emit("dve", lambda en, j=j, ptbuf=ptbuf: en.tensor_mul(
                        ptbuf[:, :], ptbuf[:, :], cx.dmask[:, j * 512:(j + 1) * 512]),
                        reads=(ptk, "dmask"), writes=(ptk,), tag="l0am")
                if kt >= nkt - 4:
                    p.emit("dve", lambda en, ptbuf=ptbuf: en.tensor_scalar_mul(
                        ptbuf[:, :], ptbuf[:, :], hf),
                        reads=(ptk, "hflag"), writes=(ptk,), tag="l0am")
                p.emit("pe", lambda en, kcol=kcol, g=g, kt=kt, nkt=nkt, ptbuf=ptbuf, cBi=2 + head % 2: en.matmul(
                    cx.pb[cBi][:, :], vt[:, (kcol // 128) * 512 + g * 128: (kcol // 128) * 512 + (g + 1) * 128],
                    ptbuf[:, :], start=(kt == 0), stop=(kt == nkt - 1)),
                    reads=(("vt", kcol // 128), ptk), writes=(cB,), tag="l0ac")
                p.emit("pe", lambda en, kt=kt, nkt=nkt, ptbuf=ptbuf: en.matmul(
                    cx.pb[4][0:1, :], cx.onesc[:, :], ptbuf[:, :],
                    start=(kt == 0), stop=(kt == nkt - 1)),
                    reads=("onesc", ptk), writes=("pb4",), tag="l0ar")
            rsum = cx.rows[0:1, 0:512]
            p.emit("dve", lambda en, rsum=rsum: en.reciprocal(rsum, cx.pb[4][0:1, :]),
                   reads=("pb4",), writes=("r_m",), tag="l0an")
            p.emit("pe", lambda en, rsum=rsum: en.matmul(
                cx.pb[7][:, :], cx.onesr[:, :], rsum, start=True, stop=True),
                reads=("r_m", "onesr"), writes=("pb7",), tag="l0an")
            p.emit("act", lambda en: en.copy(cx.sc_f[:, 0:512], cx.pb[7][:, :]),
                   reads=("pb7",), writes=("scf0",), tag="l0an")
            p.emit("dve", lambda en, head=head, cBi=2 + head % 2: en.tensor_mul(
                ctxT[:, head * 512:(head + 1) * 512], cx.pb[cBi][:, :], cx.sc_f[:, 0:512]),
                reads=(cB, "scf0"), writes=(("ctx", head),), tag="l0an")
        for oc in range(16):
            wb, wk = (wsA, "wsA") if oc % 2 == 0 else (wsB, "wsB")
            p.dma("pool", wb[:, :].rearrange("p (c n) -> p c n", n=128),
                  cx.d_wd[oc, :, :].rearrange("(c p) n -> p c n", p=128),
                  writes=(wk,), tag="l0aw")
            x0sb = cx.sc_f[:, 512:1024]
            p.dma("sp", x0sb, cx.d_xt[oc * 128:(oc + 1) * 128, qt * 512:(qt + 1) * 512],
                  reads=("scf1",), writes=("scf1",), tag="l0ax0")
            dB = f"pb{5 + oc % 2}"
            for ec in range(16):
                p.emit("pe", lambda en, wb=wb, ec=ec, dBi=5 + oc % 2: en.matmul(
                    cx.pb[dBi][:, :], wb[:, ec * 128:(ec + 1) * 128],
                    ctxT[:, ec * 512:(ec + 1) * 512], start=(ec == 0), stop=(ec == 15)),
                    reads=(wk, ("ctx", ec)), writes=(dB,), tag="l0ad")
            p.emit("dve", lambda en, oc=oc, qt=qt, dBi=5 + oc % 2: en.tensor_add(
                A[:, oc * 2048 + qt * 512: oc * 2048 + (qt + 1) * 512],
                cx.pb[dBi][:, :], x0sb),
                reads=(dB, "scf1", akey(oc, qt)), writes=(akey(oc, qt),), tag="l0adr")


def bld_l0f(cx, fences=()):
    p, nc = cx.p, cx.nc
    A, xn, hT = cx.A, cx.R1, cx.R2
    fr = tuple(fences)
    akey = lambda c, t: ("A", c, t)
    xnkey = lambda c, t: ("l0fxn", c, t)

    wgb = cx.WS[:, 0:2048]
    wub_ = cx.WS[:, 2048:4096]
    wdb0 = cx.WS[:, 4096:5504]
    wdb1 = cx.WS[:, 2048:3456]
    wr_sb = cx.WS[:, 5504:5632]
    oh_sb = cx.WS[0:8, 5632:6144]
    p.dma("pool", wr_sb, cx.d_wr[:, :], writes=("wrsb",), tag="l0fw")
    p.dma("sp", oh_sb, cx.onehot64[:, :], writes=("oh64",), tag="l0fw")

    for tt in range(4):
        ln_normalize(cx, A, lambda c: akey(c, tt), xn, lambda c: xnkey(c, tt), tt, extra_reads=fr)

    gate_math(cx, xn, xnkey, wr_sb, "wrsb", "c8", lambda tc: ("crows", tc))
    crk = tuple(("crows", tc) for tc in range(16))

    if cx.dbg == "f_c":
        for q in range(4):
            p.emit("dve", lambda en, q=q: en.tensor_copy(cx.sc_f[0:8, 0:512], cx.crows[0:8, q * 512:(q + 1) * 512]),
                   reads=crk, writes=("scf0",), tag="dbg")
            p.dma("sp", cx.d_out[0:8, q * 512:(q + 1) * 512], cx.sc_f[0:8, 0:512], reads=("scf0",), tag="dbg")
        return

    sscr = cx.sc_b[:, 0:512]

    for e in range(1 if cx.dbg == 'f_e0' else 8):
        for i in range(NIC):
            p.dma("pool", wgb[:, :].rearrange("p (c n) -> p c n", n=128),
                  cx.d_wg[e, i, :, :].rearrange("(c p) n -> p c n", p=128),
                  writes=("wsA",), tag="l0fw")
            for d in range(16):
                for tt in range(4):
                    p.emit("pe", lambda en, d=d, tt=tt: en.matmul(
                        cx.pb[tt][:, :], wgb[:, d * 128:(d + 1) * 128],
                        xn[:, d * 2048 + tt * 512: d * 2048 + tt * 512 + 512],
                        start=(d == 0), stop=(d == 15)),
                        reads=("wsA", xnkey(d, tt)), writes=(f"pb{tt}",), tag="l0fg")
            p.dma("pool", wub_[:, :].rearrange("p (c n) -> p c n", n=128),
                  cx.d_wu[e, i, :, :].rearrange("(c p) n -> p c n", p=128),
                  writes=("wsB",), tag="l0fw")
            for d in range(16):
                for tt in range(4):
                    p.emit("pe", lambda en, d=d, tt=tt: en.matmul(
                        cx.pb[4 + tt][:, :], wub_[:, d * 128:(d + 1) * 128],
                        xn[:, d * 2048 + tt * 512: d * 2048 + tt * 512 + 512],
                        start=(d == 0), stop=(d == 15)),
                        reads=("wsB", xnkey(d, tt)), writes=(f"pb{4 + tt}",), tag="l0fu")
            for tt in range(4):
                p.emit("act", lambda en, tt=tt: en.activation(sscr, cx.pb[tt][:, :], AF.Silu),
                       reads=(f"pb{tt}",), writes=("lnsq0",), tag="l0fh")
                p.emit("dve", lambda en, i=i, tt=tt: en.tensor_mul(
                    hT[:, i * 2048 + tt * 512: i * 2048 + tt * 512 + 512], sscr, cx.pb[4 + tt][:, :]),
                    reads=("lnsq0", f"pb{4 + tt}") + fr, writes=(("hT", i, tt),), tag="l0fh")
        if cx.dbg == "f_h" and e == 0:
            for i2 in (0, 5):
                for q in range(4):
                    p.emit("dve", lambda en, i2=i2, q=q: en.tensor_copy(
                        cx.sc_f[:, 0:512], hT[:, i2 * 2048 + q * 512: i2 * 2048 + q * 512 + 512]),
                        reads=(("hT", i2, q),), writes=("scf0",), tag="dbg")
                    p.dma("sp", cx.d_out[(0 if i2 == 0 else 128):(128 if i2 == 0 else 256), q * 512:(q + 1) * 512],
                          cx.sc_f[:, 0:512], reads=("scf0",), tag="dbg")
            return
        for tt in range(4):
            p.emit("pe", lambda en, e=e, tt=tt: en.matmul(
                cx.pb[6][0:64, :], oh_sb[0:8, e * 64:(e + 1) * 64],
                cx.crows[0:8, tt * 512:(tt + 1) * 512], start=True, stop=True),
                reads=("oh64",) + crk, writes=("pb6",), tag="l0fcb")
            p.emit("act", lambda en: en.copy(sscr[0:64, :], cx.pb[6][0:64, :]),
                   reads=("pb6",), writes=("lnsq0",), tag="l0fcb")
            p.emit("act", lambda en: en.copy(sscr[64:128, :], cx.pb[6][0:64, :]),
                   reads=("pb6",), writes=("lnsq0",), tag="l0fcb")
            if cx.dbg == "f_cb" and e == 0:
                p.emit("dve", lambda en: en.tensor_copy(cx.sc_f[:, 0:512], sscr),
                       reads=("lnsq0",), writes=("scf0",), tag="dbg")
                p.dma("sp", cx.d_out[0:128, tt * 512:(tt + 1) * 512], cx.sc_f[:, 0:512],
                      reads=("scf0",), tag="dbg")
            for i in range(NIC):
                p.emit("dve", lambda en, i=i, tt=tt: en.tensor_mul(
                    hT[:, i * 2048 + tt * 512: i * 2048 + tt * 512 + 512],
                    hT[:, i * 2048 + tt * 512: i * 2048 + tt * 512 + 512], sscr),
                    reads=(("hT", i, tt), "lnsq0"), writes=(("hT", i, tt),), tag="l0fcs")
        if cx.dbg == "f_h2" and e == 0:
            for i2 in (0, 5):
                for q in range(4):
                    p.emit("dve", lambda en, i2=i2, q=q: en.tensor_copy(
                        cx.sc_f[:, 0:512], hT[:, i2 * 2048 + q * 512: i2 * 2048 + q * 512 + 512]),
                        reads=(("hT", i2, q),), writes=("scf0",), tag="dbg")
                    p.dma("sp", cx.d_out[(0 if i2 == 0 else 128):(128 if i2 == 0 else 256), q * 512:(q + 1) * 512],
                          cx.sc_f[:, 0:512], reads=("scf0",), tag="dbg")
            return
        if cx.dbg == "f_cb" and e == 0:
            return
        for oc in range(16):
            wdb = wdb0 if oc % 2 == 0 else wdb1
            wdk = "wsC" if oc % 2 == 0 else "wsB"
            p.dma("pool", wdb[:, :].rearrange("p (c n) -> p c n", n=128),
                  cx.d_wdd[e, oc, :, :].rearrange("(c p) n -> p c n", p=128),
                  writes=(wdk,), tag="l0fw")
            for i in range(NIC):
                for tt in range(4):
                    p.emit("pe", lambda en, i=i, tt=tt, wdb=wdb: en.matmul(
                        cx.pb[tt][:, :], wdb[:, i * 128:(i + 1) * 128],
                        hT[:, i * 2048 + tt * 512: i * 2048 + tt * 512 + 512],
                        start=(i == 0), stop=(i == NIC - 1)),
                        reads=(wdk, ("hT", i, tt)), writes=(f"pb{tt}",), tag="l0fy")
            for tt in range(4):
                p.emit("dve", lambda en, oc=oc, tt=tt: en.tensor_add(
                    A[:, oc * 2048 + tt * 512: oc * 2048 + tt * 512 + 512],
                    A[:, oc * 2048 + tt * 512: oc * 2048 + tt * 512 + 512], cx.pb[tt][:, :]),
                    reads=(f"pb{tt}", akey(oc, tt)), writes=(akey(oc, tt),), tag="l0fres")


def bld_l1a(cx, fences=()):
    p, nc = cx.p, cx.nc
    A = cx.A
    xs = cx.R1
    fr = tuple(fences)
    akey = lambda c, t: ("A", c, t)
    xskey = lambda c, t: ("l1xs", c, t)

    l1kv = cx.R2[0:64, 0:4096]
    stateb = cx.R2[0:64, 4096:8192]
    wo_sb = cx.R2[0:64, 8192:10240]
    r2addr = nc.lookup_mloc(cx.R2).addr
    l1buf = nc.alloc_sbuf_tensor_at("l1buf", [128, 4096], F32, offset=r2addr + 20480)
    scr64 = cx.sc_b[0:64, 1024:3072]

    gcol = cx.sml[:, 8:24]
    omg = cx.sml[:, 24:40]
    lnw_sb = cx.sml[:, 40:56]
    lnb_sb = cx.sml[:, 56:72]
    wcol = cx.sml[:, 72:73]
    netf = cx.sml[0:64, 73:74]
    p.dma("sp", gcol, cx.d_gate[:, :], writes=("l1gate",), tag="l1c")
    p.dma("sp", lnw_sb, cx.d_lnw[:, :], writes=("lnw",), tag="l1c")
    p.dma("sp", lnb_sb, cx.d_lnb[:, :], writes=("lnb",), tag="l1c")
    p.dma("sp", wcol, cx.d_td[:, :], writes=("l1td",), tag="l1c")
    p.dma("sp", netf, cx.d_tf[:, :], writes=("l1tf",), tag="l1c")
    p.dma("sp", cx.i32[0:1, 0:1], cx.d_hoff[:, :], writes=("hoff",), tag="l1c")
    p.emit("act", lambda e: e.activation(gcol, gcol, AF.Sigmoid), reads=("l1gate",), writes=("l1gate",), tag="l1c")
    p.emit("dve", lambda e: e.tensor_scalar(omg, gcol, -1.0, 1.0, ALU.mult, ALU.add),
           reads=("l1gate",), writes=("l1omg",), tag="l1c")
    p.emit("act", lambda e: e.activation(wcol, wcol, AF.Exp), reads=("l1td",), writes=("l1td",), tag="l1c")
    p.emit("act", lambda e: e.activation(netf, netf, AF.Exp), reads=("l1tf",), writes=("l1tf",), tag="l1c")
    p.emit("dve", lambda e: e.tensor_scalar_mul(netf, netf, -1.0), reads=("l1tf",), writes=("l1tf",), tag="l1c")

    wtsb = [cx.WS[:, b * 2048:(b + 1) * 2048] for b in range(2)]
    wk_sb = cx.WS[:, 4096:5120]
    wv_sb = cx.WS[:, 5120:6144]
    p.dma("pool", wk_sb.rearrange("p (c n) -> p c n", n=64),
          cx.d_wkT[:, :].rearrange("(c p) n -> p c n", p=128), writes=("wks",), tag="l1w")
    p.dma("pool", wv_sb.rearrange("p (c n) -> p c n", n=64),
          cx.d_wvT[:, :].rearrange("(c p) n -> p c n", p=128), writes=("wvs",), tag="l1w")
    p.dma("pool", wo_sb, cx.d_woT[:, :], writes=("wos",), tag="l1w")

    t2 = cx.sc_b[:, 0:512]
    for ec in range(16):
        b = ec % 2
        p.dma("pool", wtsb[b][:, :].rearrange("p (c n) -> p c n", n=128),
              cx.d_wts[ec, :, :].rearrange("(c p) n -> p c n", p=128),
              writes=(("wts", b),), tag="l1w")
        for tt in range(4):
            rtt = (tt + 2) % 4
            for d in range(16):
                p.emit("pe", lambda e, b=b, d=d, rtt=rtt: e.matmul(
                    cx.pb[tt % 2][:, :] if False else cx.pb[0][:, :],
                    wtsb[b][:, d * 128:(d + 1) * 128],
                    A[:, d * 2048 + rtt * 512: d * 2048 + rtt * 512 + 512],
                    start=(d == 0), stop=(d == 15)),
                    reads=(("wts", b), akey(d, rtt)) + fr, writes=("pb0",), tag="l1mm")
            p.emit("dve", lambda e, ec=ec, tt=tt: e.tensor_scalar_mul(
                t2, A[:, ec * 2048 + tt * 512: ec * 2048 + tt * 512 + 512], omg[:, ec:ec + 1]),
                reads=(akey(ec, tt), "l1omg") + fr, writes=("lnsq0",), tag="l1mix")
            p.emit("dve", lambda e, ec=ec, tt=tt: e.scalar_tensor_tensor(
                xs[:, ec * 2048 + tt * 512: ec * 2048 + tt * 512 + 512],
                cx.pb[0][:, :], gcol[:, ec:ec + 1], t2, ALU.mult, ALU.add),
                reads=("pb0", "l1gate", "lnsq0") + fr, writes=(xskey(ec, tt),), tag="l1mix")

    for tt in range(4):
        ln_normalize(cx, xs, lambda c: xskey(c, tt), xs, lambda c: xskey(c, tt), tt,
                     lnwb=(lnw_sb, lnb_sb))

    for tt in range(4):
        for d in range(16):
            p.emit("pe", lambda e, d=d, tt=tt: e.matmul(
                cx.pb[1][0:64, :], wk_sb[:, d * 64:(d + 1) * 64],
                xs[:, d * 2048 + tt * 512: d * 2048 + tt * 512 + 512],
                start=(d == 0), stop=(d == 15)),
                reads=("wks", xskey(d, tt)), writes=("pb1",), tag="l1kv")
        for d in range(16):
            p.emit("pe", lambda e, d=d, tt=tt: e.matmul(
                cx.pb[2][0:64, :], wv_sb[:, d * 64:(d + 1) * 64],
                xs[:, d * 2048 + tt * 512: d * 2048 + tt * 512 + 512],
                start=(d == 0), stop=(d == 15)),
                reads=("wvs", xskey(d, tt)), writes=("pb2",), tag="l1kv")
        p.emit("dve", lambda e, tt=tt: e.tensor_copy(l1kv[0:64, tt * 512:(tt + 1) * 512], cx.pb[1][0:64, :]),
               reads=("pb1",), writes=(("l1k", tt),), tag="l1kv")
        p.emit("dve", lambda e, tt=tt: e.tensor_copy(l1kv[0:64, 2048 + tt * 512: 2048 + (tt + 1) * 512], cx.pb[2][0:64, :]),
               reads=("pb2",), writes=(("l1v", tt),), tag="l1kv")

    kread = tuple(("l1k", t) for t in range(4))
    vread = tuple(("l1v", t) for t in range(4))
    p.emit("act", lambda e: e.activation(scr64, l1kv[0:64, 0:2048], AF.Sigmoid),
           reads=kread, writes=("l1s1",), tag="l1wkv")
    p.emit("dve", lambda e: e.tensor_scalar_mul(scr64, scr64, netf[0:64, :]),
           reads=("l1s1", "l1tf"), writes=("l1s1",), tag="l1wkv")
    p.emit("act", lambda e: e.activation(scr64, scr64, AF.Exp),
           reads=("l1s1",), writes=("l1s1",), tag="l1wkv")
    p.emit("dve", lambda e: e.tensor_mul(l1buf[0:64, 0:2048], scr64, l1kv[0:64, 2048:4096]),
           reads=("l1s1",) + vread, writes=("wkvown",), tag="l1wkv")

    p.dma("pool", cx.d_ccin[:, :], l1buf[0:64, 0:2048], reads=("wkvown",), tag="l1cc")
    p.emit("pool", lambda e: e.collective_compute(
        "AllGather", ALU.bypass, replica_groups=REPLICA_PAIRS,
        ins=[cx.d_ccin.ap().opt()], outs=[cx.d_ccout.ap().opt()]),
        reads=("wkvown",), writes=("ccout",), dma="cc", tag="l1cc")
    for g in range(8):
        p.dma("pool", l1buf[64:128, g * 512:(g + 1) * 512],
              cx.d_ccout[(g % 2) * 64:(g % 2) * 64 + 64, (g // 2) * 512:(g // 2) * 512 + 512],
              reads=("ccout",), writes=(("wkvg", g),), tag="l1cc")

    sml_h = cx.sml
    smlw = sml_h.shape[1]
    wbcast = bass.AP(sml_h, 64 * smlw + 72, [[smlw, 64], [0, 4096]])
    p.emit("dve", lambda e: e.tensor_tensor_scan(
        l1buf[0:64, :], wbcast, l1buf[64:128, :], 0.0, ALU.mult, ALU.add),
        reads=("l1td",) + tuple(("wkvg", g) for g in range(8)) + ("wkvown",),
        writes=("states", "wkvown"), tag="l1scan")
    p.emit("dve", lambda e: e.tensor_copy(stateb, l1buf[0:64, :]),
           reads=("states",), writes=("stateb",), tag="l1scan")

    hreg = {}

    def loadreg(e):
        hreg["r"] = nc.values_load(cx.i32[0:1, 0:1], engines=[ET.PE], min_val=0, max_val=512,
                                   skip_runtime_bounds_check=True)
        return None
    p.emit("pe", loadreg, reads=("hoff",), writes=(), tag="l1reg")
    for oc in range(16):
        for c in range(4):
            p.emit("pe", lambda e, oc=oc, c=c: e.matmul(
                cx.pb[3][:, :], wo_sb[:, oc * 128:(oc + 1) * 128],
                stateb[0:64, ds(hreg["r"] + c * 1024, 512)],
                start=True, stop=True),
                reads=("wos", "stateb", "hoff"), writes=("pb3",), tag="l1out")
            p.emit("dve", lambda e, oc=oc, c=c: e.tensor_add(
                A[:, oc * 2048 + c * 512: oc * 2048 + c * 512 + 512],
                A[:, oc * 2048 + c * 512: oc * 2048 + c * 512 + 512], cx.pb[3][:, :]),
                reads=("pb3", akey(oc, c)), writes=(akey(oc, c),), tag="l1res")



def kernel(**inputs):
    return run_blocks_unfused(inputs, [["l0a", "l0f", "l1a", "l1f"]])


# version 04
# speedup vs baseline: 1.3019x; 1.3012x over previous
import math
import numpy as np
import ml_dtypes
import concourse.bass as bass
import concourse.mybir as mybir
from concourse.bass_utils import run_bass_kernel_spmd

F32 = mybir.dt.float32
BF16 = mybir.dt.bfloat16
I32 = mybir.dt.int32
AF = mybir.ActivationFunctionType
ALU = mybir.AluOpType
ET = mybir.EngineType
ds = bass.ds
BF = ml_dtypes.bfloat16

D = 2048
NCH = 16
NT = 4096
OWN = 2048
TT = 512
NTT = OWN // TT
HEADS, GROUPS, KVC = 16, 4, 128
FI, NIC = 1408, 11
NE = 8
SCALE = KVC ** (-0.5)
EXPB = -20.0
ENG_LIST = ("pe", "act", "dve", "pool", "sp")
ENG = {"pe": "tensor", "act": "scalar", "dve": "vector", "pool": "gpsimd", "sp": "sync"}
SAME_ENGINE_SYNC = True
RELU_ON_DVE = True
REPLICA_PAIRS = [[0, 1], [2, 3], [4, 5], [6, 7]]


class Op:
    __slots__ = ("idx", "eng", "fn", "reads", "writes", "dma", "deps", "needed", "sem", "tag")

    def __init__(self, idx, eng, fn, reads, writes, dma, tag):
        self.idx, self.eng, self.fn = idx, eng, fn
        self.reads, self.writes, self.dma, self.tag = tuple(reads), tuple(writes), dma, tag
        self.deps, self.needed, self.sem = (), False, None


class Prog:
    def __init__(self, nc):
        self.nc = nc
        self.ops = []

    def emit(self, eng, fn, reads=(), writes=(), dma=None, tag=""):
        op = Op(len(self.ops), eng, fn, reads, writes, dma, tag)
        self.ops.append(op)
        if not hasattr(self, "seen_keys"):
            self.seen_keys = set()
        self.seen_keys.update(writes)
        return op

    def barrier(self):
        keys = tuple(getattr(self, "seen_keys", ()))
        return self.emit("dve", lambda e: e.nop(), reads=(), writes=keys, tag="barrier")

    def dma(self, queue, out, in_, reads=(), writes=(), tag=""):
        return self.emit(queue, lambda e, o=out, i=in_: e.dma_start(out=o, in_=i),
                         reads=reads, writes=writes, dma=queue, tag=tag)

    def fence(self, tag_key, old_keys, eng="dve"):
        return self.emit(eng, lambda e: e.nop(), reads=(), writes=tuple(old_keys) + (tag_key,),
                         tag="fence")

    def finalize(self):
        nc = self.nc
        last_write, readers = {}, {}
        last_barrier = None
        for op in self.ops:
            deps = {}
            for k in op.reads:
                d = last_write.get(k)
                if d is not None:
                    deps[d.idx] = d
            for k in op.writes:
                d = last_write.get(k)
                if d is not None:
                    deps[d.idx] = d
                for r in readers.get(k, ()):
                    deps[r.idx] = r
            pruned = []
            for d in deps.values():
                if d is op:
                    continue
                if d.tag == "l1reg":
                    continue
                if d.dma is None and d.eng == op.eng and (d.eng == "pe" or not SAME_ENGINE_SYNC):
                    continue
                if d.dma is not None and d.dma == op.dma:
                    continue
                pruned.append(d)
            if last_barrier is not None and op.tag != "barrier":
                if all(d.idx < last_barrier.idx for d in pruned):
                    pruned.append(last_barrier)
            op.deps = tuple(pruned)
            for d in pruned:
                d.needed = True
            for k in op.reads:
                readers.setdefault(k, []).append(op)
            for k in op.writes:
                last_write[k] = op
                readers[k] = []
            if op.tag == "barrier":
                last_barrier = op

        NLANES_BY_Q = {"pool": 4, "sp": 8, "act": 8}
        counters = {}
        qcount = {}
        sems = set()
        for op in self.ops:
            if op.dma is not None:
                if op.dma == "cc":
                    q = "q_cc"
                    counters[q] = counters.get(q, 0) + 1
                    op.sem = (q, counters[q], 1)
                else:
                    i = qcount.get(op.dma, 0)
                    qcount[op.dma] = i + 1
                    q = f"q_{op.dma}{i % NLANES_BY_Q[op.dma]}"
                    counters[q] = counters.get(q, 0) + 16
                    op.sem = (q, counters[q], 16)
                sems.add(q)
            elif op.needed:
                q = "e_" + op.eng
                counters[q] = counters.get(q, 0) + 1
                op.sem = (q, counters[q], 1)
                sems.add(q)
        self.sem_handles = {n: nc.alloc_semaphore(n) for n in sorted(sems)}
        self.final_counts = dict(counters)

        by_eng = {}
        for op in self.ops:
            by_eng.setdefault(op.eng, []).append(op)
        by_eng.setdefault("sp", [])

        def emitter(ops_list, drain=False):
            def run(engine):
                wm = {}
                for op in ops_list:
                    for d in op.deps:
                        sn, val, _ = d.sem
                        if wm.get(sn, 0) < val:
                            engine.wait_ge(self.sem_handles[sn], val)
                            wm[sn] = val
                    if op.dma is not None and op.dma != "cc":
                        sn, val, _ = op.sem
                        if val > 16 and wm.get(sn, 0) < val - 16:
                            engine.wait_ge(self.sem_handles[sn], val - 16)
                            wm[sn] = val - 16
                    inst = op.fn(engine)
                    if op.sem is not None:
                        sn, _, inc = op.sem
                        if inst is None:
                            raise RuntimeError(f"inc-on-nop fallback hit for op tag={op.tag} eng={op.eng}")
                        inst.then_inc(self.sem_handles[sn], inc)
                if drain:
                    for sn in sorted(self.final_counts):
                        if sn.startswith("q_"):
                            engine.wait_ge(self.sem_handles[sn], self.final_counts[sn])
            return run

        with nc.Block() as block:
            for eng, ops_list in by_eng.items():
                getattr(block, ENG[eng])(emitter(ops_list, drain=(eng == "sp")))

    def stats(self):
        from collections import Counter
        c = Counter(op.eng for op in self.ops)
        return dict(ops=dict(c), n_ops=len(self.ops),
                    n_incs=sum(1 for o in self.ops if o.sem is not None),
                    n_waits=sum(len(o.deps) for o in self.ops))


def _own_perm(h):
    return np.r_[np.arange(h, 8, 2), np.arange(1 - h, 8, 2)]


def _colblocks(w, nb):
    R, C = w.shape
    cb = C // nb
    return np.ascontiguousarray(w.reshape(R, nb, cb).transpose(1, 0, 2))


def stage_shared(inp):
    s = {}
    f32 = np.float32
    s["wq_r"] = _colblocks(np.asarray(inp["L0_attn_w_q"], f32), 16)
    wkv = np.asarray(inp["L0_attn_w_kv"], f32)
    s["wkvk_r"] = _colblocks(wkv[:, :512], 4)
    s["wkvv_r"] = np.ascontiguousarray(wkv[:, 512:].reshape(16, 128, 512))
    s["wd_r"] = _colblocks(np.asarray(inp["L0_attn_w_dense"], f32), 16)
    s["wr"] = np.ascontiguousarray(np.asarray(inp["L0_ffn_Wr"], f32).reshape(16, 128, 8).transpose(1, 0, 2).reshape(128, 128))
    wg = np.asarray(inp["L0_ffn_Wg"], f32)
    wu = np.asarray(inp["L0_ffn_Wu"], f32)
    s["wg_r"] = np.ascontiguousarray(wg.reshape(8, 2048, 11, 128).transpose(0, 2, 1, 3))
    s["wu_r"] = np.ascontiguousarray(wu.reshape(8, 2048, 11, 128).transpose(0, 2, 1, 3))
    wd = np.asarray(inp["L0_ffn_Wd"], f32)
    s["wdd_r"] = np.ascontiguousarray(wd.reshape(8, 1408, 16, 128).transpose(0, 2, 1, 3))
    s["wts_r"] = _colblocks(np.ascontiguousarray(np.asarray(inp["L1_attn_Wts"], f32).T), 16)
    s["wkT"] = np.ascontiguousarray(np.asarray(inp["L1_attn_Wk"], f32).T)
    s["wvT"] = np.ascontiguousarray(np.asarray(inp["L1_attn_Wv"], f32).T)
    s["woT"] = np.ascontiguousarray(np.asarray(inp["L1_attn_Wo"], f32).T)
    s["gate2d"] = np.ascontiguousarray(np.asarray(inp["L1_attn_gate"], f32).reshape(16, 128).T)
    s["lnw2d"] = np.ascontiguousarray(np.asarray(inp["L1_attn_ln_w"], f32).reshape(16, 128).T)
    s["lnb2d"] = np.ascontiguousarray(np.asarray(inp["L1_attn_ln_b"], f32).reshape(16, 128).T)
    s["td"] = np.ascontiguousarray(np.tile(np.asarray(inp["L1_attn_time_decay"], f32).reshape(64, 1), (2, 1)))
    s["tf"] = np.ascontiguousarray(np.asarray(inp["L1_attn_time_first"], f32).reshape(64, 1))
    s["wupT_r"] = _colblocks(np.ascontiguousarray(np.asarray(inp["L1_ffn_W_up"], f32).T), 16)
    s["wdownT_r"] = _colblocks(np.ascontiguousarray(np.asarray(inp["L1_ffn_W_down"], f32).T), 16)
    s["wexp_r"] = _colblocks(np.asarray(inp["L1_ffn_W_expert"], f32), 16)
    s["wgT"] = np.ascontiguousarray(np.asarray(inp["L1_ffn_W_gate"], f32).T.reshape(16, 128, 8).transpose(1, 0, 2).reshape(128, 128))
    return s


def stage_core(inp, c):
    b, h = c // 2, c % 2
    x = np.asarray(inp["L0_attn_x"], np.float32)[b]
    xt = np.ascontiguousarray(x.T)
    perm = _own_perm(h)
    xtp = np.ascontiguousarray(xt.reshape(2048, 8, 512)[:, perm, :].reshape(2048, 4096))
    return {
        "xt": xtp,
        "hflag": np.full((128, 1), float(h), dtype=np.float32),
        "hoff": np.array([[h * 512]], dtype=np.int32),
    }


def make_consts():
    c = {}
    c["ident_bf"] = np.eye(128, dtype=BF)
    c["ident_f32"] = np.eye(128, dtype=np.float32)
    c["ones_col_bf"] = np.ones((128, 1), dtype=BF)
    c["ones_row_f32"] = np.ones((1, 128), dtype=np.float32)
    c["ones_row_bf"] = np.ones((1, 128), dtype=BF)
    dm = np.zeros((128, 4 * 512), dtype=BF)
    for j in range(4):
        kk = np.arange(128)[:, None] + j * 128
        q = np.arange(512)[None, :]
        dm[:, j * 512:(j + 1) * 512] = (kk <= q).astype(BF)
    c["diagmask"] = dm
    oh = np.zeros((8, 8, 64), dtype=BF)
    for e in range(8):
        oh[e, e, :] = 1
    c["onehot64"] = oh.reshape(8, 512)
    cc = np.zeros((128, 8), dtype=np.float32)
    cc[:, 0] = 1e-5
    cc[:, 1] = EXPB
    c["cstcols"] = cc
    return c


class Ctx:
    pass


def make_ctx(nc, blocks):
    cx = Ctx()
    cx.nc = nc
    cx.p = Prog(nc)
    dt = nc.dram_tensor

    need = set(blocks)
    if "l0a" in need:
        cx.d_xt = dt("xt", [2048, 4096], F32, kind="ExternalInput")
        cx.d_wq = dt("wq_r", [16, 2048, 128], F32, kind="ExternalInput")
        cx.d_wkvk = dt("wkvk_r", [4, 2048, 128], F32, kind="ExternalInput")
        cx.d_wkvv = dt("wkvv_r", [16, 128, 512], F32, kind="ExternalInput")
        cx.d_wd = dt("wd_r", [16, 2048, 128], F32, kind="ExternalInput")
        cx.d_hflag = dt("hflag", [128, 1], F32, kind="ExternalInput")
    if "l0f" in need:
        cx.d_wr = dt("wr", [128, 128], F32, kind="ExternalInput")
        cx.d_wg = dt("wg_r", [8, 11, 2048, 128], F32, kind="ExternalInput")
        cx.d_wu = dt("wu_r", [8, 11, 2048, 128], F32, kind="ExternalInput")
        cx.d_wdd = dt("wdd_r", [8, 16, 1408, 128], F32, kind="ExternalInput")
    if "l1a" in need:
        cx.d_wts = dt("wts_r", [16, 2048, 128], F32, kind="ExternalInput")
        cx.d_wkT = dt("wkT", [2048, 64], F32, kind="ExternalInput")
        cx.d_wvT = dt("wvT", [2048, 64], F32, kind="ExternalInput")
        cx.d_woT = dt("woT", [64, 2048], F32, kind="ExternalInput")
        cx.d_gate = dt("gate2d", [128, 16], F32, kind="ExternalInput")
        cx.d_lnw = dt("lnw2d", [128, 16], F32, kind="ExternalInput")
        cx.d_lnb = dt("lnb2d", [128, 16], F32, kind="ExternalInput")
        cx.d_td = dt("td", [128, 1], F32, kind="ExternalInput")
        cx.d_tf = dt("tf", [64, 1], F32, kind="ExternalInput")
        cx.d_hoff = dt("hoff", [1, 1], I32, kind="ExternalInput")
        cx.d_ccin = dt("ccin", [64, 2048], F32, kind="Internal")
        cx.d_ccout = dt("ccout", [128, 2048], F32, kind="Internal")
    if "l1f" in need:
        cx.d_wupT = dt("wupT_r", [16, 2048, 128], F32, kind="ExternalInput")
        cx.d_wdownT = dt("wdownT_r", [16, 2048, 128], F32, kind="ExternalInput")
        cx.d_wexp = dt("wexp_r", [16, 2048, 128], F32, kind="ExternalInput")
        cx.d_wgT = dt("wgT", [128, 128], F32, kind="ExternalInput")

    first = blocks[0]
    cx.unfused_in = None
    if first != "l0a":
        cx.unfused_in = dt("xin", [2048, 2048], F32, kind="ExternalInput")
    last = blocks[-1]
    if last == "l1f":
        cx.d_out = dt("xout", [2048, 2048], F32, kind="ExternalOutput")
    else:
        cx.d_out = dt("xout", [2048, 2048], F32, kind="ExternalOutput")

    cn = make_consts()
    cx.ident_bf = nc.inline_tensor(cn["ident_bf"], name="ident_bf")
    cx.ident_f32 = nc.inline_tensor(cn["ident_f32"], name="ident_f32")
    cx.ones_col = nc.inline_tensor(cn["ones_col_bf"], name="ones_col")
    cx.ones_row = nc.inline_tensor(cn["ones_row_f32"], name="ones_row")
    cx.ones_row_bf = nc.inline_tensor(cn["ones_row_bf"], name="ones_row_bf")
    cx.diagmask = nc.inline_tensor(cn["diagmask"], name="diagmask")
    cx.cstcols = nc.inline_tensor(cn["cstcols"], name="cstcols")
    cx.onehot64 = nc.inline_tensor(cn["onehot64"], name="onehot64")

    al = nc.alloc_sbuf_tensor
    cx.pad = al("pad", [128, 1024], mybir.dt.uint8)
    cx.A = al("A", [128, 16 * 2048], BF16)
    cx.R1 = al("R1", [128, 32768], BF16)
    cx.R2 = al("R2", [128, 22528], BF16)
    cx.WS = al("WS", [128, 6144], BF16)
    cx.rows = al("rows", [1, 1536], F32)
    cx.sc_f = al("sc_f", [128, 2048], F32)
    cx.sc_b = al("sc_b", [128, 3072], BF16)
    cx.srw = cx.sc_b[0:1, 1024:3072]
    cx.sml = al("sml", [128, 80], F32)
    cx.smlb = al("smlb", [128, 32], BF16)
    cx.i32 = al("i32", [1, 2], I32)

    cx.identb = al("identb", [128, 128], BF16)
    cx.identf = al("identf", [128, 128], F32)
    cx.onesc = al("onesc", [128, 1], BF16)
    cx.onesr = al("onesr", [1, 128], F32)
    cx.onesrb = al("onesrb", [1, 128], BF16)

    cx.cst = al("cst", [128, 8], F32)
    cx.dmask = cx.R2[:, 17920:19968]
    cx.pb = [nc.alloc_psum_tensor(f"pb{k}", [128, 512], F32) for k in range(8)]

    p = cx.p
    p.dma("sp", cx.identb[:, :], cx.ident_bf[:, :], writes=("identb",))
    p.dma("sp", cx.identf[:, :], cx.ident_f32[:, :], writes=("identf",))
    p.dma("sp", cx.onesc[:, :], cx.ones_col[:, :], writes=("onesc",))
    p.dma("sp", cx.onesr[:, :], cx.ones_row[:, :], writes=("onesr",))
    p.dma("sp", cx.onesrb[:, :], cx.ones_row_bf[:, :], writes=("onesrb",))
    if "l0a" in blocks:
        p.dma("sp", cx.dmask[:, :], cx.diagmask[:, :], writes=("dmask",))
    p.dma("sp", cx.cst[:, :], cx.cstcols[:, :], writes=("cst",))
    cx.const_keys = ("identb", "identf", "onesc", "onesr", "dmask")
    return cx


def ln_normalize(cx, src, src_key, dst, dst_key, tt, nchunk=16, colstride=2048,
                 col0=0, lnwb=None, extra_reads=()):
    p, nc = cx.p, cx.nc
    pb = cx.pb
    ps_sum, ps_sq = pb[4], pb[5]
    ps_rb, ps_nb = pb[6], pb[7]
    rows = cx.rows
    sq = cx.sc_b

    def col(c):
        return c * colstride + col0 + tt * 512

    for c in range(nchunk):
        sqslice = sq[:, (c % 2) * 512:(c % 2) * 512 + 512]
        p.emit("act", lambda e, c=c, s=sqslice: e.activation(s, src[:, col(c):col(c) + 512], AF.Square),
               reads=(src_key(c),) + tuple(extra_reads), writes=(f"lnsq{c % 2}",), tag="lnsq")
        p.emit("pe", lambda e, c=c: e.matmul(ps_sum[0:1, :], cx.onesc[:, :], src[:, col(c):col(c) + 512],
                                             start=(c == 0), stop=(c == nchunk - 1)),
               reads=(src_key(c), "onesc") + tuple(extra_reads), writes=("pb4",), tag="lnsum")
        p.emit("pe", lambda e, c=c, s=sqslice: e.matmul(ps_sq[0:1, :], cx.onesc[:, :], s,
                                                        start=(c == 0), stop=(c == nchunk - 1)),
               reads=(f"lnsq{c % 2}", "onesc"), writes=("pb5",), tag="lnsumsq")
    m = rows[0:1, 0:512]
    rs = rows[0:1, 512:1024]
    nm = rows[0:1, 1024:1536]
    tmpv = rs
    msq = nm
    inv = 1.0 / D
    p.emit("dve", lambda e: e.tensor_scalar_mul(m, ps_sum[0:1, :], inv),
           reads=("pb4",), writes=("r_m",), tag="lnrow")
    p.emit("dve", lambda e: e.tensor_scalar_mul(tmpv, ps_sq[0:1, :], inv),
           reads=("pb5",), writes=("r_rs",), tag="lnrow")
    p.emit("dve", lambda e: e.tensor_mul(msq, m, m), reads=("r_m",), writes=("r_nm",), tag="lnrow")
    p.emit("dve", lambda e: e.tensor_sub(tmpv, tmpv, msq), reads=("r_rs", "r_nm"), writes=("r_rs",), tag="lnrow")
    p.emit("act", lambda e: e.activation(rs, tmpv, AF.Ln, bias=cx.cst[0:1, 0:1]),
           reads=("r_rs", "cst"), writes=("r_rs",), tag="lnrow")
    p.emit("act", lambda e: e.activation(rs, rs, AF.Exp, scale=-0.5),
           reads=("r_rs",), writes=("r_rs",), tag="lnrow")
    p.emit("dve", lambda e: e.scalar_tensor_tensor(nm, m, -1.0, rs, ALU.mult, ALU.mult),
           reads=("r_m", "r_rs"), writes=("r_nm",), tag="lnrow")
    p.emit("pe", lambda e: e.matmul(ps_rb[:, :], cx.onesr[:, :], rs, start=True, stop=True),
           reads=("r_rs", "onesr"), writes=("pb6",), tag="lnbc")
    p.emit("pe", lambda e: e.matmul(ps_nb[:, :], cx.onesr[:, :], nm, start=True, stop=True),
           reads=("r_nm", "onesr"), writes=("pb7",), tag="lnbc")
    for c in range(nchunk):
        dsl = dst[:, col(c):col(c) + 512]
        p.emit("dve", lambda e, c=c, dsl=dsl: e.tensor_mul(dsl, src[:, col(c):col(c) + 512], ps_rb[:, :]),
               reads=(src_key(c), "pb6") + tuple(extra_reads), writes=(dst_key(c),), tag="lnn1")
        p.emit("dve", lambda e, c=c, dsl=dsl: e.tensor_add(dsl, dsl, ps_nb[:, :]),
               reads=(dst_key(c), "pb7"), writes=(dst_key(c),), tag="lnn2")
        if lnwb is not None:
            lnw_sb, lnb_sb = lnwb
            p.emit("dve", lambda e, c=c, dsl=dsl: e.tensor_scalar(
                dsl, dsl, lnw_sb[:, c:c + 1], lnb_sb[:, c:c + 1], ALU.mult, ALU.add),
                reads=(dst_key(c), "lnw", "lnb"), writes=(dst_key(c),), tag="lnn3")


def load_resid(cx, src_dram):
    p = cx.p
    for c in range(16):
        p.dma("pool", cx.A[:, c * 2048:(c + 1) * 2048],
              src_dram[c * 128:(c + 1) * 128, :],
              writes=tuple(("A", c, t) for t in range(4)), tag="residload")


def store_out_from_A(cx):
    p = cx.p
    for c in range(16):
        p.dma("pool", cx.d_out[c * 128:(c + 1) * 128, :],
              cx.A[:, c * 2048:(c + 1) * 2048],
              reads=tuple(("A", c, t) for t in range(4)), tag="residstore")


def gate_math(cx, xn, xn_key, wsb, wsb_key, mode, out_key, extra_reads=()):
    p = cx.p
    ps_l = cx.pb[3]
    m1, m2, d21, z, den, rden, negm1, zr = (cx.sml[:, j:j + 1] for j in range(8))
    K = lambda j: f"sml{j}"
    g8 = [cx.sc_f[:, 512 * j: 512 * j + 8] for j in range(4)]
    ge1, msk, expl, ctok = g8
    for tc in range(16):
        for d in range(16):
            p.emit("pe", lambda e, tc=tc, d=d: e.matmul(
                ps_l[0:128, 0:8],
                xn[:, d * 2048 + tc * 128: d * 2048 + (tc + 1) * 128],
                wsb[:, d * 8:(d + 1) * 8], start=(d == 0), stop=(d == 15)),
                reads=(xn_key(d, tc // 4), wsb_key) + tuple(extra_reads), writes=("pb3",), tag="glog")
        p.emit("dve", lambda e: e.reduce_max(m1, ps_l[0:128, 0:8], mybir.AxisListType.X),
               reads=("pb3",), writes=(K(0),), tag="gmath")
        p.emit("dve", lambda e: e.tensor_scalar(ge1, ps_l[0:128, 0:8], m1, None, ALU.is_ge),
               reads=("pb3", K(0)), writes=("scf0",), tag="gmath")
        p.emit("dve", lambda e: e.scalar_tensor_tensor(msk, ge1, -1e30, ps_l[0:128, 0:8], ALU.mult, ALU.add),
               reads=("pb3", "scf0"), writes=("scf1",), tag="gmath")
        p.emit("dve", lambda e: e.reduce_max(m2, msk, mybir.AxisListType.X),
               reads=("scf1",), writes=(K(1),), tag="gmath")
        p.emit("dve", lambda e: e.tensor_sub(d21, m2, m1),
               reads=(K(0), K(1)), writes=(K(2),), tag="gmath")
        p.emit("act", lambda e: e.activation(z, d21, AF.Exp),
               reads=(K(2),), writes=(K(3),), tag="gmath")
        p.emit("dve", lambda e: e.tensor_scalar_add(den, z, 1.0),
               reads=(K(3),), writes=(K(4),), tag="gmath")
        p.emit("dve", lambda e: e.reciprocal(rden, den),
               reads=(K(4),), writes=(K(5),), tag="gmath")
        if mode == "c8":
            p.emit("dve", lambda e: e.tensor_scalar_mul(negm1, m1, -1.0),
                   reads=(K(0),), writes=(K(6),), tag="gmath")
            p.emit("act", lambda e: e.activation(expl, ps_l[0:128, 0:8], AF.Exp, bias=negm1),
                   reads=("pb3", K(6)), writes=("scf2",), tag="gmath")
            p.emit("dve", lambda e: e.tensor_scalar(ctok, ps_l[0:128, 0:8], m2, None, ALU.is_ge),
                   reads=("pb3", K(1)), writes=("scf3",), tag="gmath")
            p.emit("dve", lambda e: e.tensor_mul(ctok, ctok, expl),
                   reads=("scf3", "scf2"), writes=("scf3",), tag="gmath")
            p.emit("dve", lambda e: e.tensor_scalar_mul(ctok, ctok, rden),
                   reads=("scf3", K(5)), writes=("scf3",), tag="gmath")
            p.emit("pe", lambda e: e.transpose(cx.pb[2][0:8, 0:128], ctok, cx.identf[:, :]),
                   reads=("scf3", "identf"), writes=("pb2",), tag="gtp")
            p.emit("dve", lambda e, tc=tc: e.tensor_copy(cx.crows[0:8, tc * 128:(tc + 1) * 128], cx.pb[2][0:8, 0:128]),
                   reads=("pb2",), writes=(out_key(tc),), tag="gtp")
        else:
            p.emit("dve", lambda e: e.tensor_mul(zr, z, rden),
                   reads=(K(3), K(5)), writes=(K(7),), tag="gmath")
            p.emit("dve", lambda e: e.tensor_add(zr, rden, zr),
                   reads=(K(5), K(7)), writes=(K(7),), tag="gmath")
            p.emit("pe", lambda e: e.transpose(cx.pb[2][0:1, 0:128], zr, cx.identf[:, :]),
                   reads=(K(7), "identf"), writes=("pb2",), tag="gtp")
            p.emit("dve", lambda e, tc=tc: e.tensor_copy(cx.srw[0:1, tc * 128:(tc + 1) * 128], cx.pb[2][0:1, 0:128]),
                   reads=("pb2",), writes=(out_key(tc),), tag="gtp")


def bld_l1f(cx, fences=()):
    p, nc = cx.p, cx.nc
    A = cx.A
    xn = cx.R1
    y1 = cx.R2
    fr = tuple(fences)

    cx.wg_sb = nc.alloc_sbuf_tensor("wg_sb", [128, 128], BF16)
    p.dma("pool", cx.wg_sb[:, :], cx.d_wgT[:, :], writes=("wgT",), tag="wgT")

    akey = lambda c, t: ("A", c, t)
    xnkey = lambda c, t: ("xn3", c, t)

    if cx.dbg not in ("w0nl",):
        for tt in range(4):
            ln_normalize(cx, A, lambda c: akey(c, tt), xn, lambda c: xnkey(c, tt), tt, extra_reads=fr)

    if cx.dbg == "xn":
        for c in range(16):
            p.dma("pool", cx.d_out[c * 128:(c + 1) * 128, :], xn[:, c * 2048:(c + 1) * 2048],
                  reads=tuple(xnkey(c, t) for t in range(4)), tag="dbg")
        return
    if cx.dbg not in ("xng", "w0ng", "w0nl", "y1ng", "psdump", "y1nc", "mx"):
        gate_math(cx, xn, xnkey, cx.wg_sb, "wgT", "rw", lambda tc: ("rw", tc))
    if cx.dbg == "xng":
        for c in range(16):
            p.dma("pool", cx.d_out[c * 128:(c + 1) * 128, :], xn[:, c * 2048:(c + 1) * 2048],
                  reads=tuple(xnkey(c, t) for t in range(4)), tag="dbg")
        return
    if cx.dbg == "rw":
        p.dma("pool", cx.d_out[0:1, 0:2048], cx.srw[0:1, :],
              reads=tuple(("rw", tc) for tc in range(16)), tag="dbg")
        return

    wub = [cx.WS[:, 0:2048], cx.WS[:, 2048:4096]]
    wdb = [cx.WS[:, 0:2048], cx.WS[:, 2048:4096]]
    web = [cx.R2[:, 18432 + b * 2048: 18432 + (b + 1) * 2048] for b in range(2)]

    def wdma(dram, buf, key, idx):
        p.dma("pool", buf[:, :].rearrange("p (c n) -> p c n", n=128),
              dram[idx, :, :].rearrange("(c p) n -> p c n", p=128),
              writes=(key,), tag="wstream")

    SCF = lambda j: cx.sc_f[:, j * 512:(j + 1) * 512]

    if cx.dbg == "w2":
        wdma(cx.d_wupT, wub[0], ("wup", 0), 0)
        wdma(cx.d_wupT, wub[1], ("wup", 1), 1)
        for b2 in range(2):
            p.dma("pool", cx.d_out[b2 * 128:(b2 + 1) * 128, 0:2048], wub[b2][:, :],
                  reads=(("wup", b2),), tag="dbg")
        return

    NOCU = {"y1one": 1, "y1four": 4, "y1ng": 1, "psdump": 1, "y1nc": 1, "mx": 1}.get(cx.dbg, 16)
    for half in range(2):
        tts = (2 * half, 2 * half + 1)
        for ocu in range(NOCU):
            bu = ocu % 2
            wdma(cx.d_wupT, wub[bu], ("wsAB", bu), ocu)
            for d in range(16):
                for j, tt in enumerate(tts):
                    p.emit("pe", lambda e, bu=bu, d=d, tt=tt, j=j: e.matmul(
                        cx.pb[4 + j][:, :], wub[bu][:, d * 128:(d + 1) * 128],
                        xn[:, d * 2048 + tt * 512: d * 2048 + tt * 512 + 512],
                        start=(d == 0), stop=(d == 15)),
                        reads=(("wsAB", bu), xnkey(d, tt)), writes=(f"pb{4 + j}",), tag="up")
            if cx.dbg == "psdump" and half == 0 and ocu == 0:
                p.emit("dve", lambda e: e.tensor_copy(cx.sc_f[:, 0:512], cx.pb[4][:, :]),
                       reads=("pb4",), writes=("scf0",), tag="dbg")
                p.dma("sp", cx.d_out[0:128, 0:512], cx.sc_f[:, 0:512], reads=("scf0",), tag="dbg")
                p.emit("dve", lambda e: e.tensor_copy(cx.sc_f[:, 512:1024], cx.pb[5][:, :]),
                       reads=("pb5",), writes=("scf1",), tag="dbg")
                p.dma("sp", cx.d_out[0:128, 512:1024], cx.sc_f[:, 512:1024], reads=("scf1",), tag="dbg")
                p.dma("pool", cx.d_out[128:256, 0:2048], wub[0][:, :], reads=(("wup", 0),), tag="dbg")
                p.dma("pool", cx.d_out[256:384, 0:2048], xn[:, 0:2048], reads=tuple(xnkey(0, t) for t in range(4)), tag="dbg")
                return
            for j, tt in enumerate(tts):
                if RELU_ON_DVE:
                    p.emit("dve", lambda e, ocu=ocu, j=j: e.tensor_relu(
                        y1[:, ocu * 1024 + j * 512: ocu * 1024 + j * 512 + 512],
                        cx.pb[4 + j][:, :]),
                        reads=(f"pb{4 + j}",) + fr, writes=(("y1", ocu, j),), tag="relu")
                else:
                    p.emit("act", lambda e, ocu=ocu, j=j: e.activation(
                        y1[:, ocu * 1024 + j * 512: ocu * 1024 + j * 512 + 512],
                        cx.pb[4 + j][:, :], AF.Relu),
                        reads=(f"pb{4 + j}",) + fr, writes=(("y1", ocu, j),), tag="relu")
        if cx.dbg == "m_y1" and half == 0:
            for c in range(16):
                p.emit("dve", lambda e, c=c: e.tensor_copy(cx.sc_f[:, 0:512], y1[:, c * 1024:c * 1024 + 512]),
                       reads=(("y1", c, 0),), writes=("scf0",), tag="dbg")
                p.dma("sp", cx.d_out[c * 128:(c + 1) * 128, 0:512], cx.sc_f[:, 0:512], reads=("scf0",), tag="dbg")
                p.emit("dve", lambda e, c=c: e.tensor_copy(cx.sc_f[:, 512:1024], y1[:, c * 1024 + 512:c * 1024 + 1024]),
                       reads=(("y1", c, 1),), writes=("scf1",), tag="dbg")
                p.dma("sp", cx.d_out[c * 128:(c + 1) * 128, 512:1024], cx.sc_f[:, 512:1024], reads=("scf1",), tag="dbg")
            return
        for j, tt in enumerate(tts):
            rb_slot = 1 if j == 0 else 3
            p.emit("pe", lambda e, tt=tt: e.matmul(
                cx.pb[6][:, :], cx.onesrb[:, :], cx.srw[0:1, tt * 512:(tt + 1) * 512],
                start=True, stop=True),
                reads=tuple(("rw", 4 * tt + i2) for i2 in range(4)) + ("onesrb",),
                writes=("pb6",), tag="rwbc")
            p.emit("act", lambda e, rb_slot=rb_slot: e.copy(SCF(rb_slot), cx.pb[6][:, :]),
                   reads=("pb6",), writes=(f"scf{rb_slot}",), tag="rwbc")
            if cx.dbg == "m_rb" and half == 0 and j == 0:
                p.dma("sp", cx.d_out[0:128, 0:512], SCF(rb_slot), reads=(f"scf{rb_slot}",), tag="dbg")
                p.emit("dve", lambda e: e.tensor_copy(cx.sc_f[:, 1024:1536], cx.pb[6][:, :]),
                       reads=("pb6",), writes=("scf2",), tag="dbg")
                p.dma("sp", cx.d_out[128:256, 0:512], cx.sc_f[:, 1024:1536], reads=("scf2",), tag="dbg")
                p.emit("dve", lambda e: e.tensor_copy(cx.sc_f[0:1, 0:512], cx.srw[0:1, 0:512]),
                       reads=tuple(("rw", i2) for i2 in range(4)), writes=("scf0",), tag="dbg")
                p.dma("sp", cx.d_out[256:257, 0:512], cx.sc_f[0:1, 0:512], reads=("scf0",), tag="dbg")
                return
        for oc in range(16):
            bd = oc % 2
            wdma(cx.d_wdownT, wdb[bd], ("wsAB", bd), oc)
            wdma(cx.d_wexp, web[bd], ("wexp", bd), oc)
            for j, tt in enumerate(tts):
                rb_slot = 1 if j == 0 else 3
                for od in range(16):
                    p.emit("pe", lambda e, bd=bd, od=od, j=j: e.matmul(
                        cx.pb[j][:, :], wdb[bd][:, od * 128:(od + 1) * 128],
                        y1[:, od * 1024 + j * 512: od * 1024 + j * 512 + 512],
                        start=(od == 0), stop=(od == 15)),
                        reads=(("wsAB", bd), ("y1", od, j)), writes=(f"pb{j}",), tag="down")
                for d in range(16):
                    p.emit("pe", lambda e, bd=bd, d=d, tt=tt, j=j: e.matmul(
                        cx.pb[2 + j][:, :], web[bd][:, d * 128:(d + 1) * 128],
                        xn[:, d * 2048 + tt * 512: d * 2048 + tt * 512 + 512],
                        start=(d == 0), stop=(d == 15)),
                        reads=(("wexp", bd), xnkey(d, tt)), writes=(f"pb{2 + j}",), tag="wex")
                p.emit("dve", lambda e, rb_slot=rb_slot, j=j: e.tensor_mul(SCF(2), cx.pb[2 + j][:, :], SCF(rb_slot)),
                       reads=(f"pb{2 + j}", f"scf{rb_slot}"), writes=("scf2",), tag="comb")
                p.emit("dve", lambda e, j=j: e.tensor_add(SCF(2), SCF(2), cx.pb[j][:, :]),
                       reads=("scf2", f"pb{j}"), writes=("scf2",), tag="comb")
                p.emit("dve", lambda e, oc=oc, tt=tt: e.tensor_add(
                    SCF(0), SCF(2), A[:, oc * 2048 + tt * 512: oc * 2048 + tt * 512 + 512]),
                    reads=("scf2", akey(oc, tt)), writes=("scf0",), tag="comb")
                p.dma("sp", cx.d_out[oc * 128:(oc + 1) * 128, tt * 512:(tt + 1) * 512], SCF(0),
                      reads=("scf0",), tag="outstore")
                if cx.dbg == "m_comb" and half == 0 and oc == 0 and j == 0:
                    p.emit("dve", lambda e, j=j: e.tensor_copy(cx.sc_f[:, 512:1024], cx.pb[j][:, :]),
                           reads=(f"pb{j}",), writes=("scf1",), tag="dbg")
                    p.dma("sp", cx.d_out[128:256, 0:512], cx.sc_f[:, 512:1024], reads=("scf1",), tag="dbg")
                    p.emit("dve", lambda e, j=j: e.tensor_copy(cx.sc_f[:, 1536:2048], cx.pb[2 + j][:, :]),
                           reads=(f"pb{2 + j}",), writes=("scf3",), tag="dbg")
                    p.dma("sp", cx.d_out[256:384, 0:512], cx.sc_f[:, 1536:2048], reads=("scf3",), tag="dbg")
                    p.dma("sp", cx.d_out[384:512, 0:512], SCF(2), reads=("scf2",), tag="dbg")
                    p.dma("sp", cx.d_out[512:640, 0:512], SCF(1), reads=("scf1",), tag="dbg")
                    p.dma("sp", cx.d_out[640:768, 0:512], SCF(0), reads=("scf0",), tag="dbg")
                    p.dma("pool", cx.d_out[768:896, 0:512], A[:, 0:512], reads=(akey(0, 0),), tag="dbg")
                    return


def build_launch(blocks, dbg=None):
    nc = bass.Bass("TRN2", target_bir_lowering=False, dynamic_dma_scratch_size=16384)
    cx = make_ctx(nc, blocks)
    cx.dbg = dbg
    if "l0f" in blocks:
        cx.crows = cx.sc_b[0:8, 1024:3072]
    if cx.unfused_in is not None:
        load_resid(cx, cx.unfused_in)
    for bi, b in enumerate(blocks):
        if bi > 0:
            cx.p.barrier()
        if b == "l0a":
            bld_l0a(cx)
        elif b == "l0f":
            bld_l0f(cx)
        elif b == "l1a":
            bld_l1a(cx)
        elif b == "l1f":
            bld_l1f(cx)
            if cx.dbg:
                return_after = True
    if blocks[-1] != "l1f" and (not dbg or dbg == "f_e0"):
        store_out_from_A(cx)
    cx.p.finalize()
    return nc, cx


def own_cols(h):
    perm = _own_perm(h)[:4]
    return (perm[:, None] * 512 + np.arange(512)[None, :]).reshape(-1)


def run_blocks_unfused(inp, blocks_groups, x_stage=None):
    sh = stage_shared(inp)
    cores = [stage_core(inp, c) for c in range(8)]
    cur = None
    for gi, blocks in enumerate(blocks_groups):
        nc, cx = build_launch(blocks)
        in_maps = []
        for c in range(8):
            im = {}
            for attr, key in [("d_xt", "xt"), ("d_wq", "wq_r"), ("d_wkvk", "wkvk_r"),
                              ("d_wkvv", "wkvv_r"), ("d_wd", "wd_r"), ("d_hflag", "hflag"),
                              ("d_wr", "wr"), ("d_wg", "wg_r"), ("d_wu", "wu_r"), ("d_wdd", "wdd_r"),
                              ("d_wts", "wts_r"), ("d_wkT", "wkT"), ("d_wvT", "wvT"), ("d_woT", "woT"),
                              ("d_gate", "gate2d"), ("d_lnw", "lnw2d"), ("d_lnb", "lnb2d"),
                              ("d_td", "td"), ("d_tf", "tf"), ("d_hoff", "hoff"),
                              ("d_wupT", "wupT_r"), ("d_wdownT", "wdownT_r"), ("d_wexp", "wexp_r"),
                              ("d_wgT", "wgT")]:
                if hasattr(cx, attr):
                    im[key] = sh[key] if key in sh else cores[c][key]
            if cx.unfused_in is not None:
                im["xin"] = x_stage[c] if x_stage is not None else cur[c]
            in_maps.append(im)
        res = run_bass_kernel_spmd(nc, in_maps, core_ids=list(range(8)))
        cur = [np.ascontiguousarray(res.results[c]["xout"]) for c in range(8)]
        x_stage = None
    out = np.zeros((4, 4096, 2048), np.float32)
    for c in range(8):
        b, h = c // 2, c % 2
        cols = own_cols(h)
        out[b][cols, :] = cur[c].T
    return out



def bld_l0a(cx, fences=()):
    p, nc = cx.p, cx.nc
    A = cx.A
    fr = tuple(fences)
    akey = lambda c, t: ("A", c, t)

    kT = cx.R1[:, 0:16384]
    vt = cx.R1[:, 16384:32768]
    xtile = cx.R2[:, 0:8192]
    ptb = [cx.R2[:, 8192 + b * 512: 8192 + (b + 1) * 512] for b in range(3)]
    ctxT = cx.R2[:, 9728:17920]
    wsA = cx.WS[:, 0:2048]
    wsB = cx.WS[:, 2048:4096]
    wvv = [cx.WS[:, 4096 + b * 512: 4096 + (b + 1) * 512] for b in range(2)]
    hf = cx.sml[:, 74:75]
    p.dma("sp", hf, cx.d_hflag[:, :], writes=("hflag",), tag="l0aw")

    for pt in range(8):
        xk = lambda c, pt=pt: ("xt", c)
        p.dma("pool", xtile[:, :].rearrange("p (c n) -> p c n", n=512),
              cx.d_xt[:, pt * 512:(pt + 1) * 512].rearrange("(c p) n -> p c n", p=128),
              reads=tuple(("xt", c) for c in range(16)), writes=tuple(("xt", c) for c in range(16)),
              tag="l0ax")
        ln_normalize(cx, xtile, xk, xtile, xk, 0, colstride=512, extra_reads=fr)
        for kfc in range(4):
            wb, wk = (wsA, "wsA") if kfc % 2 == 0 else (wsB, "wsB")
            p.dma("pool", wb[:, :].rearrange("p (c n) -> p c n", n=128),
                  cx.d_wkvk[kfc, :, :].rearrange("(c p) n -> p c n", p=128),
                  writes=(wk,), tag="l0aw")
            for d in range(16):
                p.emit("pe", lambda en, wb=wb, d=d: en.matmul(
                    cx.pb[0][:, :], wb[:, d * 128:(d + 1) * 128],
                    xtile[:, d * 512:(d + 1) * 512], start=(d == 0), stop=(d == 15)),
                    reads=(wk, ("xt", d)), writes=("pb0",), tag="l0akv")
            p.emit("dve", lambda en, kfc=kfc, pt=pt: en.tensor_copy(
                kT[:, kfc * 4096 + pt * 512: kfc * 4096 + (pt + 1) * 512], cx.pb[0][:, :]),
                reads=("pb0",), writes=(("kT", kfc, pt),), tag="l0akv")
        for d in range(16):
            b = d % 2
            p.dma("pool", wvv[b][:, :], cx.d_wkvv[d, :, :],
                  writes=(("wvv", b),), tag="l0aw")
            for ps4 in range(4):
                p.emit("pe", lambda en, d=d, ps4=ps4, b=b: en.matmul(
                    cx.pb[1 + ps4][:, :], xtile[:, d * 512 + ps4 * 128: d * 512 + (ps4 + 1) * 128],
                    wvv[b][:, :], start=(d == 0), stop=(d == 15)),
                    reads=(("wvv", b), ("xt", d)), writes=(f"pb{1 + ps4}",), tag="l0akv")
        for ps4 in range(4):
            p.emit("dve", lambda en, pt=pt, ps4=ps4: en.tensor_copy(
                vt[:, (pt * 4 + ps4) * 512: (pt * 4 + ps4 + 1) * 512], cx.pb[1 + ps4][:, :]),
                reads=(f"pb{1 + ps4}",), writes=(("vt", pt * 4 + ps4),), tag="l0akv")
        if pt < 4:
            for ec in range(16):
                wb, wk = (wsA, "wsA") if ec % 2 == 0 else (wsB, "wsB")
                p.dma("pool", wb[:, :].rearrange("p (c n) -> p c n", n=128),
                      cx.d_wq[ec, :, :].rearrange("(c p) n -> p c n", p=128),
                      writes=(wk,), tag="l0aw")
                for d in range(16):
                    p.emit("pe", lambda en, wb=wb, d=d, qb=5 + ec % 2: en.matmul(
                        cx.pb[qb][:, :], wb[:, d * 128:(d + 1) * 128],
                        xtile[:, d * 512:(d + 1) * 512], start=(d == 0), stop=(d == 15)),
                        reads=(wk, ("xt", d)), writes=(f"pb{5 + ec % 2}",), tag="l0aq")
                p.emit("dve", lambda en, ec=ec, pt=pt: en.tensor_copy(
                    A[:, ec * 2048 + pt * 512: ec * 2048 + (pt + 1) * 512], cx.pb[5 + ec % 2][:, :]),
                    reads=(f"pb{5 + ec % 2}",), writes=(akey(ec, pt),), tag="l0aq")

    if cx.dbg == "a_s1":
        for c in range(8):
            p.emit("dve", lambda en, c=c: en.tensor_copy(cx.sc_f[:, 0:512], A[:, c * 2048: c * 2048 + 512]),
                   reads=(akey(c, 0),), writes=("scf0",), tag="dbg")
            p.dma("sp", cx.d_out[c * 128:(c + 1) * 128, 0:512], cx.sc_f[:, 0:512], reads=("scf0",), tag="dbg")
        for kfc in range(4):
            p.emit("dve", lambda en, kfc=kfc: en.tensor_copy(cx.sc_f[:, 0:512], kT[:, kfc * 4096: kfc * 4096 + 512]),
                   reads=(("kT", kfc, 0),), writes=("scf0",), tag="dbg")
            p.dma("sp", cx.d_out[(8 + kfc) * 128:(9 + kfc) * 128, 0:512], cx.sc_f[:, 0:512], reads=("scf0",), tag="dbg")
        for p4 in range(4):
            p.emit("dve", lambda en, p4=p4: en.tensor_copy(cx.sc_f[:, 0:512], vt[:, p4 * 512:(p4 + 1) * 512]),
                   reads=(("vt", p4),), writes=("scf0",), tag="dbg")
            p.dma("sp", cx.d_out[(12 + p4) * 128:(13 + p4) * 128, 0:512], cx.sc_f[:, 0:512], reads=("scf0",), tag="dbg")
        return
    for qt in range(4):
        nkt = 8 * (qt + 1)
        for head in range(16):
            g = head // 4
            cB = f"pb{2 + head % 2}"

            def kmeta(kt, qt=qt):
                if kt < 4 * (qt + 1):
                    return kt * 128
                return 2048 + (kt - 4 * (qt + 1)) * 128

            def emit_spt(kt, head=head, qt=qt, nkt=nkt, g=g, kmeta=kmeta):
                kcol = kmeta(kt)
                b3 = kt % 3
                pb_s = cx.pb[(0, 1, 5)[b3]]
                sB = f"pb{(0, 1, 5)[b3]}"
                ptbuf = ptb[b3]
                ptk = ("ptb", b3)
                p.emit("pe", lambda en, g=g, kcol=kcol, head=head, qt=qt, pb_s=pb_s: en.matmul(
                    pb_s[:, :], kT[:, g * 4096 + kcol: g * 4096 + kcol + 128],
                    A[:, head * 2048 + qt * 512: head * 2048 + (qt + 1) * 512],
                    start=True, stop=True),
                    reads=(("kT", g, kcol // 512), akey(head, qt)), writes=(sB,), tag="l0as")
                p.emit("act", lambda en, pb_s=pb_s, ptbuf=ptbuf: en.activation(
                    ptbuf[:, :], pb_s[:, :], AF.Exp, bias=cx.cst[:, 1:2], scale=SCALE),
                    reads=(sB, "cst"), writes=(ptk,), tag="l0ae")
                if 4 * qt <= kt < 4 * (qt + 1):
                    j = kt - 4 * qt
                    p.emit("dve", lambda en, j=j, ptbuf=ptbuf: en.tensor_mul(
                        ptbuf[:, :], ptbuf[:, :], cx.dmask[:, j * 512:(j + 1) * 512]),
                        reads=(ptk, "dmask"), writes=(ptk,), tag="l0am")
                if kt >= nkt - 4:
                    p.emit("dve", lambda en, ptbuf=ptbuf: en.tensor_scalar_mul(
                        ptbuf[:, :], ptbuf[:, :], hf),
                        reads=(ptk, "hflag"), writes=(ptk,), tag="l0am")

            def emit_cr(kt, head=head, qt=qt, nkt=nkt, g=g, kmeta=kmeta):
                kcol = kmeta(kt)
                b3 = kt % 3
                ptbuf = ptb[b3]
                ptk = ("ptb", b3)
                p.emit("pe", lambda en, kcol=kcol, g=g, kt=kt, nkt=nkt, ptbuf=ptbuf, cBi=2 + head % 2: en.matmul(
                    cx.pb[cBi][:, :], vt[:, (kcol // 128) * 512 + g * 128: (kcol // 128) * 512 + (g + 1) * 128],
                    ptbuf[:, :], start=(kt == 0), stop=(kt == nkt - 1)),
                    reads=(("vt", kcol // 128), ptk), writes=(cB,), tag="l0ac")
                p.emit("pe", lambda en, kt=kt, nkt=nkt, ptbuf=ptbuf: en.matmul(
                    cx.pb[4][0:1, :], cx.onesc[:, :], ptbuf[:, :],
                    start=(kt == 0), stop=(kt == nkt - 1)),
                    reads=("onesc", ptk), writes=("pb4",), tag="l0ar")

            emit_spt(0)
            if nkt > 1:
                emit_spt(1)
            for kt in range(nkt):
                if kt + 2 < nkt:
                    emit_spt(kt + 2)
                emit_cr(kt)
            rsum = cx.rows[0:1, 0:512]
            p.emit("dve", lambda en, rsum=rsum: en.reciprocal(rsum, cx.pb[4][0:1, :]),
                   reads=("pb4",), writes=("r_m",), tag="l0an")
            p.emit("pe", lambda en, rsum=rsum: en.matmul(
                cx.pb[7][:, :], cx.onesr[:, :], rsum, start=True, stop=True),
                reads=("r_m", "onesr"), writes=("pb7",), tag="l0an")
            p.emit("act", lambda en: en.copy(cx.sc_f[:, 0:512], cx.pb[7][:, :]),
                   reads=("pb7",), writes=("scf0",), tag="l0an")
            p.emit("dve", lambda en, head=head, cBi=2 + head % 2: en.tensor_mul(
                ctxT[:, head * 512:(head + 1) * 512], cx.pb[cBi][:, :], cx.sc_f[:, 0:512]),
                reads=(cB, "scf0"), writes=(("ctx", head),), tag="l0an")
        for oc in range(16):
            wb, wk = (wsA, "wsA") if oc % 2 == 0 else (wsB, "wsB")
            p.dma("pool", wb[:, :].rearrange("p (c n) -> p c n", n=128),
                  cx.d_wd[oc, :, :].rearrange("(c p) n -> p c n", p=128),
                  writes=(wk,), tag="l0aw")
            x0sb = cx.sc_f[:, 512:1024]
            p.dma("sp", x0sb, cx.d_xt[oc * 128:(oc + 1) * 128, qt * 512:(qt + 1) * 512],
                  reads=("scf1",), writes=("scf1",), tag="l0ax0")
            dB = f"pb{5 + oc % 2}"
            for ec in range(16):
                p.emit("pe", lambda en, wb=wb, ec=ec, dBi=5 + oc % 2: en.matmul(
                    cx.pb[dBi][:, :], wb[:, ec * 128:(ec + 1) * 128],
                    ctxT[:, ec * 512:(ec + 1) * 512], start=(ec == 0), stop=(ec == 15)),
                    reads=(wk, ("ctx", ec)), writes=(dB,), tag="l0ad")
            p.emit("dve", lambda en, oc=oc, qt=qt, dBi=5 + oc % 2: en.tensor_add(
                A[:, oc * 2048 + qt * 512: oc * 2048 + (qt + 1) * 512],
                cx.pb[dBi][:, :], x0sb),
                reads=(dB, "scf1", akey(oc, qt)), writes=(akey(oc, qt),), tag="l0adr")


def bld_l0f(cx, fences=()):
    p, nc = cx.p, cx.nc
    A, xn, hT = cx.A, cx.R1, cx.R2
    fr = tuple(fences)
    akey = lambda c, t: ("A", c, t)
    xnkey = lambda c, t: ("l0fxn", c, t)

    wgb = cx.WS[:, 0:2048]
    wub_ = cx.WS[:, 2048:4096]
    wdb0 = cx.WS[:, 4096:5504]
    wdb1 = cx.WS[:, 2048:3456]
    wr_sb = cx.WS[:, 5504:5632]
    oh_sb = cx.WS[0:8, 5632:6144]
    p.dma("pool", wr_sb, cx.d_wr[:, :], writes=("wrsb",), tag="l0fw")
    p.dma("sp", oh_sb, cx.onehot64[:, :], writes=("oh64",), tag="l0fw")

    for tt in range(4):
        ln_normalize(cx, A, lambda c: akey(c, tt), xn, lambda c: xnkey(c, tt), tt, extra_reads=fr)

    gate_math(cx, xn, xnkey, wr_sb, "wrsb", "c8", lambda tc: ("crows", tc))
    crk = tuple(("crows", tc) for tc in range(16))

    if cx.dbg == "f_c":
        for q in range(4):
            p.emit("dve", lambda en, q=q: en.tensor_copy(cx.sc_f[0:8, 0:512], cx.crows[0:8, q * 512:(q + 1) * 512]),
                   reads=crk, writes=("scf0",), tag="dbg")
            p.dma("sp", cx.d_out[0:8, q * 512:(q + 1) * 512], cx.sc_f[0:8, 0:512], reads=("scf0",), tag="dbg")
        return

    sscr = cx.sc_b[:, 0:512]

    for e in range(1 if cx.dbg == 'f_e0' else 8):
        for i in range(NIC):
            p.dma("pool", wgb[:, :].rearrange("p (c n) -> p c n", n=128),
                  cx.d_wg[e, i, :, :].rearrange("(c p) n -> p c n", p=128),
                  writes=("wsA",), tag="l0fw")
            for d in range(16):
                for tt in range(4):
                    p.emit("pe", lambda en, d=d, tt=tt: en.matmul(
                        cx.pb[tt][:, :], wgb[:, d * 128:(d + 1) * 128],
                        xn[:, d * 2048 + tt * 512: d * 2048 + tt * 512 + 512],
                        start=(d == 0), stop=(d == 15)),
                        reads=("wsA", xnkey(d, tt)), writes=(f"pb{tt}",), tag="l0fg")
            p.dma("pool", wub_[:, :].rearrange("p (c n) -> p c n", n=128),
                  cx.d_wu[e, i, :, :].rearrange("(c p) n -> p c n", p=128),
                  writes=("wsB",), tag="l0fw")
            for d in range(16):
                for tt in range(4):
                    p.emit("pe", lambda en, d=d, tt=tt: en.matmul(
                        cx.pb[4 + tt][:, :], wub_[:, d * 128:(d + 1) * 128],
                        xn[:, d * 2048 + tt * 512: d * 2048 + tt * 512 + 512],
                        start=(d == 0), stop=(d == 15)),
                        reads=("wsB", xnkey(d, tt)), writes=(f"pb{4 + tt}",), tag="l0fu")
            for tt in range(4):
                p.emit("act", lambda en, tt=tt: en.activation(sscr, cx.pb[tt][:, :], AF.Silu),
                       reads=(f"pb{tt}",), writes=("lnsq0",), tag="l0fh")
                p.emit("dve", lambda en, i=i, tt=tt: en.tensor_mul(
                    hT[:, i * 2048 + tt * 512: i * 2048 + tt * 512 + 512], sscr, cx.pb[4 + tt][:, :]),
                    reads=("lnsq0", f"pb{4 + tt}") + fr, writes=(("hT", i, tt),), tag="l0fh")
        if cx.dbg == "f_h" and e == 0:
            for i2 in (0, 5):
                for q in range(4):
                    p.emit("dve", lambda en, i2=i2, q=q: en.tensor_copy(
                        cx.sc_f[:, 0:512], hT[:, i2 * 2048 + q * 512: i2 * 2048 + q * 512 + 512]),
                        reads=(("hT", i2, q),), writes=("scf0",), tag="dbg")
                    p.dma("sp", cx.d_out[(0 if i2 == 0 else 128):(128 if i2 == 0 else 256), q * 512:(q + 1) * 512],
                          cx.sc_f[:, 0:512], reads=("scf0",), tag="dbg")
            return
        for tt in range(4):
            p.emit("pe", lambda en, e=e, tt=tt: en.matmul(
                cx.pb[6][0:64, :], oh_sb[0:8, e * 64:(e + 1) * 64],
                cx.crows[0:8, tt * 512:(tt + 1) * 512], start=True, stop=True),
                reads=("oh64",) + crk, writes=("pb6",), tag="l0fcb")
            p.emit("act", lambda en: en.copy(sscr[0:64, :], cx.pb[6][0:64, :]),
                   reads=("pb6",), writes=("lnsq0",), tag="l0fcb")
            p.emit("act", lambda en: en.copy(sscr[64:128, :], cx.pb[6][0:64, :]),
                   reads=("pb6",), writes=("lnsq0",), tag="l0fcb")
            if cx.dbg == "f_cb" and e == 0:
                p.emit("dve", lambda en: en.tensor_copy(cx.sc_f[:, 0:512], sscr),
                       reads=("lnsq0",), writes=("scf0",), tag="dbg")
                p.dma("sp", cx.d_out[0:128, tt * 512:(tt + 1) * 512], cx.sc_f[:, 0:512],
                      reads=("scf0",), tag="dbg")
            for i in range(NIC):
                p.emit("dve", lambda en, i=i, tt=tt: en.tensor_mul(
                    hT[:, i * 2048 + tt * 512: i * 2048 + tt * 512 + 512],
                    hT[:, i * 2048 + tt * 512: i * 2048 + tt * 512 + 512], sscr),
                    reads=(("hT", i, tt), "lnsq0"), writes=(("hT", i, tt),), tag="l0fcs")
        if cx.dbg == "f_h2" and e == 0:
            for i2 in (0, 5):
                for q in range(4):
                    p.emit("dve", lambda en, i2=i2, q=q: en.tensor_copy(
                        cx.sc_f[:, 0:512], hT[:, i2 * 2048 + q * 512: i2 * 2048 + q * 512 + 512]),
                        reads=(("hT", i2, q),), writes=("scf0",), tag="dbg")
                    p.dma("sp", cx.d_out[(0 if i2 == 0 else 128):(128 if i2 == 0 else 256), q * 512:(q + 1) * 512],
                          cx.sc_f[:, 0:512], reads=("scf0",), tag="dbg")
            return
        if cx.dbg == "f_cb" and e == 0:
            return
        for oc in range(16):
            wdb = wdb0 if oc % 2 == 0 else wdb1
            wdk = "wsC" if oc % 2 == 0 else "wsB"
            p.dma("pool", wdb[:, :].rearrange("p (c n) -> p c n", n=128),
                  cx.d_wdd[e, oc, :, :].rearrange("(c p) n -> p c n", p=128),
                  writes=(wdk,), tag="l0fw")
            for i in range(NIC):
                for tt in range(4):
                    p.emit("pe", lambda en, i=i, tt=tt, wdb=wdb: en.matmul(
                        cx.pb[tt][:, :], wdb[:, i * 128:(i + 1) * 128],
                        hT[:, i * 2048 + tt * 512: i * 2048 + tt * 512 + 512],
                        start=(i == 0), stop=(i == NIC - 1)),
                        reads=(wdk, ("hT", i, tt)), writes=(f"pb{tt}",), tag="l0fy")
            for tt in range(4):
                p.emit("dve", lambda en, oc=oc, tt=tt: en.tensor_add(
                    A[:, oc * 2048 + tt * 512: oc * 2048 + tt * 512 + 512],
                    A[:, oc * 2048 + tt * 512: oc * 2048 + tt * 512 + 512], cx.pb[tt][:, :]),
                    reads=(f"pb{tt}", akey(oc, tt)), writes=(akey(oc, tt),), tag="l0fres")


def bld_l1a(cx, fences=()):
    p, nc = cx.p, cx.nc
    A = cx.A
    xs = cx.R1
    fr = tuple(fences)
    akey = lambda c, t: ("A", c, t)
    xskey = lambda c, t: ("l1xs", c, t)

    l1kv = cx.R2[0:64, 0:4096]
    stateb = cx.R2[0:64, 4096:8192]
    wo_sb = cx.R2[0:64, 8192:10240]
    r2addr = nc.lookup_mloc(cx.R2).addr
    l1buf = nc.alloc_sbuf_tensor_at("l1buf", [128, 4096], F32, offset=r2addr + 20480)
    scr64 = cx.sc_b[0:64, 1024:3072]

    gcol = cx.sml[:, 8:24]
    omg = cx.sml[:, 24:40]
    lnw_sb = cx.sml[:, 40:56]
    lnb_sb = cx.sml[:, 56:72]
    wcol = cx.sml[:, 72:73]
    netf = cx.sml[0:64, 73:74]
    p.dma("sp", gcol, cx.d_gate[:, :], writes=("l1gate",), tag="l1c")
    p.dma("sp", lnw_sb, cx.d_lnw[:, :], writes=("lnw",), tag="l1c")
    p.dma("sp", lnb_sb, cx.d_lnb[:, :], writes=("lnb",), tag="l1c")
    p.dma("sp", wcol, cx.d_td[:, :], writes=("l1td",), tag="l1c")
    p.dma("sp", netf, cx.d_tf[:, :], writes=("l1tf",), tag="l1c")
    p.dma("sp", cx.i32[0:1, 0:1], cx.d_hoff[:, :], writes=("hoff",), tag="l1c")
    p.emit("act", lambda e: e.activation(gcol, gcol, AF.Sigmoid), reads=("l1gate",), writes=("l1gate",), tag="l1c")
    p.emit("dve", lambda e: e.tensor_scalar(omg, gcol, -1.0, 1.0, ALU.mult, ALU.add),
           reads=("l1gate",), writes=("l1omg",), tag="l1c")
    p.emit("act", lambda e: e.activation(wcol, wcol, AF.Exp), reads=("l1td",), writes=("l1td",), tag="l1c")
    p.emit("act", lambda e: e.activation(netf, netf, AF.Exp), reads=("l1tf",), writes=("l1tf",), tag="l1c")
    p.emit("dve", lambda e: e.tensor_scalar_mul(netf, netf, -1.0), reads=("l1tf",), writes=("l1tf",), tag="l1c")

    wtsb = [cx.WS[:, b * 2048:(b + 1) * 2048] for b in range(2)]
    wk_sb = cx.WS[:, 4096:5120]
    wv_sb = cx.WS[:, 5120:6144]
    p.dma("pool", wk_sb.rearrange("p (c n) -> p c n", n=64),
          cx.d_wkT[:, :].rearrange("(c p) n -> p c n", p=128), writes=("wks",), tag="l1w")
    p.dma("pool", wv_sb.rearrange("p (c n) -> p c n", n=64),
          cx.d_wvT[:, :].rearrange("(c p) n -> p c n", p=128), writes=("wvs",), tag="l1w")
    p.dma("pool", wo_sb, cx.d_woT[:, :], writes=("wos",), tag="l1w")

    t2 = cx.sc_b[:, 0:512]
    for ec in range(16):
        b = ec % 2
        p.dma("pool", wtsb[b][:, :].rearrange("p (c n) -> p c n", n=128),
              cx.d_wts[ec, :, :].rearrange("(c p) n -> p c n", p=128),
              writes=(("wts", b),), tag="l1w")
        for tt in range(4):
            rtt = (tt + 2) % 4
            for d in range(16):
                p.emit("pe", lambda e, b=b, d=d, rtt=rtt, xb=tt % 2: e.matmul(
                    cx.pb[xb][:, :],
                    wtsb[b][:, d * 128:(d + 1) * 128],
                    A[:, d * 2048 + rtt * 512: d * 2048 + rtt * 512 + 512],
                    start=(d == 0), stop=(d == 15)),
                    reads=(("wts", b), akey(d, rtt)) + fr, writes=(f"pb{tt % 2}",), tag="l1mm")
            t2v = cx.sc_b[:, (tt % 2) * 512:(tt % 2 + 1) * 512]
            p.emit("dve", lambda e, ec=ec, tt=tt, t2v=t2v: e.tensor_scalar_mul(
                t2v, A[:, ec * 2048 + tt * 512: ec * 2048 + tt * 512 + 512], omg[:, ec:ec + 1]),
                reads=(akey(ec, tt), "l1omg") + fr, writes=(f"lnsq{tt % 2}",), tag="l1mix")
            p.emit("dve", lambda e, ec=ec, tt=tt, t2v=t2v, xb=tt % 2: e.scalar_tensor_tensor(
                xs[:, ec * 2048 + tt * 512: ec * 2048 + tt * 512 + 512],
                cx.pb[xb][:, :], gcol[:, ec:ec + 1], t2v, ALU.mult, ALU.add),
                reads=(f"pb{tt % 2}", "l1gate", f"lnsq{tt % 2}") + fr, writes=(xskey(ec, tt),), tag="l1mix")

    for tt in range(4):
        ln_normalize(cx, xs, lambda c: xskey(c, tt), xs, lambda c: xskey(c, tt), tt,
                     lnwb=(lnw_sb, lnb_sb))

    for tt in range(4):
        for d in range(16):
            p.emit("pe", lambda e, d=d, tt=tt: e.matmul(
                cx.pb[1][0:64, :], wk_sb[:, d * 64:(d + 1) * 64],
                xs[:, d * 2048 + tt * 512: d * 2048 + tt * 512 + 512],
                start=(d == 0), stop=(d == 15)),
                reads=("wks", xskey(d, tt)), writes=("pb1",), tag="l1kv")
        for d in range(16):
            p.emit("pe", lambda e, d=d, tt=tt: e.matmul(
                cx.pb[2][0:64, :], wv_sb[:, d * 64:(d + 1) * 64],
                xs[:, d * 2048 + tt * 512: d * 2048 + tt * 512 + 512],
                start=(d == 0), stop=(d == 15)),
                reads=("wvs", xskey(d, tt)), writes=("pb2",), tag="l1kv")
        p.emit("dve", lambda e, tt=tt: e.tensor_copy(l1kv[0:64, tt * 512:(tt + 1) * 512], cx.pb[1][0:64, :]),
               reads=("pb1",), writes=(("l1k", tt),), tag="l1kv")
        p.emit("dve", lambda e, tt=tt: e.tensor_copy(l1kv[0:64, 2048 + tt * 512: 2048 + (tt + 1) * 512], cx.pb[2][0:64, :]),
               reads=("pb2",), writes=(("l1v", tt),), tag="l1kv")

    kread = tuple(("l1k", t) for t in range(4))
    vread = tuple(("l1v", t) for t in range(4))
    p.emit("act", lambda e: e.activation(scr64, l1kv[0:64, 0:2048], AF.Sigmoid),
           reads=kread, writes=("l1s1",), tag="l1wkv")
    p.emit("dve", lambda e: e.tensor_scalar_mul(scr64, scr64, netf[0:64, :]),
           reads=("l1s1", "l1tf"), writes=("l1s1",), tag="l1wkv")
    p.emit("act", lambda e: e.activation(scr64, scr64, AF.Exp),
           reads=("l1s1",), writes=("l1s1",), tag="l1wkv")
    p.emit("dve", lambda e: e.tensor_mul(l1buf[0:64, 0:2048], scr64, l1kv[0:64, 2048:4096]),
           reads=("l1s1",) + vread, writes=("wkvown",), tag="l1wkv")

    p.dma("pool", cx.d_ccin[:, :], l1buf[0:64, 0:2048], reads=("wkvown",),
          writes=("ccin",), tag="l1cc")
    p.emit("pool", lambda e: e.collective_compute(
        "AllGather", ALU.bypass, replica_groups=REPLICA_PAIRS,
        ins=[cx.d_ccin.ap().opt()], outs=[cx.d_ccout.ap().opt()]),
        reads=("ccin",), writes=("ccout",), dma="cc", tag="l1cc")
    for g in range(8):
        p.dma("pool", l1buf[64:128, g * 512:(g + 1) * 512],
              cx.d_ccout[(g % 2) * 64:(g % 2) * 64 + 64, (g // 2) * 512:(g // 2) * 512 + 512],
              reads=("ccout",), writes=(("wkvg", g),), tag="l1cc")

    sml_h = cx.sml
    smlw = sml_h.shape[1]
    wbcast = bass.AP(sml_h, 64 * smlw + 72, [[smlw, 64], [0, 4096]])
    p.emit("dve", lambda e: e.tensor_tensor_scan(
        l1buf[0:64, :], wbcast, l1buf[64:128, :], 0.0, ALU.mult, ALU.add),
        reads=("l1td",) + tuple(("wkvg", g) for g in range(8)) + ("wkvown",),
        writes=("states", "wkvown"), tag="l1scan")
    p.emit("dve", lambda e: e.tensor_copy(stateb, l1buf[0:64, :]),
           reads=("states",), writes=("stateb",), tag="l1scan")

    hreg = {}

    def loadreg(e):
        hreg["r"] = nc.values_load(cx.i32[0:1, 0:1], engines=[ET.PE], min_val=0, max_val=512,
                                   skip_runtime_bounds_check=True)
        return None
    p.emit("pe", loadreg, reads=("hoff",), writes=(), tag="l1reg")
    for oc in range(16):
        for c in range(4):
            p.emit("pe", lambda e, oc=oc, c=c: e.matmul(
                cx.pb[3][:, :], wo_sb[:, oc * 128:(oc + 1) * 128],
                stateb[0:64, ds(hreg["r"] + c * 1024, 512)],
                start=True, stop=True),
                reads=("wos", "stateb", "hoff"), writes=("pb3",), tag="l1out")
            p.emit("dve", lambda e, oc=oc, c=c: e.tensor_add(
                A[:, oc * 2048 + c * 512: oc * 2048 + c * 512 + 512],
                A[:, oc * 2048 + c * 512: oc * 2048 + c * 512 + 512], cx.pb[3][:, :]),
                reads=("pb3", akey(oc, c)), writes=(akey(oc, c),), tag="l1res")



def kernel(**inputs):
    return run_blocks_unfused(inputs, [["l0a", "l0f", "l1a", "l1f"]])
